# version 13
# baseline (speedup 1.0000x reference)
import numpy as np
import ml_dtypes
import concourse.bass as bass
import concourse.mybir as mybir
from concourse.bass_utils import run_bass_kernel_spmd
from contextlib import ExitStack

F32 = mybir.dt.float32
BF16 = mybir.dt.bfloat16
AF = mybir.ActivationFunctionType
ALU = mybir.AluOpType
AXX = mybir.AxisListType.X

KDEBUG = False

D = 4096
NCH = 32
TOK = 1024
LOC = 1536
H0 = 32
NKV = 8
SCALE = 1.0 / float(np.sqrt(128.0))
NEG = -1e9


class Buf:
    __slots__ = ("w", "r")

    def __init__(self):
        self.w = None
        self.r = {}

    def rd_waits(self):
        return [self.w] if self.w is not None else []

    def wr_waits(self):
        ws = dict(self.r)
        if self.w is not None:
            s, v = self.w
            ws[s] = max(ws.get(s, -1), v)
        return list(ws.items())


class Prog:
    ENGS = ("tensor", "vector", "scalar", "gpsimd", "sync")

    def __init__(self, nc):
        self.nc = nc
        self.ev = {e: [] for e in self.ENGS}
        self.sems = {}
        self.cnt = {}
        self.lastwait = {e: {} for e in self.ENGS}

    def sem(self, name):
        self.sems[name] = self.nc.alloc_semaphore(name)
        self.cnt[name] = 0

    def emit(self, eng, fns, reads=(), writes=(), waits=(), incs=()):
        if callable(fns):
            fns = [fns]
        wset = {}
        for s, v in waits:
            wset[s] = max(wset.get(s, -1), v)
        for b in reads:
            for s, v in b.rd_waits():
                wset[s] = max(wset.get(s, -1), v)
        for b in writes:
            for s, v in b.wr_waits():
                wset[s] = max(wset.get(s, -1), v)
        fw = []
        for s, v in wset.items():
            if self.lastwait[eng].get(s, -1) < v:
                fw.append((self.sems[s], v))
                self.lastwait[eng][s] = v
        marks = []
        for s, n in incs:
            self.cnt[s] += n
            marks.append((s, self.cnt[s]))
        mark = marks[0] if marks else None
        self.ev[eng].append((fns, fw, [(self.sems[s], n) for s, n in incs]))
        if mark is not None:
            for b in reads:
                b.r[mark[0]] = max(b.r.get(mark[0], -1), mark[1])
            for b in writes:
                b.w = mark
                b.r = {}
        return mark

    def join(self, eng, sems=None):
        if sems is None:
            sems = list(self.sems)
        fw = []
        for s in sems:
            v = self.cnt[s]
            if v > 0 and self.lastwait[eng].get(s, -1) < v:
                fw.append((self.sems[s], v))
                self.lastwait[eng][s] = v
        if fw:
            self.ev[eng].append(([], fw, []))

    def join_all(self):
        for e in self.ENGS:
            self.join(e)

    def replay(self, block):
        def mk(engname):
            def body(e):
                for fns, fw, incs in self.ev[engname]:
                    for h, v in fw:
                        e.wait_ge(h, v)
                    ins = None
                    for f in fns:
                        ins = f(e)
                    for h, n in incs:
                        assert ins is not None
                        ins = ins.then_inc(h, n)
            return body
        block.tensor(mk("tensor"))
        block.vector(mk("vector"))
        block.scalar(mk("scalar"))
        block.gpsimd(mk("gpsimd"))
        block.sync(mk("sync"))


def _rope_cache(S=2048, Dh=128, base=10000.0):
    inv = 1.0 / base ** (np.arange(0, Dh, 2, dtype=np.float32) / Dh)
    t = np.arange(S, dtype=np.float32)
    f = np.outer(t, inv)
    cos = np.zeros((S, Dh), np.float32)
    sin = np.zeros((S, Dh), np.float32)
    cos[:, 0::2] = np.cos(f); cos[:, 1::2] = np.cos(f)
    sin[:, 0::2] = np.sin(f); sin[:, 1::2] = np.sin(f)
    return cos, sin


def _consts_for(half):
    off = 1024 * half
    cosF, sinF = _rope_cache()
    qpos = np.arange(off, off + TOK)
    kpos = np.clip(np.arange(off - 256, off + 1280), 0, 2047)
    c = {}
    c["cosq"] = np.ascontiguousarray((cosF[qpos] * SCALE).T).astype(ml_dtypes.bfloat16)
    c["sinq"] = np.ascontiguousarray((sinF[qpos] * SCALE).T).astype(ml_dtypes.bfloat16)
    c["cosk"] = np.ascontiguousarray(cosF[kpos].T).astype(ml_dtypes.bfloat16)
    c["sink"] = np.ascontiguousarray(sinF[kpos].T).astype(ml_dtypes.bfloat16)
    P = np.zeros((128, 128), np.float32)
    for d in range(64):
        P[d, d + 64] = -1.0
        P[d + 64, d] = 1.0
    c["permT"] = np.ascontiguousarray(P.T).astype(ml_dtypes.bfloat16)
    c["identb"] = np.eye(128, dtype=ml_dtypes.bfloat16)
    c["identf"] = np.eye(8, dtype=np.float32)
    mask = np.zeros((128, 12), np.float32)
    if half == 0:
        mask[:, 0] = NEG; mask[:, 1] = NEG
    else:
        mask[:, 10] = NEG; mask[:, 11] = NEG
    c["maskb"] = mask
    ramp = np.arange(256, dtype=np.float32) / 255.0
    coef = np.zeros((8, 512), np.float32)
    for w in range(5):
        if w >= 1:
            coef[w, 0:256] = ramp
        if w <= 3:
            coef[w, 256:512] = 1.0 - ramp
    if half == 0:
        coef[0, 256:512] = 0.0
        coef[1, 0:256] = 1.0
    coef2 = np.zeros((8, 1024), np.float32)
    coef2[0, 0:512] = coef[1]; coef2[1, 0:512] = coef[0]; coef2[2, 0:512] = coef[2]
    coef2[0, 512:1024] = coef[3]; coef2[1, 512:1024] = coef[4]
    c["coef"] = coef2
    ROW_OF = {1: 0, 0: 1, 2: 2, 3: 0, 4: 1}
    basis = np.zeros((128, 40), ml_dtypes.bfloat16)
    for w in range(5):
        basis[:, w * 8 + ROW_OF[w]] = 1.0
    c["basis"] = basis
    c["ones1b"] = np.ones((128, 1), ml_dtypes.bfloat16)
    c["onesbf"] = np.ones((1, 128), np.float32)
    rs = np.zeros((8, 6 * 128), ml_dtypes.bfloat16)
    for w in range(6):
        rs[w, w * 128:(w + 1) * 128] = 1.0
    c["rowselb"] = rs
    blk = np.zeros((128, 16), ml_dtypes.bfloat16)
    for oc in range(2):
        for p in range(128):
            blk[p, oc * 8 + oc * 4 + p // 32] = 1.0
    c["blocksel"] = blk
    ex1 = np.zeros((8, 128), ml_dtypes.bfloat16)
    ex2 = np.zeros((8, 128), ml_dtypes.bfloat16)
    for m in range(128):
        ex1[m // 16, m] = 1.0
        ex2[m // 16, m] = 2.0
    c["expand1"] = ex1
    c["expand2"] = ex2
    return c


_CONST_SHAPES = {
    "cosq": ([128, 1024], BF16), "sinq": ([128, 1024], BF16),
    "cosk": ([128, 1536], BF16), "sink": ([128, 1536], BF16),
    "permT": ([128, 128], BF16), "identb": ([128, 128], BF16),
    "identf": ([8, 8], F32), "maskb": ([128, 12], F32),
    "coef": ([8, 1024], F32), "basis": ([128, 40], BF16),
    "ones1b": ([128, 1], BF16), "rowselb": ([8, 768], BF16),
    "onesbf": ([1, 128], F32), "blocksel": ([128, 16], BF16),
    "expand1": ([8, 128], BF16), "expand2": ([8, 128], BF16),
}

_WEIGHT_SHAPES = {
    "w0q": [D, D], "w0k": [D, 128], "w0v": [D, 128], "w0o": [D, D],
    "w0f": [D, D], "w0fq": [D, 256], "w0fk": [D, 256],
    "a0": [D, 128], "bm0": [128, D],
    "w1k": [D, 1024], "w1v": [D, 1024], "w1o": [1024, D],
    "gw1": [D, 8], "bw1": [D, D], "la1": [D, 128], "lb1": [128, D],
}


def build_nc():
    nc = bass.Bass("TRN2", target_bir_lowering=False)
    ap = {}
    ap["xloc"] = nc.dram_tensor("xloc", [D, LOC], F32, kind="ExternalInput").ap()
    for n, shp in _WEIGHT_SHAPES.items():
        ap[n] = nc.dram_tensor(n, shp, F32, kind="ExternalInput").ap()
    ap["b0"] = nc.dram_tensor("b0", [128, 32], F32, kind="ExternalInput").ap()
    ap["b1"] = nc.dram_tensor("b1", [128, 32], F32, kind="ExternalInput").ap()
    for n, (shp, dt) in _CONST_SHAPES.items():
        ap[n] = nc.dram_tensor(n, shp, dt, kind="ExternalInput").ap()
    y = nc.dram_tensor("y", [D, TOK], F32, kind="ExternalOutput").ap()
    dbg = {}
    if KDEBUG:
        for n, shp, dt in [("dxn", [D, TOK], BF16), ("dao", [D, TOK], BF16),
                           ("dx1", [D, TOK], F32), ("dxn1", [D, TOK], BF16),
                           ("dqs", [1024, TOK], BF16), ("dval", [1024, TOK], BF16),
                           ("dhw", [128, TOK], BF16), ("dhw1", [128, TOK], BF16),
                           ("dkr", [128, LOC], BF16)]:
            dbg[n] = nc.dram_tensor(n, shp, dt, kind="ExternalOutput").ap()
    ap["w1qsl"] = nc.dram_tensor("w1qsl", [4, 128, D], F32, kind="ExternalInput").ap()
    x1d = nc.dram_tensor("x1d", [D, TOK], F32).ap()
    x1b = nc.dram_tensor("x1b", [D, TOK], BF16).ap()
    kvs = nc.dram_tensor("kvs", [2048, TOK], BF16).ap()
    kva = nc.dram_tensor("kva", [2, 2048, TOK], BF16).ap()
    wqpart = nc.dram_tensor("wqpart", [4, 128, 1024], BF16).ap()
    wqall = nc.dram_tensor("wqall", [8, 4, 128, 1024], BF16, addr_space="Shared").ap()

    st = ExitStack()
    with st:
        xn = st.enter_context(nc.sbuf_tensor("xn", [128, NCH * TOK], BF16))
        big2 = st.enter_context(nc.sbuf_tensor("big2", [128, NCH * TOK], BF16))
        wp = [st.enter_context(nc.sbuf_tensor(f"wp{i}", [128, 4096], BF16)) for i in range(4)]
        hw = st.enter_context(nc.sbuf_tensor("hw", [128, 1024], BF16))
        permT = st.enter_context(nc.sbuf_tensor("permT_s", [128, 128], BF16))
        identb = st.enter_context(nc.sbuf_tensor("identb_s", [128, 128], BF16))
        identf = st.enter_context(nc.sbuf_tensor("identf_s", [8, 8], F32))
        maskb = st.enter_context(nc.sbuf_tensor("maskb_s", [128, 12], F32))
        basis = st.enter_context(nc.sbuf_tensor("basis_s", [128, 40], BF16))
        ones1b = st.enter_context(nc.sbuf_tensor("ones1b_s", [128, 1], BF16))
        rowselb = st.enter_context(nc.sbuf_tensor("rowselb_s", [8, 768], BF16))
        onesbf = st.enter_context(nc.sbuf_tensor("onesbf_s", [1, 128], F32))
        blocksel = st.enter_context(nc.sbuf_tensor("blocksel_s", [128, 16], BF16))
        expand1 = st.enter_context(nc.sbuf_tensor("expand1_s", [8, 128], BF16))
        expand2 = st.enter_context(nc.sbuf_tensor("expand2_s", [8, 128], BF16))
        b0sb = st.enter_context(nc.sbuf_tensor("b0sb", [128, 32], F32))
        b1sb = st.enter_context(nc.sbuf_tensor("b1sb", [128, 32], F32))

        prog = Prog(nc)
        for s in ("te", "ve", "sc", "gp", "cc"):
            prog.sem(s)
        NLANES = {"ds": 8, "dg": 6}
        for q, nl in NLANES.items():
            for i in range(nl):
                prog.sem(f"{q}{i}")
        _rr = {"ds": 0, "dg": 0}
        P = prog

        B_xn = [Buf() for _ in range(NCH)]
        B_big = [Buf() for _ in range(NCH)]
        B_wp = [Buf() for _ in range(4)]
        B_hw = Buf()
        B_const = Buf()
        B_x1d = [Buf() for _ in range(NCH)]
        B_x1b = [Buf() for _ in range(NCH)]
        B_kvs = Buf()
        B_kva = Buf()
        B_y = Buf()

        def dma(eng, q, out, in_, reads=(), writes=(), waits=(), **kw):
            lane = f"{q}{_rr[q]}"
            _rr[q] = (_rr[q] + 1) % NLANES[q]
            w = list(waits) + [(lane, P.cnt[lane])]
            return P.emit(eng, lambda e: e.dma_start(out=out, in_=in_, **kw),
                          reads=reads, writes=writes, waits=w, incs=[(lane, 16)])

        for sb, n in [(permT, "permT"), (identb, "identb"), (identf, "identf"),
                      (maskb, "maskb"), (basis, "basis"), (ones1b, "ones1b"),
                      (rowselb, "rowselb"), (onesbf, "onesbf"), (blocksel, "blocksel"),
                      (expand1, "expand1"), (expand2, "expand2"),
                      (b0sb, "b0"), (b1sb, "b1")]:
            dma("sync", "ds", sb[:], ap[n], writes=[B_const])

        def load_panel(wname, c0, ncol, slot, nch=NCH):
            src = ap[wname].rearrange("(c p) o -> p c o", p=128)[:, :, c0:c0 + ncol]
            dstv = wp[slot][:, 0:nch * ncol].rearrange("p (c o) -> p c o", c=nch)
            return dma("gpsimd", "dg", dstv, src, writes=[B_wp[slot]])

        def emit_stats_finalize(B_stat, sx_ps, sx2_ps, B_sxps, B_sx2ps,
                                rstdb, negmrb, B_rstdb, B_negmrb, ncols, bc_ps, B_bcps):
            inv = 1.0 / float(D)
            mrow = negmrb[0:1, 0:ncols]
            scr = rstdb[0:1, 0:ncols]
            P.emit("vector", lambda e: e.tensor_scalar(out=mrow, in0=sx_ps[0:1, 0:ncols],
                   scalar1=inv, scalar2=None, op0=ALU.mult),
                   reads=[B_sxps], writes=[B_stat], incs=[("ve", 1)])
            P.emit("vector", lambda e: e.tensor_tensor(out=scr, in0=mrow, in1=mrow, op=ALU.mult),
                   reads=[B_stat], writes=[B_stat], incs=[("ve", 1)])
            P.emit("vector", lambda e: e.scalar_tensor_tensor(out=scr, in0=sx2_ps[0:1, 0:ncols],
                   scalar=inv, in1=scr, op0=ALU.mult, op1=ALU.subtract),
                   reads=[B_sx2ps, B_stat], writes=[B_stat], incs=[("ve", 1)])
            P.emit("vector", lambda e: e.tensor_scalar(out=scr, in0=scr,
                   scalar1=1e-5, scalar2=None, op0=ALU.add),
                   reads=[B_stat], writes=[B_stat], incs=[("ve", 1)])
            P.emit("scalar", lambda e: e.activation(out=scr, in_=scr, func=AF.Sqrt),
                   reads=[B_stat], writes=[B_stat], incs=[("sc", 1)])
            P.emit("vector", lambda e: e.reciprocal(out=scr, in_=scr),
                   reads=[B_stat], writes=[B_stat], incs=[("ve", 1)])
            P.emit("vector", lambda e: e.tensor_tensor(out=mrow, in0=mrow, in1=scr, op=ALU.mult),
                   reads=[B_stat], writes=[B_stat], incs=[("ve", 1)])
            P.emit("vector", lambda e: e.tensor_scalar(out=mrow, in0=mrow,
                   scalar1=-1.0, scalar2=None, op0=ALU.mult),
                   reads=[B_stat], writes=[B_stat], incs=[("ve", 1)])
            for (srcr, dst, Bd) in ((mrow, negmrb, B_negmrb), (scr, rstdb, B_rstdb)):
                for s in range(0, ncols, 512):
                    n = min(512, ncols - s)
                    P.emit("tensor", lambda e, srcr=srcr, s=s, n=n: e.matmul(
                        bc_ps[:, 0:n], onesbf[:], srcr[0:1, s:s + n],
                        start=True, stop=True),
                        reads=[B_stat, B_const], writes=[B_bcps], incs=[("te", 1)])
                    P.emit("scalar", lambda e, dst=dst, s=s, n=n: e.copy(
                        out=dst[:, s:s + n], in_=bc_ps[:, 0:n]),
                        reads=[B_bcps, B_stat], writes=[Bd], incs=[("sc", 1)])

        xb = big2[:, 0:1536]
        sqv = big2[:, 1536:3072]
        coskb = big2[:, 3072:4608]
        sinkb = big2[:, 4608:6144]
        kraw = big2[:, 6144:7680]
        xnh = big2[:, 7680:8192]
        B_xb, B_sq, B_ck, B_sk, B_kraw, B_xnh = Buf(), Buf(), Buf(), Buf(), Buf(), Buf()

        st23 = ExitStack()
        Kr = st23.enter_context(nc.sbuf_tensor("Kr", [128, LOC], BF16))
        V0 = st23.enter_context(nc.sbuf_tensor("V0", [128, LOC], BF16))
        B_Kr, B_V0 = Buf(), Buf()

        st12 = ExitStack()
        xf = [st12.enter_context(nc.sbuf_tensor(f"xf{i}", [128, LOC], F32)) for i in range(2)]
        rstdb = st12.enter_context(nc.sbuf_tensor("rstdb", [128, LOC], F32))
        negmrb = st12.enter_context(nc.sbuf_tensor("negmrb", [128, LOC], F32))
        xscr = st12.enter_context(nc.sbuf_tensor("xscr", [128, LOC], F32))
        B_xf = [Buf(), Buf()]
        B_stat, B_rstdb, B_negmrb, B_xscr = Buf(), Buf(), Buf(), Buf()

        ps12 = ExitStack()
        sx_ps = ps12.enter_context(nc.psum_tensor("sx_ps", [1, LOC], F32))
        sx2_ps = ps12.enter_context(nc.psum_tensor("sx2_ps", [1, LOC], F32))
        bc_ps = ps12.enter_context(nc.psum_tensor("bc_ps", [128, 512], F32))
        B_sxps, B_sx2ps, B_bcps = Buf(), Buf(), Buf()

        xloc_t = ap["xloc"].rearrange("(c p) t -> c p t", p=128)

        for c in range(NCH):
            p = c % 2
            dma("sync", "ds", xf[p][:], xloc_t[c], writes=[B_xf[p]])
            P.emit("vector", lambda e, p=p: e.tensor_copy(out=xb, in_=xf[p][:]),
                   reads=[B_xf[p]], writes=[B_xb], incs=[("ve", 1)])
            P.emit("vector", lambda e: e.tensor_tensor(out=sqv, in0=xb, in1=xb, op=ALU.mult),
                   reads=[B_xb], writes=[B_sq], incs=[("ve", 1)])
            for (src, Bs, pst, Bp) in ((xb, B_xb, sx_ps, B_sxps), (sqv, B_sq, sx2_ps, B_sx2ps)):
                fns = [lambda e, src=src, pst=pst, s=s, c=c: e.matmul(
                    pst[0:1, s * 512:(s + 1) * 512], ones1b[:], src[:, s * 512:(s + 1) * 512],
                    start=(c == 0), stop=(c == NCH - 1)) for s in range(3)]
                P.emit("tensor", fns, reads=[Bs, B_const], writes=[Bp], incs=[("te", 1)])

        emit_stats_finalize(B_stat, sx_ps, sx2_ps, B_sxps, B_sx2ps,
                            rstdb, negmrb, B_rstdb, B_negmrb, LOC, bc_ps, B_bcps)
        P.join_all()
        ps12.close()

        psKV = ExitStack()
        k_ps = psKV.enter_context(nc.psum_tensor("k_ps", [128, LOC], F32))
        v_ps = psKV.enter_context(nc.psum_tensor("v_ps", [128, LOC], F32))
        pkp = psKV.enter_context(nc.psum_tensor("pkp", [128, 512], F32))
        vt_ps = psKV.enter_context(nc.psum_tensor("vt_ps", [128, 768], BF16))
        B_kps, B_vps, B_pkp, B_vtps = Buf(), Buf(), Buf(), Buf()

        load_panel("w0k", 0, 128, 2)
        load_panel("w0v", 0, 128, 3)

        for c in range(NCH):
            p = c % 2
            dma("sync", "ds", xf[p][:], xloc_t[c], writes=[B_xf[p]])
            P.emit("vector", lambda e, p=p: e.tensor_tensor(out=xscr[:], in0=xf[p][:],
                   in1=rstdb[:], op=ALU.mult),
                   reads=[B_xf[p], B_rstdb], writes=[B_xscr], incs=[("ve", 1)])
            P.emit("vector", lambda e, c=c: e.tensor_tensor(out=xn[:, c * TOK:(c + 1) * TOK],
                   in0=xscr[:, 256:1280], in1=negmrb[:, 256:1280], op=ALU.add),
                   reads=[B_xscr, B_negmrb], writes=[B_xn[c]], incs=[("ve", 1)])
            P.emit("vector", lambda e: e.tensor_tensor(out=xnh[:, 0:256],
                   in0=xscr[:, 0:256], in1=negmrb[:, 0:256], op=ALU.add),
                   reads=[B_xscr, B_negmrb], writes=[B_xnh], incs=[("ve", 1)])
            P.emit("vector", lambda e: e.tensor_tensor(out=xnh[:, 256:512],
                   in0=xscr[:, 1280:1536], in1=negmrb[:, 1280:1536], op=ALU.add),
                   reads=[B_xscr, B_negmrb], writes=[B_xnh], incs=[("ve", 1)])
            segs = [(0, 256, "h", 0, True, False), (256, 512, "o", 0, False, True),
                    (512, 1024, "o", 256, True, True),
                    (1024, 1280, "o", 768, True, False), (1280, 1536, "h", 256, False, True)]
            for (pst, Bp, slot) in ((k_ps, B_kps, 2), (v_ps, B_vps, 3)):
                fns = []
                for (a, b, kind, rc, bf, bl) in segs:
                    if kind == "o":
                        rhs = xn[:, c * TOK + rc: c * TOK + rc + (b - a)]
                    else:
                        rhs = xnh[:, rc: rc + (b - a)]
                    fns.append(lambda e, pst=pst, a=a, b=b, rhs=rhs, slot=slot, c=c,
                               bf=bf, bl=bl: e.matmul(
                        pst[:, a:b], wp[slot][:, c * 128:(c + 1) * 128], rhs,
                        start=(c == 0 and bf), stop=(c == NCH - 1 and bl)))
                P.emit("tensor", fns, reads=[B_xn[c], B_xnh, B_wp[slot], B_const],
                       writes=[Bp], incs=[("te", 1)])

        dma("sync", "ds", coskb, ap["cosk"], writes=[B_ck])
        dma("sync", "ds", sinkb, ap["sink"], writes=[B_sk])
        P.emit("scalar", lambda e: e.copy(out=kraw, in_=k_ps[:]),
               reads=[B_kps], writes=[B_kraw], incs=[("sc", 1)])
        P.emit("vector", lambda e: e.tensor_tensor(out=Kr[:], in0=kraw, in1=coskb, op=ALU.mult),
               reads=[B_kraw, B_ck], writes=[B_Kr], incs=[("ve", 1)])
        for s in range(3):
            P.emit("tensor", lambda e, s=s: e.matmul(pkp[:], permT[:],
                   kraw[:, s * 512:(s + 1) * 512], start=True, stop=True),
                   reads=[B_kraw, B_const], writes=[B_pkp], incs=[("te", 1)])
            P.emit("vector", lambda e, s=s: e.tensor_tensor(out=xb[:, 0:512], in0=pkp[:],
                   in1=sinkb[:, s * 512:(s + 1) * 512], op=ALU.mult),
                   reads=[B_pkp, B_sk], writes=[B_xb], incs=[("ve", 1)])
            P.emit("vector", lambda e, s=s: e.tensor_tensor(out=Kr[:, s * 512:(s + 1) * 512],
                   in0=Kr[:, s * 512:(s + 1) * 512], in1=xb[:, 0:512], op=ALU.add),
                   reads=[B_Kr, B_xb], writes=[B_Kr], incs=[("ve", 1)])
        P.emit("scalar", lambda e: e.copy(out=kraw, in_=v_ps[:]),
               reads=[B_vps], writes=[B_kraw], incs=[("sc", 1)])
        for g in range(2):
            fns = [lambda e, t=t, g=g: e.transpose(vt_ps[:, t * 128:(t + 1) * 128],
                   kraw[:, (g * 6 + t) * 128:(g * 6 + t + 1) * 128], identb[:]) for t in range(6)]
            P.emit("tensor", fns, reads=[B_kraw, B_const], writes=[B_vtps], incs=[("te", 1)])
            P.emit("vector", lambda e, g=g: e.tensor_copy(out=V0[:, g * 768:(g + 1) * 768],
                   in_=vt_ps[:]), reads=[B_vtps], writes=[B_V0], incs=[("ve", 1)])

        P.join_all()
        psKV.close()
        st12.close()

        B_wqall = Buf()
        B_wqpart = [Buf() for _ in range(4)]
        for i in range(4):
            a0 = (24 + 4 * (i % 2)) * TOK
            Av = big2[:, a0: a0 + 4096].rearrange("p (h x) -> p h x", x=512)
            bbufs = [B_big[24 + 4 * (i % 2) + k] for k in range(4)]
            dma("gpsimd", "dg", big2[:, a0: a0 + 4096], ap["w1qsl"][i], writes=bbufs)
            P.emit("vector", lambda e, Av=Av: e.tensor_tensor(
                out=Av[:, :, 0:128], in0=Av[:, :, 0:128], in1=Av[:, :, 128:256], op=ALU.add),
                reads=bbufs, writes=bbufs, incs=[("ve", 1)])
            P.emit("vector", lambda e, Av=Av: e.tensor_tensor(
                out=Av[:, :, 256:384], in0=Av[:, :, 256:384], in1=Av[:, :, 384:512], op=ALU.add),
                reads=bbufs, writes=bbufs, incs=[("ve", 1)])
            P.emit("vector", lambda e, Av=Av: e.tensor_tensor(
                out=Av[:, :, 128:256], in0=Av[:, :, 0:128], in1=Av[:, :, 256:384], op=ALU.add),
                reads=bbufs, writes=bbufs, incs=[("ve", 1)])
            dma("sync", "ds", wqpart[i].rearrange("p (h j) -> p h j", j=128),
                Av[:, :, 128:256], reads=bbufs, writes=[B_wqpart[i]])
        P.emit("gpsimd", lambda e: e.collective_compute(
            "AllGather", ALU.bypass, replica_groups=[[0, 1, 2, 3, 4, 5, 6, 7]],
            ins=[wqpart], outs=[wqall]), reads=B_wqpart, writes=[B_wqall],
            incs=[("cc", 1)])

        if KDEBUG:
            for c in range(NCH):
                dma("sync", "ds", dbg["dxn"].rearrange("(c p) t -> c p t", p=128)[c],
                    xn[:, c * TOK:(c + 1) * TOK], reads=[B_xn[c]])
            dma("sync", "ds", dbg["dkr"], Kr[:], reads=[B_Kr])

        st3 = ExitStack()
        qsb = st3.enter_context(nc.sbuf_tensor("qsb", [128, 1024], BF16))
        qr2 = [st3.enter_context(nc.sbuf_tensor(f"qr{i}", [128, 1024], BF16)) for i in range(2)]
        qtmp = st3.enter_context(nc.sbuf_tensor("qtmp", [128, 1024], BF16))
        rtmp = st3.enter_context(nc.sbuf_tensor("rtmp", [128, 1024], BF16))
        sig = st3.enter_context(nc.sbuf_tensor("sig", [128, 4096], BF16))
        csb = st3.enter_context(nc.sbuf_tensor("csb", [128, 512], BF16))
        cosq = st3.enter_context(nc.sbuf_tensor("cosq_s", [128, 1024], BF16))
        sinq = st3.enter_context(nc.sbuf_tensor("sinq_s", [128, 1024], BF16))
        coef = st3.enter_context(nc.sbuf_tensor("coef_s", [8, 1024], F32))
        csr = st3.enter_context(nc.sbuf_tensor("csr", [8, 512], F32))
        cs = st3.enter_context(nc.sbuf_tensor("cs", [8, 1024], BF16))
        B_qsb, B_qtmp, B_rtmp, B_sig, B_csb = Buf(), Buf(), Buf(), Buf(), Buf()
        B_qr2 = [Buf(), Buf()]
        B_csr = Buf()
        B_csA, B_csB = Buf(), Buf()
        dma("sync", "ds", cosq[:], ap["cosq"], writes=[B_const])
        dma("sync", "ds", sinq[:], ap["sinq"], writes=[B_const])
        dma("sync", "ds", coef[:], ap["coef"], writes=[B_const])

        ps3 = ExitStack()
        scp = ps3.enter_context(nc.psum_tensor("scp", [128, 512], F32))
        qg = ps3.enter_context(nc.psum_tensor("qg", [128, 512], F32))
        wob = [ps3.enter_context(nc.psum_tensor(f"wob{i}", [128, 512], F32)) for i in range(3)]
        denA = ps3.enter_context(nc.psum_tensor("denA", [8, 512], F32))
        denB = ps3.enter_context(nc.psum_tensor("denB", [8, 512], F32))
        csb_ps = ps3.enter_context(nc.psum_tensor("csb_ps", [128, 512], F32))
        B_scp, B_qg = Buf(), Buf()
        B_wob = [Buf() for _ in range(3)]
        B_denA, B_denB, B_csbps = Buf(), Buf(), Buf()

        WGEO = [(0, 256, 256), (0, 512, 0), (256, 512, 0), (512, 512, 0), (768, 256, 0)]
        WBANK = {1: 0, 0: 1, 2: 2, 3: 0, 4: 1}
        ROW_OF = {1: 0, 0: 1, 2: 2, 3: 0, 4: 1}
        DGRP = {1: 0, 0: 0, 2: 0, 3: 1, 4: 1}

        def emit_qgemm_filler(h):
            if h >= H0:
                return []
            slot = h % 2
            par = h % 2
            th = []
            if h + 1 < H0:
                th.append(lambda h=h: load_panel("w0q", (h + 1) * 128, 128, (h + 1) % 2))
            for t in range(2):
                for g in range(8):
                    fns = []
                    for c in range(g * 4, g * 4 + 4):
                        fns.append(lambda e, c=c, t=t, slot=slot: e.matmul(
                            qg[:], wp[slot][:, c * 128:(c + 1) * 128],
                            xn[:, c * TOK + t * 512: c * TOK + (t + 1) * 512],
                            start=(c == 0), stop=(c == NCH - 1)))
                    th.append(lambda fns=fns, slot=slot: P.emit(
                        "tensor", fns, reads=[B_wp[slot], B_const] + B_xn,
                        writes=[B_qg], incs=[("te", 1)]))
                th.append(lambda t=t: P.emit("scalar",
                    lambda e, t=t: e.copy(out=qsb[:, t * 512:(t + 1) * 512], in_=qg[:]),
                    reads=[B_qg], writes=[B_qsb], incs=[("sc", 1)]))
            for t in range(2):
                th.append(lambda t=t: P.emit("tensor",
                    lambda e, t=t: e.matmul(qg[:], permT[:], qsb[:, t * 512:(t + 1) * 512],
                                            start=True, stop=True),
                    reads=[B_qsb, B_const], writes=[B_qg], incs=[("te", 1)]))
                th.append(lambda t=t, par=par: P.emit("vector",
                    lambda e, t=t, par=par: e.tensor_tensor(
                        out=qr2[par][:, t * 512:(t + 1) * 512], in0=qg[:],
                        in1=sinq[:, t * 512:(t + 1) * 512], op=ALU.mult),
                    reads=[B_qg, B_const], writes=[B_qr2[par]], incs=[("ve", 1)]))
            th.append(lambda par=par: P.emit("vector",
                lambda e: e.tensor_tensor(out=rtmp[:], in0=qsb[:], in1=cosq[:], op=ALU.mult),
                reads=[B_qsb, B_const], writes=[B_rtmp], incs=[("ve", 1)]))
            th.append(lambda par=par: P.emit("vector",
                lambda e, par=par: e.tensor_tensor(out=qr2[par][:], in0=qr2[par][:],
                                                   in1=rtmp[:], op=ALU.add),
                reads=[B_rtmp], writes=[B_qr2[par]], incs=[("ve", 1)]))
            return th

        def emit_unit(h, w, j, with_av, fill=None):
            par = h % 2
            qa, qn, g0 = WGEO[w]
            kc = 2 * w + j
            slotj = j if w not in (4, 0) else 4 + j
            dgrp = DGRP[w]
            dps, Bd = (denA, B_denA) if dgrp == 0 else (denB, B_denB)
            P.emit("tensor", lambda e, kc=kc, qa=qa, qn=qn, g0=g0, par=par: e.matmul(
                scp[:, g0:g0 + qn], Kr[:, kc * 128:(kc + 1) * 128],
                qr2[par][:, qa:qa + qn], start=True, stop=True),
                reads=[B_Kr, B_qr2[par]], writes=[B_scp], incs=[("te", 1)])
            P.emit("scalar", lambda e, slotj=slotj, kc=kc, g0=g0, qn=qn: e.activation(
                out=sig[:, slotj * 512 + g0: slotj * 512 + g0 + qn], in_=scp[:, g0:g0 + qn],
                func=AF.Sigmoid, bias=maskb[:, kc:kc + 1], scale=1.0),
                reads=[B_scp, B_const], writes=[B_sig], incs=[("sc", 1)])
            if fill is not None:
                fill(1)
            P.emit("tensor", lambda e, w=w, j=j, g0=g0, qn=qn, slotj=slotj, dps=dps: e.matmul(
                dps[:, g0:g0 + qn], basis[:, w * 8:(w + 1) * 8],
                sig[:, slotj * 512 + g0: slotj * 512 + g0 + qn],
                start=(w in (1, 3) and j == 0), stop=(w in (2, 4) and j == 3)),
                reads=[B_sig, B_const], writes=[Bd], incs=[("te", 1)])
            if with_av:
                emit_av(h, w, j)

        def emit_av(h, w, j):
            qa, qn, g0 = WGEO[w]
            kc = 2 * w + j
            slotj = j if w not in (4, 0) else 4 + j
            bk = WBANK[w]
            P.emit("tensor", lambda e, w=w, j=j, kc=kc, g0=g0, qn=qn, slotj=slotj, bk=bk: e.matmul(
                wob[bk][:, g0:g0 + qn], V0[:, kc * 128:(kc + 1) * 128],
                sig[:, slotj * 512 + g0: slotj * 512 + g0 + qn], start=(j == 0), stop=(j == 3)),
                reads=[B_sig, B_V0], writes=[B_wob[bk]], incs=[("te", 1)])

        def emit_recip(grp):
            dps, Bd = (denA, B_denA) if grp == 0 else (denB, B_denB)
            Bcs = B_csA if grp == 0 else B_csB
            o = grp * 512
            P.emit("vector", lambda e, dps=dps: e.tensor_scalar(out=csr[:], in0=dps[:],
                   scalar1=1e-30, scalar2=None, op0=ALU.max),
                   reads=[Bd], writes=[B_csr], incs=[("ve", 1)])
            P.emit("vector", lambda e: e.reciprocal(out=csr[:], in_=csr[:]),
                   reads=[B_csr], writes=[B_csr], incs=[("ve", 1)])
            P.emit("vector", lambda e, o=o: e.tensor_tensor(out=cs[:, o:o + 512],
                   in0=coef[:, o:o + 512], in1=csr[:], op=ALU.mult),
                   reads=[B_csr, B_const], writes=[Bcs], incs=[("ve", 1)])

        BROW = {0: ((0, 1), (0, 0)), 1: ((0, 0), (0, 2)),
                2: ((0, 2), (1, 0)), 3: ((1, 0), (1, 1))}

        def emit_blend(h, j):
            (fp, fr), (rp, rr) = BROW[j]
            Bf = B_csA if fp == 0 else B_csB
            Br = B_csA if rp == 0 else B_csB
            P.emit("tensor", [
                lambda e, fp=fp, fr=fr: e.matmul(csb_ps[:, 0:256],
                    rowselb[:, fr * 128:(fr + 1) * 128],
                    cs[0:8, fp * 512 + 256: fp * 512 + 512], start=True, stop=True),
                lambda e, rp=rp, rr=rr: e.matmul(csb_ps[:, 256:512],
                    rowselb[:, rr * 128:(rr + 1) * 128],
                    cs[0:8, rp * 512: rp * 512 + 256], start=True, stop=True),
            ], reads=[Bf, Br, B_const], writes=[B_csbps], incs=[("te", 1)])
            P.emit("scalar", lambda e: e.copy(out=csb[:], in_=csb_ps[:]),
                   reads=[B_csbps], writes=[B_csb], incs=[("sc", 1)])
            bf, br = WBANK[j], WBANK[j + 1]
            P.emit("vector", lambda e, bf=bf: e.tensor_tensor(out=qtmp[:, 0:256],
                   in0=wob[bf][:, 256:512], in1=csb[:, 0:256], op=ALU.mult),
                   reads=[B_wob[bf], B_csb], writes=[B_qtmp], incs=[("ve", 1)])
            P.emit("vector", lambda e, br=br: e.tensor_tensor(out=qtmp[:, 256:512],
                   in0=wob[br][:, 0:256], in1=csb[:, 256:512], op=ALU.mult),
                   reads=[B_wob[br], B_csb], writes=[B_qtmp], incs=[("ve", 1)])
            P.emit("vector", lambda e, j=j, h=h: e.tensor_tensor(
                   out=big2[:, h * TOK + j * 256: h * TOK + (j + 1) * 256],
                   in0=qtmp[:, 0:256], in1=qtmp[:, 256:512], op=ALU.add),
                   reads=[B_qtmp], writes=[B_big[h]], incs=[("ve", 1)])

        load_panel("w0q", 0, 128, 0)
        for th in emit_qgemm_filler(0):
            th()
        for h in range(H0):
            filler = list(emit_qgemm_filler(h + 1))
            fi = 0

            def fill(n=1):
                nonlocal fi
                for _ in range(n):
                    if fi < len(filler):
                        filler[fi]()
                        fi += 1

            for j in range(4):
                emit_unit(h, 1, j, with_av=False, fill=fill)
            if h > 0:
                emit_blend(h - 1, 2)
                fill(1)
                emit_blend(h - 1, 3)
                fill(1)
            for j in range(4):
                emit_av(h, 1, j)
            fill(1)
            for w in (0, 2):
                for j in range(4):
                    emit_unit(h, w, j, with_av=True, fill=fill)
            emit_recip(0)
            for j in range(4):
                emit_unit(h, 3, j, with_av=False, fill=fill)
            emit_blend(h, 0)
            fill(1)
            emit_blend(h, 1)
            fill(1)
            for j in range(4):
                emit_av(h, 3, j)
            fill(1)
            for j in range(4):
                emit_unit(h, 4, j, with_av=False, fill=fill)
            emit_recip(1)
            for j in range(4):
                emit_av(h, 4, j)
            fill(len(filler))
            if h == H0 - 1:
                emit_blend(h, 2)
                emit_blend(h, 3)

        P.join_all()
        ps3.close()
        st3.close()
        st23.close()

        if KDEBUG:
            for c in range(NCH):
                dma("sync", "ds", dbg["dao"].rearrange("(c p) t -> c p t", p=128)[c],
                    big2[:, c * TOK:(c + 1) * TOK], reads=[B_big[c]])

        def emit_gate_lora(layer):
            stg = ExitStack()
            g1 = stg.enter_context(nc.sbuf_tensor(f"g1_{layer}", [128, 2048], BF16))
            lg = stg.enter_context(nc.sbuf_tensor(f"lg_{layer}", [8, 1024], F32))
            lgT = stg.enter_context(nc.sbuf_tensor(f"lgT_{layer}", [128, 64], F32))
            tA = stg.enter_context(nc.sbuf_tensor(f"tA_{layer}", [128, 64], F32))
            tB = stg.enter_context(nc.sbuf_tensor(f"tB_{layer}", [128, 64], F32))
            tC = stg.enter_context(nc.sbuf_tensor(f"tC_{layer}", [128, 8], F32))
            tD = stg.enter_context(nc.sbuf_tensor(f"tD_{layer}", [128, 8], F32))
            wtk = stg.enter_context(nc.sbuf_tensor(f"wtk_{layer}", [128, 64], BF16))
            wfe = stg.enter_context(nc.sbuf_tensor(f"wfe_{layer}", [8, 1024], BF16))
            B_g1, B_lg, B_lgT = Buf(), Buf(), Buf()
            B_tA, B_tB, B_tC, B_tD, B_wtk, B_wfe = Buf(), Buf(), Buf(), Buf(), Buf(), Buf()

            psg = ExitStack()
            pg1 = psg.enter_context(nc.psum_tensor(f"pg1_{layer}", [128, 1024], F32))
            pg2 = psg.enter_context(nc.psum_tensor(f"pg2_{layer}", [128, 1024], F32))
            psc = psg.enter_context(nc.psum_tensor(f"psc_{layer}", [8, 1024], F32))
            B_psc = Buf()
            ptr = psg.enter_context(nc.psum_tensor(f"ptr_{layer}", [128, 64], F32))
            pwf = psg.enter_context(nc.psum_tensor(f"pwf_{layer}", [8, 1024], BF16))
            B_pg1, B_pg2, B_ptr, B_pwf = Buf(), Buf(), Buf(), Buf()

            def gemm(pst, Bp, slot, ncol=128, extra=None):
                fns = []
                for c in range(NCH):
                    for t in range(2):
                        fns.append(lambda e, c=c, t=t, slot=slot, ncol=ncol, pst=pst: e.matmul(
                            pst[:, t * 512:(t + 1) * 512] if ncol == 128 else
                            pst[0:ncol, t * 512:(t + 1) * 512],
                            wp[slot][:, c * ncol:(c + 1) * ncol],
                            xn[:, c * TOK + t * 512: c * TOK + (t + 1) * 512],
                            start=(c == 0), stop=(c == NCH - 1)))
                P.emit("tensor", fns, reads=[B_wp[slot], B_const] + B_xn,
                       writes=[Bp], incs=[("te", 1)])

            if layer == 0:
                for oc in range(2):
                    load_panel("w0fq", oc * 128, 128, 0)
                    gemm(pg1, B_pg1, 0)
                    P.emit("scalar", lambda e: e.copy(out=g1[:, 0:1024], in_=pg1[:]),
                           reads=[B_pg1], writes=[B_g1], incs=[("sc", 1)])
                    load_panel("w0fk", oc * 128, 128, 1)
                    gemm(pg2, B_pg2, 1)
                    P.emit("vector", lambda e: e.tensor_tensor(out=g1[:, 1024:2048],
                           in0=pg2[:], in1=g1[:, 0:1024], op=ALU.mult),
                           reads=[B_pg2, B_g1], writes=[B_g1], incs=[("ve", 1)])
                    fns = [lambda e, t=t, oc=oc: e.matmul(psc[0:8, t * 512:(t + 1) * 512],
                           blocksel[:, oc * 8:(oc + 1) * 8],
                           g1[:, 1024 + t * 512:1024 + (t + 1) * 512],
                           start=(oc == 0), stop=(oc == 1)) for t in range(2)]
                    P.emit("tensor", fns, reads=[B_g1, B_const], writes=[B_psc], incs=[("te", 1)])
                P.emit("scalar", lambda e: e.copy(out=lg[:], in_=psc[0:8, :]),
                       reads=[B_psc], writes=[B_lg], incs=[("sc", 1)])
            else:
                load_panel("gw1", 0, 8, 0)
                gemm(pg1, B_pg1, 0, ncol=8)
                P.emit("scalar", lambda e: e.copy(out=lg[:], in_=pg1[0:8, :]),
                       reads=[B_pg1], writes=[B_lg], incs=[("sc", 1)])

            fns = [lambda e, tc=tc: e.transpose(ptr[:, tc * 8:(tc + 1) * 8],
                   lg[0:8, tc * 128:(tc + 1) * 128], identf[:]) for tc in range(8)]
            P.emit("tensor", fns, reads=[B_lg, B_const], writes=[B_ptr], incs=[("te", 1)])
            P.emit("vector", lambda e: e.tensor_copy(out=lgT[:], in_=ptr[:]),
                   reads=[B_ptr], writes=[B_lgT], incs=[("ve", 1)])
            v3 = lgT[:].rearrange("p (a b) -> p a b", a=8)
            P.emit("vector", lambda e: e.reduce_max(out=tC[:], in_=v3, axis=AXX),
                   reads=[B_lgT], writes=[B_tC], incs=[("ve", 1)])
            if layer == 0:
                P.emit("vector", lambda e: e.tensor_tensor(out=tA[:], in0=lgT[:],
                       in1=tC[:].rearrange("p (a b) -> p a b", b=1).broadcast_to([128, 8, 8]), op=ALU.subtract),
                       reads=[B_lgT, B_tC], writes=[B_tA], incs=[("ve", 1)])
                P.emit("scalar", lambda e: e.activation(out=tA[:], in_=tA[:], func=AF.Exp,
                       scale=float(1.0 / np.sqrt(32.0))),
                       reads=[B_tA], writes=[B_tA], incs=[("sc", 1)])
                P.emit("vector", lambda e: e.reduce_sum(out=tC[:], in_=tA[:].rearrange(
                       "p (a b) -> p a b", a=8), axis=AXX),
                       reads=[B_tA], writes=[B_tC], incs=[("ve", 1)])
                P.emit("vector", lambda e: e.reciprocal(out=tD[:], in_=tC[:]),
                       reads=[B_tC], writes=[B_tD], incs=[("ve", 1)])
                P.emit("vector", lambda e: e.tensor_tensor(out=wtk[:], in0=tA[:],
                       in1=tD[:].rearrange("p (a b) -> p a b", b=1).broadcast_to([128, 8, 8]), op=ALU.mult),
                       reads=[B_tA, B_tD], writes=[B_wtk], incs=[("ve", 1)])
            else:
                P.emit("vector", lambda e: e.tensor_tensor(out=tA[:], in0=lgT[:],
                       in1=tC[:].rearrange("p (a b) -> p a b", b=1).broadcast_to([128, 8, 8]), op=ALU.is_equal),
                       reads=[B_lgT, B_tC], writes=[B_tA], incs=[("ve", 1)])
                P.emit("vector", lambda e: e.scalar_tensor_tensor(out=tA[:], in0=tA[:],
                       scalar=NEG, in1=lgT[:], op0=ALU.mult, op1=ALU.add),
                       reads=[B_tA, B_lgT], writes=[B_tA], incs=[("ve", 1)])
                P.emit("vector", lambda e: e.reduce_max(out=tD[:], in_=tA[:].rearrange(
                       "p (a b) -> p a b", a=8), axis=AXX),
                       reads=[B_tA], writes=[B_tD], incs=[("ve", 1)])
                P.emit("vector", lambda e: e.tensor_tensor(out=tA[:], in0=lgT[:],
                       in1=tC[:].rearrange("p (a b) -> p a b", b=1).broadcast_to([128, 8, 8]), op=ALU.subtract),
                       reads=[B_lgT, B_tC], writes=[B_tA], incs=[("ve", 1)])
                P.emit("scalar", lambda e: e.activation(out=tA[:], in_=tA[:], func=AF.Exp),
                       reads=[B_tA], writes=[B_tA], incs=[("sc", 1)])
                P.emit("vector", lambda e: e.tensor_tensor(out=tB[:], in0=lgT[:],
                       in1=tD[:].rearrange("p (a b) -> p a b", b=1).broadcast_to([128, 8, 8]), op=ALU.is_ge),
                       reads=[B_lgT, B_tD], writes=[B_tB], incs=[("ve", 1)])
                P.emit("vector", lambda e: e.tensor_tensor(out=tD[:], in0=tD[:], in1=tC[:],
                       op=ALU.subtract), reads=[B_tD, B_tC], writes=[B_tD], incs=[("ve", 1)])
                P.emit("scalar", lambda e: e.activation(out=tD[:], in_=tD[:], func=AF.Exp),
                       reads=[B_tD], writes=[B_tD], incs=[("sc", 1)])
                P.emit("vector", lambda e: e.tensor_scalar(out=tD[:], in0=tD[:], scalar1=1.0,
                       scalar2=None, op0=ALU.add), reads=[B_tD], writes=[B_tD], incs=[("ve", 1)])
                P.emit("vector", lambda e: e.reciprocal(out=tD[:], in_=tD[:]),
                       reads=[B_tD], writes=[B_tD], incs=[("ve", 1)])
                P.emit("vector", lambda e: e.tensor_tensor(out=tA[:], in0=tA[:], in1=tB[:],
                       op=ALU.mult), reads=[B_tA, B_tB], writes=[B_tA], incs=[("ve", 1)])
                P.emit("vector", lambda e: e.tensor_tensor(out=wtk[:], in0=tA[:],
                       in1=tD[:].rearrange("p (a b) -> p a b", b=1).broadcast_to([128, 8, 8]), op=ALU.mult),
                       reads=[B_tA, B_tD], writes=[B_wtk], incs=[("ve", 1)])

            fns = [lambda e, tc=tc: e.transpose(pwf[:, tc * 128:(tc + 1) * 128],
                   wtk[:, tc * 8:(tc + 1) * 8], identb[:]) for tc in range(8)]
            P.emit("tensor", fns, reads=[B_wtk, B_const], writes=[B_pwf], incs=[("te", 1)])
            P.emit("scalar", lambda e: e.copy(out=wfe[:], in_=pwf[:]),
                   reads=[B_pwf], writes=[B_wfe], incs=[("sc", 1)])
            exp = expand1 if layer == 0 else expand2
            fns = [lambda e, t=t, exp=exp: e.matmul(pg1[:, t * 512:(t + 1) * 512], exp[:],
                   wfe[:, t * 512:(t + 1) * 512], start=True, stop=True) for t in range(2)]
            P.emit("tensor", fns, reads=[B_wfe, B_const], writes=[B_pg1], incs=[("te", 1)])
            load_panel("a0" if layer == 0 else "la1", 0, 128, 0)
            gemm(pg2, B_pg2, 0)
            P.emit("scalar", lambda e: e.copy(out=g1[:, 0:1024], in_=pg2[:]),
                   reads=[B_pg2], writes=[B_g1], incs=[("sc", 1)])
            P.emit("vector", lambda e: e.tensor_tensor(out=hw[:], in0=pg1[:], in1=g1[:, 0:1024],
                   op=ALU.mult), reads=[B_pg1, B_g1], writes=[B_hw], incs=[("ve", 1)])
            P.join_all()
            psg.close()
            stg.close()

        emit_gate_lora(0)
        if KDEBUG:
            dma("sync", "ds", dbg["dhw"], hw[:], reads=[B_hw])

        x1d_t = x1d.rearrange("(c p) t -> c p t", p=128)
        x1b_t = x1b.rearrange("(c p) t -> c p t", p=128)
        y_t = y.rearrange("(c p) t -> c p t", p=128)

        def emit_epilogue(layer):
            is0 = layer == 0
            wo_name = "w0o" if is0 else "w1o"
            wf_name = "w0f" if is0 else "bw1"
            bias_sb = b0sb if is0 else b1sb
            nsc = NCH if is0 else 8
            ste = ExitStack()
            bml = ste.enter_context(nc.sbuf_tensor(f"bml_{layer}", [128, 4096], BF16))
            gsb = ste.enter_context(nc.sbuf_tensor(f"gsb_{layer}", [128, 1024], BF16))
            tsb = ste.enter_context(nc.sbuf_tensor(f"tsb_{layer}", [128, 1024], BF16))
            xob = [ste.enter_context(nc.sbuf_tensor(f"xob{layer}_{i}", [128, 1024], F32)) for i in range(2)]
            B_bml, B_gsb, B_tsb = Buf(), Buf(), Buf()
            B_xob = [Buf(), Buf()]
            ret = None
            pse = ExitStack()
            s_ps = pse.enter_context(nc.psum_tensor(f"s_ps_{layer}", [128, 1024], F32))
            g_ps = pse.enter_context(nc.psum_tensor(f"g_ps_{layer}", [128, 1024], F32))
            B_sps, B_gps = Buf(), Buf()
            if is0:
                xb1 = ste.enter_context(nc.sbuf_tensor(f"xb1_{layer}", [128, 1024], BF16))
                sq1 = ste.enter_context(nc.sbuf_tensor(f"sq1_{layer}", [128, 1024], BF16))
                B_xb1, B_sq1 = Buf(), Buf()
                sx1 = pse.enter_context(nc.psum_tensor(f"sx1_{layer}", [1, 1024], F32))
                sx21 = pse.enter_context(nc.psum_tensor(f"sx21_{layer}", [1, 1024], F32))
                B_sx1, B_sx21 = Buf(), Buf()
                rstdb1 = ste.enter_context(nc.sbuf_tensor(f"rstdb1_{layer}", [128, 1024], F32))
                negmrb1 = ste.enter_context(nc.sbuf_tensor(f"negmrb1_{layer}", [128, 1024], F32))
                xscr6 = ste.enter_context(nc.sbuf_tensor(f"xscr6_{layer}", [128, 1024], F32))
                ret = (ste, pse, sx1, sx21, B_sx1, B_sx21, s_ps, B_sps,
                       rstdb1, negmrb1, xscr6,
                       [xb1, sq1, gsb, tsb], [B_xb1, B_sq1, B_gsb, B_tsb])

            dma("gpsimd", "dg", bml[:], ap["bm0" if is0 else "lb1"], writes=[B_bml])
            load_panel(wo_name, 0, 128, 0, nch=nsc)
            if is0:
                load_panel(wf_name, 0, 128, 2)
            for oc in range(NCH):
                so, sf = oc % 2, 2 + (oc % 2)
                if oc + 1 < NCH:
                    load_panel(wo_name, (oc + 1) * 128, 128, (oc + 1) % 2, nch=nsc)
                    if is0 or oc + 1 >= 8:
                        load_panel(wf_name, (oc + 1) * 128, 128, 2 + ((oc + 1) % 2))
                po = oc % 2
                if is0:
                    dma("sync", "ds", xob[po][:], xloc_t[oc][:, 256:1280], writes=[B_xob[po]])
                else:
                    dma("sync", "ds", xob[po][:], x1d_t[oc], reads=[B_x1d[oc]], writes=[B_xob[po]])
                fns = []
                for c in range(nsc):
                    for t in range(2):
                        fns.append(lambda e, c=c, t=t, so=so: e.matmul(
                            s_ps[:, t * 512:(t + 1) * 512], wp[so][:, c * 128:(c + 1) * 128],
                            big2[:, c * TOK + t * 512: c * TOK + (t + 1) * 512],
                            start=(c == 0), stop=(c == nsc - 1)))
                P.emit("tensor", fns, reads=[B_wp[so], B_const] + B_big[:nsc],
                       writes=[B_sps], incs=[("te", 1)])
                gpre = (not is0) and oc < 8
                if not gpre:
                    fns = []
                    for c in range(NCH):
                        for t in range(2):
                            fns.append(lambda e, c=c, t=t, sf=sf: e.matmul(
                                g_ps[:, t * 512:(t + 1) * 512], wp[sf][:, c * 128:(c + 1) * 128],
                                xn[:, c * TOK + t * 512: c * TOK + (t + 1) * 512],
                                start=(c == 0), stop=False))
                    for t in range(2):
                        fns.append(lambda e, t=t, oc=oc: e.matmul(
                            g_ps[:, t * 512:(t + 1) * 512], bml[:, oc * 128:(oc + 1) * 128],
                            hw[:, t * 512:(t + 1) * 512], start=False, stop=True))
                    P.emit("tensor", fns, reads=[B_wp[sf], B_bml, B_hw, B_const] + B_xn,
                           writes=[B_gps], incs=[("te", 1)])
                    P.emit("scalar", lambda e, oc=oc: e.activation(out=gsb[:], in_=g_ps[:],
                           func=AF.Silu, bias=bias_sb[:, oc:oc + 1], scale=1.0),
                           reads=[B_gps, B_const], writes=[B_gsb], incs=[("sc", 1)])
                else:
                    P.emit("scalar", lambda e, oc=oc: e.activation(out=gsb[:],
                           in_=big2[:, (24 + oc) * TOK:(25 + oc) * TOK],
                           func=AF.Silu, bias=bias_sb[:, oc:oc + 1], scale=1.0),
                           reads=[B_big[24 + oc], B_const], writes=[B_gsb], incs=[("sc", 1)])
                P.emit("vector", lambda e: e.tensor_tensor(out=tsb[:], in0=s_ps[:], in1=gsb[:],
                       op=ALU.mult), reads=[B_sps, B_gsb], writes=[B_tsb], incs=[("ve", 1)])
                P.emit("gpsimd", lambda e, po=po: e.tensor_tensor(out=xob[po][:], in0=xob[po][:],
                       in1=tsb[:], op=ALU.add),
                       reads=[B_xob[po], B_tsb], writes=[B_xob[po]], incs=[("gp", 1)])
                if is0:
                    dma("sync", "ds", x1d_t[oc], xob[po][:], reads=[B_xob[po]], writes=[B_x1d[oc]])
                    P.emit("vector", lambda e, po=po: e.tensor_copy(out=xb1[:], in_=xob[po][:]),
                           reads=[B_xob[po]], writes=[B_xb1], incs=[("ve", 1)])
                    dma("sync", "ds", x1b_t[oc], xb1[:], reads=[B_xb1], writes=[B_x1b[oc]])
                    P.emit("vector", lambda e: e.tensor_tensor(out=sq1[:], in0=xb1[:], in1=xb1[:],
                           op=ALU.mult), reads=[B_xb1], writes=[B_sq1], incs=[("ve", 1)])
                    for (src, Bs, pst, Bp) in ((xb1, B_xb1, sx1, B_sx1), (sq1, B_sq1, sx21, B_sx21)):
                        fns = [lambda e, src=src, pst=pst, t=t, oc=oc: e.matmul(
                            pst[0:1, t * 512:(t + 1) * 512], ones1b[:], src[:, t * 512:(t + 1) * 512],
                            start=(oc == 0), stop=(oc == NCH - 1)) for t in range(2)]
                        P.emit("tensor", fns, reads=[Bs, B_const], writes=[Bp], incs=[("te", 1)])
                else:
                    dma("sync", "ds", y_t[oc], xob[po][:], reads=[B_xob[po]], writes=[B_y])
            if not is0:
                P.join_all()
                pse.close()
                ste.close()
            return ret

        (ste0, pse0, sx1, sx21, B_sx1, B_sx21, sps0, B_sps0,
         rstdb1, negmrb1, xscr6, x6, B_x6) = emit_epilogue(0)

        B_stat1, B_rstdb1, B_negmrb1 = Buf(), Buf(), Buf()
        B_xscr6 = Buf()
        emit_stats_finalize(B_stat1, sx1, sx21, B_sx1, B_sx21,
                            rstdb1, negmrb1, B_rstdb1, B_negmrb1, 1024,
                            sps0[:, 0:512], B_sps0)
        P.join("vector")
        P.join("tensor")
        P.join("scalar")
        for c in range(NCH):
            p = c % 4
            dma("sync", "ds", x6[p][:], x1b_t[c], reads=[B_x1b[c]], writes=[B_x6[p]])
            P.emit("vector", lambda e, p=p: e.tensor_tensor(out=xscr6[:], in0=x6[p][:],
                   in1=rstdb1[:], op=ALU.mult),
                   reads=[B_x6[p], B_rstdb1], writes=[B_xscr6], incs=[("ve", 1)])
            P.emit("vector", lambda e, c=c: e.tensor_tensor(out=xn[:, c * TOK:(c + 1) * TOK],
                   in0=xscr6[:], in1=negmrb1[:], op=ALU.add),
                   reads=[B_xscr6, B_negmrb1], writes=[B_xn[c]], incs=[("ve", 1)])
        P.join_all()
        pse0.close()
        ste0.close()
        if KDEBUG:
            for c in range(NCH):
                dma("sync", "ds", dbg["dxn1"].rearrange("(c p) t -> c p t", p=128)[c],
                    xn[:, c * TOK:(c + 1) * TOK], reads=[B_xn[c]])

        st7 = ExitStack()
        kevac = st7.enter_context(nc.sbuf_tensor("kevac", [128, 1024], BF16))
        B_kevac = Buf()
        ps7 = ExitStack()
        p7 = ps7.enter_context(nc.psum_tensor("p7", [128, 1024], F32))
        B_p7 = Buf()
        kvs_t = kvs.rearrange("(c p) t -> c p t", p=128)
        QS = 8 * TOK

        for (wname, base) in (("w1k", 0), ("w1v", 8)):
            for oc in range(8):
                slot = oc % 2
                load_panel(wname, oc * 128, 128, slot)
                fns = []
                for c in range(NCH):
                    for t in range(2):
                        fns.append(lambda e, c=c, t=t, slot=slot: e.matmul(
                            p7[:, t * 512:(t + 1) * 512], wp[slot][:, c * 128:(c + 1) * 128],
                            xn[:, c * TOK + t * 512: c * TOK + (t + 1) * 512],
                            start=(c == 0), stop=(c == NCH - 1)))
                P.emit("tensor", fns, reads=[B_wp[slot], B_const] + B_xn,
                       writes=[B_p7], incs=[("te", 1)])
                P.emit("scalar", lambda e: e.copy(out=kevac[:], in_=p7[:]),
                       reads=[B_p7], writes=[B_kevac], incs=[("sc", 1)])
                dma("sync", "ds", kvs_t[base + oc], kevac[:], reads=[B_kevac], writes=[B_kvs])
        P.join_all()
        ps7.close()

        emit_gate_lora(1)

        ps7 = ExitStack()
        p7 = ps7.enter_context(nc.psum_tensor("p7b", [128, 1024], F32))
        B_p7 = Buf()

        def load_wqs_panel(oc, slot):
            src = wqall.rearrange("r i p o -> p (r i) o")[:, :, oc * 128:(oc + 1) * 128]
            dstv = wp[slot][:, 0:NCH * 128].rearrange("p (c o) -> p c o", c=NCH)
            dma("gpsimd", "dg", dstv, src, reads=[B_wqall], writes=[B_wp[slot]])

        load_wqs_panel(0, 0)
        for oc in range(8):
            slot = oc % 2
            if oc + 1 < 8:
                load_wqs_panel(oc + 1, (oc + 1) % 2)
            fns = []
            for c in range(NCH):
                for t in range(2):
                    fns.append(lambda e, c=c, t=t, slot=slot: e.matmul(
                        p7[:, t * 512:(t + 1) * 512], wp[slot][:, c * 128:(c + 1) * 128],
                        xn[:, c * TOK + t * 512: c * TOK + (t + 1) * 512],
                        start=(c == 0), stop=(c == NCH - 1)))
            P.emit("tensor", fns, reads=[B_wp[slot], B_const] + B_xn,
                   writes=[B_p7], incs=[("te", 1)])
            P.emit("scalar", lambda e, oc=oc: e.copy(
                   out=big2[:, QS + oc * TOK: QS + (oc + 1) * TOK], in_=p7[:]),
                   reads=[B_p7], writes=[B_big[8 + oc]], incs=[("sc", 1)])
        P.join_all()
        ps7.close()
        st7.close()
        if KDEBUG:
            for c in range(8):
                dma("sync", "ds", dbg["dqs"].rearrange("(c p) t -> c p t", p=128)[c],
                    big2[:, QS + c * TOK: QS + (c + 1) * TOK], reads=[B_big[8 + c]])

        if KDEBUG:
            dma("sync", "ds", dbg["dhw1"], hw[:], reads=[B_hw])

        st9 = ExitStack()
        Kh2 = [st9.enter_context(nc.sbuf_tensor(f"Kh{i}", [128, 2048], BF16)) for i in range(2)]
        Vh2 = [st9.enter_context(nc.sbuf_tensor(f"Vh{i}", [128, 2048], BF16)) for i in range(2)]
        Vtm2 = [st9.enter_context(nc.sbuf_tensor(f"Vtm{i}", [128, 2048], BF16)) for i in range(2)]
        rdenS = st9.enter_context(nc.sbuf_tensor("rdenS", [1, 512], F32))
        bcS = st9.enter_context(nc.sbuf_tensor("bcS", [128, 512], F32))
        bm9 = st9.enter_context(nc.sbuf_tensor("bm9", [128, 4096], BF16))
        B_bm9 = Buf()
        B_Kh2 = [Buf(), Buf()]
        B_Vh2 = [Buf(), Buf()]
        B_Vtm2 = [Buf(), Buf()]
        B_rden, B_bcS = Buf(), Buf()
        PTB = 16 * TOK
        B_pt = [Buf() for _ in range(8)]
        ps9 = ExitStack()
        sc9 = [ps9.enter_context(nc.psum_tensor(f"sc9_{i}", [128, 512], F32)) for i in range(3)]
        av9 = ps9.enter_context(nc.psum_tensor("av9", [128, 512], F32))
        den9 = ps9.enter_context(nc.psum_tensor("den9", [1, 512], F32))
        bc9 = ps9.enter_context(nc.psum_tensor("bc9", [128, 512], F32))
        vt9 = ps9.enter_context(nc.psum_tensor("vt9", [128, 1024], BF16))
        g9 = ps9.enter_context(nc.psum_tensor("g9", [128, 512], F32))
        B_sc9 = [Buf() for _ in range(3)]
        B_av9, B_den9, B_bc9, B_vt9, B_g9 = Buf(), Buf(), Buf(), Buf(), Buf()

        def emit_g_filler():
            th = []
            th.append(lambda: dma("gpsimd", "dg", bm9[:], ap["lb1"], writes=[B_bm9]))
            th.append(lambda: load_panel("bw1", 0, 128, 0))
            for oc in range(8):
                slot = oc % 2
                if oc + 1 < 8:
                    th.append(lambda oc=oc: load_panel("bw1", (oc + 1) * 128, 128, (oc + 1) % 2))
                for t in range(2):
                    for g in range(8):
                        fns = []
                        for c in range(g * 4, g * 4 + 4):
                            fns.append(lambda e, c=c, t=t, slot=slot: e.matmul(
                                g9[:], wp[slot][:, c * 128:(c + 1) * 128],
                                xn[:, c * TOK + t * 512: c * TOK + (t + 1) * 512],
                                start=(c == 0), stop=False))
                        th.append(lambda fns=fns, slot=slot: P.emit("tensor", fns,
                            reads=[B_wp[slot], B_const] + B_xn, writes=[B_g9], incs=[("te", 1)]))
                    th.append(lambda t=t, oc=oc: P.emit("tensor",
                        lambda e, t=t, oc=oc: e.matmul(g9[:], bm9[:, oc * 128:(oc + 1) * 128],
                            hw[:, t * 512:(t + 1) * 512], start=False, stop=True),
                        reads=[B_bm9, B_hw, B_const], writes=[B_g9], incs=[("te", 1)]))
                    th.append(lambda t=t, oc=oc: P.emit("scalar", lambda e, t=t, oc=oc: e.copy(
                        out=big2[:, (24 + oc) * TOK + t * 512:(24 + oc) * TOK + (t + 1) * 512],
                        in_=g9[:]), reads=[B_g9], writes=[B_big[24 + oc]], incs=[("sc", 1)]))
            return th

        gfill = emit_g_filler()
        gfi = 0

        def gf(n=1):
            nonlocal gfi
            for _ in range(n):
                if gfi < len(gfill):
                    gfill[gfi]()
                    gfi += 1

        def emit_kv_load(h):
            hp = h % 2
            for r in range(2):
                dma("sync", "ds", Kh2[hp][:, r * 1024:(r + 1) * 1024],
                    kva[r, h * 128:(h + 1) * 128, :], reads=[B_kva], writes=[B_Kh2[hp]],
                    waits=[("cc", P.cnt["cc"])])
                dma("sync", "ds", Vh2[hp][:, r * 1024:(r + 1) * 1024],
                    kva[r, 1024 + h * 128: 1024 + (h + 1) * 128, :],
                    reads=[B_kva], writes=[B_Vh2[hp]], waits=[("cc", P.cnt["cc"])])

        pendfin = [None]

        def emit_fin():
            if pendfin[0] is None:
                return
            h, qgv = pendfin[0]
            pendfin[0] = None
            P.emit("vector", lambda e: e.reciprocal(out=rdenS[:], in_=den9[0:1, :]),
                   reads=[B_den9], writes=[B_rden], incs=[("ve", 1)])
            P.emit("tensor", lambda e: e.matmul(bc9[:], onesbf[:], rdenS[:],
                   start=True, stop=True), reads=[B_rden, B_const], writes=[B_bc9],
                   incs=[("te", 1)])
            P.emit("scalar", lambda e: e.copy(out=bcS[:], in_=bc9[:]),
                   reads=[B_bc9], writes=[B_bcS], incs=[("sc", 1)])
            P.emit("vector", lambda e, h=h, qgv=qgv: e.tensor_tensor(
                   out=big2[:, h * TOK + qgv * 512: h * TOK + (qgv + 1) * 512],
                   in0=av9[:], in1=bcS[:], op=ALU.mult),
                   reads=[B_av9, B_bcS], writes=[B_big[h]], incs=[("ve", 1)])

        gf(44)
        _kvw = [(f"ds{i}", P.cnt[f"ds{i}"]) for i in range(NLANES["ds"])]
        P.emit("gpsimd", lambda e: e.collective_compute(
            "AllGather", ALU.bypass, replica_groups=[[0, 1], [2, 3], [4, 5], [6, 7]],
            ins=[kvs], outs=[kva]), reads=[B_kvs], writes=[B_kva], incs=[("cc", 1)],
            waits=_kvw)

        emit_kv_load(0)
        for h in range(NKV):
            hp = h % 2
            Kh, Vh, Vtm = Kh2[hp], Vh2[hp], Vtm2[hp]
            B_Kh, B_Vh, B_Vtm = B_Kh2[hp], B_Vh2[hp], B_Vtm2[hp]
            if h + 1 < NKV:
                emit_kv_load(h + 1)
            for g in range(2):
                fns = [lambda e, g=g, t=t, Vh=Vh: e.transpose(vt9[:, t * 128:(t + 1) * 128],
                       Vh[:, (g * 8 + t) * 128:(g * 8 + t + 1) * 128], identb[:]) for t in range(8)]
                P.emit("tensor", fns, reads=[B_Vh, B_const], writes=[B_vt9], incs=[("te", 1)])
                P.emit("vector", lambda e, g=g, Vtm=Vtm: e.tensor_copy(
                       out=Vtm[:, g * 1024:(g + 1) * 1024],
                       in_=vt9[:]), reads=[B_vt9], writes=[B_Vtm], incs=[("ve", 1)])
                gf(1)
            for qgv in range(2):
                Qs = big2[:, QS + h * TOK + qgv * 512: QS + h * TOK + (qgv + 1) * 512]

                def s_mm(kc, Kh=Kh, B_Kh=B_Kh, Qs=Qs, h=h):
                    P.emit("tensor", lambda e, kc=kc, Kh=Kh, Qs=Qs: e.matmul(
                        sc9[kc % 3][:], Kh[:, kc * 128:(kc + 1) * 128], Qs,
                        start=True, stop=True),
                        reads=[B_Kh, B_big[8 + h]], writes=[B_sc9[kc % 3]], incs=[("te", 1)])

                def e_act(kc):
                    P.emit("scalar", lambda e, kc=kc: e.activation(
                        out=big2[:, PTB + (kc % 8) * 512: PTB + (kc % 8 + 1) * 512],
                        in_=sc9[kc % 3][:], func=AF.Exp, scale=SCALE),
                        reads=[B_sc9[kc % 3]], writes=[B_pt[kc % 8]], incs=[("sc", 1)])

                for kc in range(3):
                    s_mm(kc)
                    e_act(kc)
                emit_fin()
                for kc in range(16):
                    pt = big2[:, PTB + (kc % 8) * 512: PTB + (kc % 8 + 1) * 512]
                    P.emit("tensor", lambda e, pt=pt, kc=kc, Vtm=Vtm: e.matmul(
                        av9[:], Vtm[:, kc * 128:(kc + 1) * 128], pt,
                        start=(kc == 0), stop=(kc == 15)),
                        reads=[B_pt[kc % 8], B_Vtm], writes=[B_av9], incs=[("te", 1)])
                    P.emit("tensor", lambda e, pt=pt, kc=kc: e.matmul(
                        den9[:], ones1b[:], pt, start=(kc == 0), stop=(kc == 15)),
                        reads=[B_pt[kc % 8], B_const], writes=[B_den9], incs=[("te", 1)])
                    if kc + 3 < 16:
                        s_mm(kc + 3)
                        e_act(kc + 3)
                    gf(1)
                pendfin[0] = (h, qgv)
        emit_fin()
        gf(len(gfill))

        P.join_all()
        ps9.close()
        st9.close()
        if KDEBUG:
            for c in range(8):
                dma("sync", "ds", dbg["dval"].rearrange("(c p) t -> c p t", p=128)[c],
                    big2[:, c * TOK:(c + 1) * TOK], reads=[B_big[c]])
            for c in range(NCH):
                dma("sync", "ds", dbg["dx1"].rearrange("(c p) t -> c p t", p=128)[c],
                    x1d_t[c], reads=[B_x1d[c]])

        emit_epilogue(1)

        P.join("sync")

        with nc.Block() as block:
            prog.replay(block)
    return nc


def kernel(**inputs):
    f32 = np.float32
    x = np.asarray(inputs["L0_attn_x"], f32)
    W = {}
    W["w0q"] = np.ascontiguousarray(np.asarray(inputs["L0_attn_Wq"], f32).T)
    W["w0k"] = np.ascontiguousarray(np.asarray(inputs["L0_attn_Wk"], f32).T)
    W["w0v"] = np.ascontiguousarray(np.asarray(inputs["L0_attn_Wv"], f32).T)
    W["w0o"] = np.ascontiguousarray(np.asarray(inputs["L0_attn_Wo"], f32).T)
    W["w0f"] = np.ascontiguousarray(np.asarray(inputs["L0_ffn_W"], f32).T)
    W["w0fq"] = np.ascontiguousarray(np.asarray(inputs["L0_ffn_Wq"], f32).T)
    W["w0fk"] = np.ascontiguousarray(np.asarray(inputs["L0_ffn_Wk"], f32).T)
    W["a0"] = np.ascontiguousarray(np.asarray(inputs["L0_ffn_A"], f32).reshape(128, D).T)
    W["bm0"] = np.ascontiguousarray(np.asarray(inputs["L0_ffn_Bm"], f32).transpose(0, 2, 1).reshape(128, D))
    w1qf = np.asarray(inputs["L1_attn_wq"], f32)
    W["w1k"] = np.ascontiguousarray(np.asarray(inputs["L1_attn_wk"], f32))
    W["w1v"] = np.ascontiguousarray(np.asarray(inputs["L1_attn_wv"], f32))
    W["w1o"] = np.ascontiguousarray(np.asarray(inputs["L1_attn_wo"], f32))
    W["gw1"] = np.ascontiguousarray(np.asarray(inputs["L1_ffn_gate_w"], f32).T)
    W["bw1"] = np.ascontiguousarray(np.asarray(inputs["L1_ffn_base_w"], f32).T)
    W["la1"] = np.ascontiguousarray(np.asarray(inputs["L1_ffn_lora_A"], f32).reshape(128, D).T)
    W["lb1"] = np.ascontiguousarray(np.asarray(inputs["L1_ffn_lora_B"], f32).transpose(0, 2, 1).reshape(128, D))
    b0 = np.ascontiguousarray(np.asarray(inputs["L0_ffn_b"], f32).reshape(32, 128).T)
    b1 = np.ascontiguousarray(np.asarray(inputs["L1_ffn_base_b"], f32).reshape(32, 128).T)

    consts = [_consts_for(0), _consts_for(1)]
    in_maps = []
    for core in range(8):
        b, half = core // 2, core % 2
        off = 1024 * half
        lo, hi = off - 256, off + 1280
        s0, s1 = max(lo, 0), min(hi, 2048)
        xl = np.zeros((LOC, D), f32)
        xl[s0 - lo:s1 - lo] = x[b, s0:s1]
        m = {"xloc": np.ascontiguousarray(xl.T), "b0": b0, "b1": b1,
             "w1qsl": np.ascontiguousarray(
                 w1qf[core * 512:(core + 1) * 512].reshape(4, 128, 4096))}
        m.update(W)
        m.update(consts[half])
        in_maps.append(m)

    nc = build_nc()
    res = run_bass_kernel_spmd(nc, in_maps, core_ids=list(range(8)))
    out = np.zeros((4, 2048, 4096), np.float32)
    for core in range(8):
        b, half = core // 2, core % 2
        out[b, 1024 * half:1024 * (half + 1)] = res.results[core]["y"].T
    return out


# version 14
# speedup vs baseline: 1.0099x; 1.0099x over previous
import numpy as np
import ml_dtypes
import concourse.bass as bass
import concourse.mybir as mybir
from concourse.bass_utils import run_bass_kernel_spmd
from contextlib import ExitStack

F32 = mybir.dt.float32
BF16 = mybir.dt.bfloat16
AF = mybir.ActivationFunctionType
ALU = mybir.AluOpType
AXX = mybir.AxisListType.X

KDEBUG = False

D = 4096
NCH = 32
TOK = 1024
LOC = 1536
H0 = 32
NKV = 8
SCALE = 1.0 / float(np.sqrt(128.0))
NEG = -1e9


class Buf:
    __slots__ = ("w", "r")

    def __init__(self):
        self.w = None
        self.r = {}

    def rd_waits(self):
        return [self.w] if self.w is not None else []

    def wr_waits(self):
        ws = dict(self.r)
        if self.w is not None:
            s, v = self.w
            ws[s] = max(ws.get(s, -1), v)
        return list(ws.items())


class Prog:
    ENGS = ("tensor", "vector", "scalar", "gpsimd", "sync")

    def __init__(self, nc):
        self.nc = nc
        self.ev = {e: [] for e in self.ENGS}
        self.sems = {}
        self.cnt = {}
        self.lastwait = {e: {} for e in self.ENGS}

    def sem(self, name):
        self.sems[name] = self.nc.alloc_semaphore(name)
        self.cnt[name] = 0

    def emit(self, eng, fns, reads=(), writes=(), waits=(), incs=()):
        if callable(fns):
            fns = [fns]
        wset = {}
        for s, v in waits:
            wset[s] = max(wset.get(s, -1), v)
        for b in reads:
            for s, v in b.rd_waits():
                wset[s] = max(wset.get(s, -1), v)
        for b in writes:
            for s, v in b.wr_waits():
                wset[s] = max(wset.get(s, -1), v)
        fw = []
        for s, v in wset.items():
            if self.lastwait[eng].get(s, -1) < v:
                fw.append((self.sems[s], v))
                self.lastwait[eng][s] = v
        marks = []
        for s, n in incs:
            self.cnt[s] += n
            marks.append((s, self.cnt[s]))
        mark = marks[0] if marks else None
        self.ev[eng].append((fns, fw, [(self.sems[s], n) for s, n in incs]))
        if mark is not None:
            for b in reads:
                b.r[mark[0]] = max(b.r.get(mark[0], -1), mark[1])
            for b in writes:
                b.w = mark
                b.r = {}
        return mark

    def join(self, eng, sems=None):
        if sems is None:
            sems = list(self.sems)
        fw = []
        for s in sems:
            v = self.cnt[s]
            if v > 0 and self.lastwait[eng].get(s, -1) < v:
                fw.append((self.sems[s], v))
                self.lastwait[eng][s] = v
        if fw:
            self.ev[eng].append(([], fw, []))

    def join_all(self):
        for e in self.ENGS:
            self.join(e)

    def replay(self, block):
        def mk(engname):
            def body(e):
                for fns, fw, incs in self.ev[engname]:
                    for h, v in fw:
                        e.wait_ge(h, v)
                    ins = None
                    for f in fns:
                        ins = f(e)
                    for h, n in incs:
                        assert ins is not None
                        ins = ins.then_inc(h, n)
            return body
        block.tensor(mk("tensor"))
        block.vector(mk("vector"))
        block.scalar(mk("scalar"))
        block.gpsimd(mk("gpsimd"))
        block.sync(mk("sync"))


def _rope_cache(S=2048, Dh=128, base=10000.0):
    inv = 1.0 / base ** (np.arange(0, Dh, 2, dtype=np.float32) / Dh)
    t = np.arange(S, dtype=np.float32)
    f = np.outer(t, inv)
    cos = np.zeros((S, Dh), np.float32)
    sin = np.zeros((S, Dh), np.float32)
    cos[:, 0::2] = np.cos(f); cos[:, 1::2] = np.cos(f)
    sin[:, 0::2] = np.sin(f); sin[:, 1::2] = np.sin(f)
    return cos, sin


def _consts_for(half):
    off = 1024 * half
    cosF, sinF = _rope_cache()
    qpos = np.arange(off, off + TOK)
    kpos = np.clip(np.arange(off - 256, off + 1280), 0, 2047)
    c = {}
    c["cosq"] = np.ascontiguousarray((cosF[qpos] * SCALE).T).astype(ml_dtypes.bfloat16)
    c["sinq"] = np.ascontiguousarray((sinF[qpos] * SCALE).T).astype(ml_dtypes.bfloat16)
    c["cosk"] = np.ascontiguousarray(cosF[kpos].T).astype(ml_dtypes.bfloat16)
    c["sink"] = np.ascontiguousarray(sinF[kpos].T).astype(ml_dtypes.bfloat16)
    P = np.zeros((128, 128), np.float32)
    for d in range(64):
        P[d, d + 64] = -1.0
        P[d + 64, d] = 1.0
    c["permT"] = np.ascontiguousarray(P.T).astype(ml_dtypes.bfloat16)
    c["identb"] = np.eye(128, dtype=ml_dtypes.bfloat16)
    c["identf"] = np.eye(8, dtype=np.float32)
    mask = np.zeros((128, 12), np.float32)
    if half == 0:
        mask[:, 0] = NEG; mask[:, 1] = NEG
    else:
        mask[:, 10] = NEG; mask[:, 11] = NEG
    c["maskb"] = mask
    ramp = np.arange(256, dtype=np.float32) / 255.0
    coef = np.zeros((8, 512), np.float32)
    for w in range(5):
        if w >= 1:
            coef[w, 0:256] = ramp
        if w <= 3:
            coef[w, 256:512] = 1.0 - ramp
    if half == 0:
        coef[0, 256:512] = 0.0
        coef[1, 0:256] = 1.0
    coef2 = np.zeros((8, 1024), np.float32)
    coef2[0, 0:512] = coef[1]; coef2[1, 0:512] = coef[0]; coef2[2, 0:512] = coef[2]
    coef2[0, 512:1024] = coef[3]; coef2[1, 512:1024] = coef[4]
    c["coef"] = coef2
    ROW_OF = {1: 0, 0: 1, 2: 2, 3: 0, 4: 1}
    basis = np.zeros((128, 40), ml_dtypes.bfloat16)
    for w in range(5):
        basis[:, w * 8 + ROW_OF[w]] = 1.0
    c["basis"] = basis
    c["ones1b"] = np.ones((128, 1), ml_dtypes.bfloat16)
    c["onesbf"] = np.ones((1, 128), np.float32)
    rs = np.zeros((8, 6 * 128), ml_dtypes.bfloat16)
    for w in range(6):
        rs[w, w * 128:(w + 1) * 128] = 1.0
    c["rowselb"] = rs
    blk = np.zeros((128, 16), ml_dtypes.bfloat16)
    for oc in range(2):
        for p in range(128):
            blk[p, oc * 8 + oc * 4 + p // 32] = 1.0
    c["blocksel"] = blk
    ex1 = np.zeros((8, 128), ml_dtypes.bfloat16)
    ex2 = np.zeros((8, 128), ml_dtypes.bfloat16)
    for m in range(128):
        ex1[m // 16, m] = 1.0
        ex2[m // 16, m] = 2.0
    c["expand1"] = ex1
    c["expand2"] = ex2
    return c


_CONST_SHAPES = {
    "cosq": ([128, 1024], BF16), "sinq": ([128, 1024], BF16),
    "cosk": ([128, 1536], BF16), "sink": ([128, 1536], BF16),
    "permT": ([128, 128], BF16), "identb": ([128, 128], BF16),
    "identf": ([8, 8], F32), "maskb": ([128, 12], F32),
    "coef": ([8, 1024], F32), "basis": ([128, 40], BF16),
    "ones1b": ([128, 1], BF16), "rowselb": ([8, 768], BF16),
    "onesbf": ([1, 128], F32), "blocksel": ([128, 16], BF16),
    "expand1": ([8, 128], BF16), "expand2": ([8, 128], BF16),
}

_WEIGHT_SHAPES = {
    "w0q": [D, D], "w0k": [D, 128], "w0v": [D, 128], "w0o": [D, D],
    "w0f": [D, D], "w0fq": [D, 256], "w0fk": [D, 256],
    "a0": [D, 128], "bm0": [128, D],
    "w1k": [D, 1024], "w1v": [D, 1024], "w1o": [1024, D],
    "gw1": [D, 8], "bw1": [D, D], "la1": [D, 128], "lb1": [128, D],
}


def build_nc():
    nc = bass.Bass("TRN2", target_bir_lowering=False)
    ap = {}
    ap["xloc"] = nc.dram_tensor("xloc", [D, LOC], F32, kind="ExternalInput").ap()
    for n, shp in _WEIGHT_SHAPES.items():
        ap[n] = nc.dram_tensor(n, shp, F32, kind="ExternalInput").ap()
    ap["b0"] = nc.dram_tensor("b0", [128, 32], F32, kind="ExternalInput").ap()
    ap["b1"] = nc.dram_tensor("b1", [128, 32], F32, kind="ExternalInput").ap()
    for n, (shp, dt) in _CONST_SHAPES.items():
        ap[n] = nc.dram_tensor(n, shp, dt, kind="ExternalInput").ap()
    y = nc.dram_tensor("y", [D, TOK], F32, kind="ExternalOutput").ap()
    dbg = {}
    if KDEBUG:
        for n, shp, dt in [("dxn", [D, TOK], BF16), ("dao", [D, TOK], BF16),
                           ("dx1", [D, TOK], F32), ("dxn1", [D, TOK], BF16),
                           ("dqs", [1024, TOK], BF16), ("dval", [1024, TOK], BF16),
                           ("dhw", [128, TOK], BF16), ("dhw1", [128, TOK], BF16),
                           ("dkr", [128, LOC], BF16)]:
            dbg[n] = nc.dram_tensor(n, shp, dt, kind="ExternalOutput").ap()
    ap["w1qsl"] = nc.dram_tensor("w1qsl", [4, 128, D], F32, kind="ExternalInput").ap()
    x1d = nc.dram_tensor("x1d", [D, TOK], F32).ap()
    x1b = nc.dram_tensor("x1b", [D, TOK], BF16).ap()
    kvs = nc.dram_tensor("kvs", [2048, TOK], BF16).ap()
    kva = nc.dram_tensor("kva", [2, 2048, TOK], BF16).ap()
    wqpart = nc.dram_tensor("wqpart", [4, 128, 1024], BF16).ap()
    wqall = nc.dram_tensor("wqall", [8, 4, 128, 1024], BF16, addr_space="Shared").ap()

    st = ExitStack()
    with st:
        xn = st.enter_context(nc.sbuf_tensor("xn", [128, NCH * TOK], BF16))
        big2 = st.enter_context(nc.sbuf_tensor("big2", [128, NCH * TOK], BF16))
        wp = [st.enter_context(nc.sbuf_tensor(f"wp{i}", [128, 4096], BF16)) for i in range(4)]
        hw = st.enter_context(nc.sbuf_tensor("hw", [128, 1024], BF16))
        permT = st.enter_context(nc.sbuf_tensor("permT_s", [128, 128], BF16))
        identb = st.enter_context(nc.sbuf_tensor("identb_s", [128, 128], BF16))
        identf = st.enter_context(nc.sbuf_tensor("identf_s", [8, 8], F32))
        maskb = st.enter_context(nc.sbuf_tensor("maskb_s", [128, 12], F32))
        basis = st.enter_context(nc.sbuf_tensor("basis_s", [128, 40], BF16))
        ones1b = st.enter_context(nc.sbuf_tensor("ones1b_s", [128, 1], BF16))
        rowselb = st.enter_context(nc.sbuf_tensor("rowselb_s", [8, 768], BF16))
        onesbf = st.enter_context(nc.sbuf_tensor("onesbf_s", [1, 128], F32))
        blocksel = st.enter_context(nc.sbuf_tensor("blocksel_s", [128, 16], BF16))
        expand1 = st.enter_context(nc.sbuf_tensor("expand1_s", [8, 128], BF16))
        expand2 = st.enter_context(nc.sbuf_tensor("expand2_s", [8, 128], BF16))
        b0sb = st.enter_context(nc.sbuf_tensor("b0sb", [128, 32], F32))
        b1sb = st.enter_context(nc.sbuf_tensor("b1sb", [128, 32], F32))

        prog = Prog(nc)
        for s in ("te", "ve", "sc", "gp", "cc"):
            prog.sem(s)
        NLANES = {"ds": 8, "dg": 6}
        for q, nl in NLANES.items():
            for i in range(nl):
                prog.sem(f"{q}{i}")
        _rr = {"ds": 0, "dg": 0}
        P = prog

        B_xn = [Buf() for _ in range(NCH)]
        B_big = [Buf() for _ in range(NCH)]
        B_wp = [Buf() for _ in range(4)]
        B_hw = Buf()
        B_const = Buf()
        B_x1d = [Buf() for _ in range(NCH)]
        B_x1b = [Buf() for _ in range(NCH)]
        B_kvs = Buf()
        B_kva = Buf()
        B_y = Buf()

        def dma(eng, q, out, in_, reads=(), writes=(), waits=(), **kw):
            lane = f"{q}{_rr[q]}"
            _rr[q] = (_rr[q] + 1) % NLANES[q]
            w = list(waits) + [(lane, P.cnt[lane])]
            return P.emit(eng, lambda e: e.dma_start(out=out, in_=in_, **kw),
                          reads=reads, writes=writes, waits=w, incs=[(lane, 16)])

        for sb, n in [(permT, "permT"), (identb, "identb"), (identf, "identf"),
                      (maskb, "maskb"), (basis, "basis"), (ones1b, "ones1b"),
                      (rowselb, "rowselb"), (onesbf, "onesbf"), (blocksel, "blocksel"),
                      (expand1, "expand1"), (expand2, "expand2"),
                      (b0sb, "b0"), (b1sb, "b1")]:
            dma("sync", "ds", sb[:], ap[n], writes=[B_const])

        def load_panel(wname, c0, ncol, slot, nch=NCH):
            src = ap[wname].rearrange("(c p) o -> p c o", p=128)[:, :, c0:c0 + ncol]
            dstv = wp[slot][:, 0:nch * ncol].rearrange("p (c o) -> p c o", c=nch)
            return dma("gpsimd", "dg", dstv, src, writes=[B_wp[slot]])

        def emit_stats_finalize(B_stat, sx_ps, sx2_ps, B_sxps, B_sx2ps,
                                rstdb, negmrb, B_rstdb, B_negmrb, ncols, bc_ps, B_bcps):
            inv = 1.0 / float(D)
            mrow = negmrb[0:1, 0:ncols]
            scr = rstdb[0:1, 0:ncols]
            P.emit("vector", lambda e: e.tensor_scalar(out=mrow, in0=sx_ps[0:1, 0:ncols],
                   scalar1=inv, scalar2=None, op0=ALU.mult),
                   reads=[B_sxps], writes=[B_stat], incs=[("ve", 1)])
            P.emit("vector", lambda e: e.tensor_tensor(out=scr, in0=mrow, in1=mrow, op=ALU.mult),
                   reads=[B_stat], writes=[B_stat], incs=[("ve", 1)])
            P.emit("vector", lambda e: e.scalar_tensor_tensor(out=scr, in0=sx2_ps[0:1, 0:ncols],
                   scalar=inv, in1=scr, op0=ALU.mult, op1=ALU.subtract),
                   reads=[B_sx2ps, B_stat], writes=[B_stat], incs=[("ve", 1)])
            P.emit("vector", lambda e: e.tensor_scalar(out=scr, in0=scr,
                   scalar1=1e-5, scalar2=None, op0=ALU.add),
                   reads=[B_stat], writes=[B_stat], incs=[("ve", 1)])
            P.emit("scalar", lambda e: e.activation(out=scr, in_=scr, func=AF.Sqrt),
                   reads=[B_stat], writes=[B_stat], incs=[("sc", 1)])
            P.emit("vector", lambda e: e.reciprocal(out=scr, in_=scr),
                   reads=[B_stat], writes=[B_stat], incs=[("ve", 1)])
            P.emit("vector", lambda e: e.tensor_tensor(out=mrow, in0=mrow, in1=scr, op=ALU.mult),
                   reads=[B_stat], writes=[B_stat], incs=[("ve", 1)])
            P.emit("vector", lambda e: e.tensor_scalar(out=mrow, in0=mrow,
                   scalar1=-1.0, scalar2=None, op0=ALU.mult),
                   reads=[B_stat], writes=[B_stat], incs=[("ve", 1)])
            for (srcr, dst, Bd) in ((mrow, negmrb, B_negmrb), (scr, rstdb, B_rstdb)):
                for s in range(0, ncols, 512):
                    n = min(512, ncols - s)
                    P.emit("tensor", lambda e, srcr=srcr, s=s, n=n: e.matmul(
                        bc_ps[:, 0:n], onesbf[:], srcr[0:1, s:s + n],
                        start=True, stop=True),
                        reads=[B_stat, B_const], writes=[B_bcps], incs=[("te", 1)])
                    P.emit("scalar", lambda e, dst=dst, s=s, n=n: e.copy(
                        out=dst[:, s:s + n], in_=bc_ps[:, 0:n]),
                        reads=[B_bcps, B_stat], writes=[Bd], incs=[("sc", 1)])

        xb = big2[:, 0:1536]
        sqv = big2[:, 1536:3072]
        coskb = big2[:, 3072:4608]
        sinkb = big2[:, 4608:6144]
        kraw = big2[:, 6144:7680]
        xnh = big2[:, 7680:8192]
        B_xb, B_sq, B_ck, B_sk, B_kraw, B_xnh = Buf(), Buf(), Buf(), Buf(), Buf(), Buf()

        st23 = ExitStack()
        Kr = st23.enter_context(nc.sbuf_tensor("Kr", [128, LOC], BF16))
        V0 = st23.enter_context(nc.sbuf_tensor("V0", [128, LOC], BF16))
        B_Kr, B_V0 = Buf(), Buf()

        st12 = ExitStack()
        xf = [st12.enter_context(nc.sbuf_tensor(f"xf{i}", [128, LOC], F32)) for i in range(2)]
        rstdb = st12.enter_context(nc.sbuf_tensor("rstdb", [128, LOC], F32))
        negmrb = st12.enter_context(nc.sbuf_tensor("negmrb", [128, LOC], F32))
        xscr = st12.enter_context(nc.sbuf_tensor("xscr", [128, LOC], F32))
        B_xf = [Buf(), Buf()]
        B_stat, B_rstdb, B_negmrb, B_xscr = Buf(), Buf(), Buf(), Buf()

        ps12 = ExitStack()
        sx_ps = ps12.enter_context(nc.psum_tensor("sx_ps", [1, LOC], F32))
        sx2_ps = ps12.enter_context(nc.psum_tensor("sx2_ps", [1, LOC], F32))
        bc_ps = ps12.enter_context(nc.psum_tensor("bc_ps", [128, 512], F32))
        B_sxps, B_sx2ps, B_bcps = Buf(), Buf(), Buf()

        xloc_t = ap["xloc"].rearrange("(c p) t -> c p t", p=128)

        for c in range(NCH):
            p = c % 2
            dma("sync", "ds", xf[p][:], xloc_t[c], writes=[B_xf[p]])
            P.emit("vector", lambda e, p=p: e.tensor_copy(out=xb, in_=xf[p][:]),
                   reads=[B_xf[p]], writes=[B_xb], incs=[("ve", 1)])
            P.emit("vector", lambda e: e.tensor_tensor(out=sqv, in0=xb, in1=xb, op=ALU.mult),
                   reads=[B_xb], writes=[B_sq], incs=[("ve", 1)])
            for (src, Bs, pst, Bp) in ((xb, B_xb, sx_ps, B_sxps), (sqv, B_sq, sx2_ps, B_sx2ps)):
                fns = [lambda e, src=src, pst=pst, s=s, c=c: e.matmul(
                    pst[0:1, s * 512:(s + 1) * 512], ones1b[:], src[:, s * 512:(s + 1) * 512],
                    start=(c == 0), stop=(c == NCH - 1)) for s in range(3)]
                P.emit("tensor", fns, reads=[Bs, B_const], writes=[Bp], incs=[("te", 1)])

        emit_stats_finalize(B_stat, sx_ps, sx2_ps, B_sxps, B_sx2ps,
                            rstdb, negmrb, B_rstdb, B_negmrb, LOC, bc_ps, B_bcps)
        P.join_all()
        ps12.close()

        psKV = ExitStack()
        k_ps = psKV.enter_context(nc.psum_tensor("k_ps", [128, LOC], F32))
        v_ps = psKV.enter_context(nc.psum_tensor("v_ps", [128, LOC], F32))
        pkp = psKV.enter_context(nc.psum_tensor("pkp", [128, 512], F32))
        vt_ps = psKV.enter_context(nc.psum_tensor("vt_ps", [128, 768], BF16))
        B_kps, B_vps, B_pkp, B_vtps = Buf(), Buf(), Buf(), Buf()

        load_panel("w0k", 0, 128, 2)
        load_panel("w0v", 0, 128, 3)

        for c in range(NCH):
            p = c % 2
            dma("sync", "ds", xf[p][:], xloc_t[c], writes=[B_xf[p]])
            P.emit("vector", lambda e, p=p: e.tensor_tensor(out=xscr[:], in0=xf[p][:],
                   in1=rstdb[:], op=ALU.mult),
                   reads=[B_xf[p], B_rstdb], writes=[B_xscr], incs=[("ve", 1)])
            P.emit("vector", lambda e, c=c: e.tensor_tensor(out=xn[:, c * TOK:(c + 1) * TOK],
                   in0=xscr[:, 256:1280], in1=negmrb[:, 256:1280], op=ALU.add),
                   reads=[B_xscr, B_negmrb], writes=[B_xn[c]], incs=[("ve", 1)])
            P.emit("vector", lambda e: e.tensor_tensor(out=xnh[:, 0:256],
                   in0=xscr[:, 0:256], in1=negmrb[:, 0:256], op=ALU.add),
                   reads=[B_xscr, B_negmrb], writes=[B_xnh], incs=[("ve", 1)])
            P.emit("vector", lambda e: e.tensor_tensor(out=xnh[:, 256:512],
                   in0=xscr[:, 1280:1536], in1=negmrb[:, 1280:1536], op=ALU.add),
                   reads=[B_xscr, B_negmrb], writes=[B_xnh], incs=[("ve", 1)])
            segs = [(0, 256, "h", 0, True, False), (256, 512, "o", 0, False, True),
                    (512, 1024, "o", 256, True, True),
                    (1024, 1280, "o", 768, True, False), (1280, 1536, "h", 256, False, True)]
            for (pst, Bp, slot) in ((k_ps, B_kps, 2), (v_ps, B_vps, 3)):
                fns = []
                for (a, b, kind, rc, bf, bl) in segs:
                    if kind == "o":
                        rhs = xn[:, c * TOK + rc: c * TOK + rc + (b - a)]
                    else:
                        rhs = xnh[:, rc: rc + (b - a)]
                    fns.append(lambda e, pst=pst, a=a, b=b, rhs=rhs, slot=slot, c=c,
                               bf=bf, bl=bl: e.matmul(
                        pst[:, a:b], wp[slot][:, c * 128:(c + 1) * 128], rhs,
                        start=(c == 0 and bf), stop=(c == NCH - 1 and bl)))
                P.emit("tensor", fns, reads=[B_xn[c], B_xnh, B_wp[slot], B_const],
                       writes=[Bp], incs=[("te", 1)])

        dma("sync", "ds", coskb, ap["cosk"], writes=[B_ck])
        dma("sync", "ds", sinkb, ap["sink"], writes=[B_sk])
        P.emit("scalar", lambda e: e.copy(out=kraw, in_=k_ps[:]),
               reads=[B_kps], writes=[B_kraw], incs=[("sc", 1)])
        P.emit("vector", lambda e: e.tensor_tensor(out=Kr[:], in0=kraw, in1=coskb, op=ALU.mult),
               reads=[B_kraw, B_ck], writes=[B_Kr], incs=[("ve", 1)])
        for s in range(3):
            P.emit("tensor", lambda e, s=s: e.matmul(pkp[:], permT[:],
                   kraw[:, s * 512:(s + 1) * 512], start=True, stop=True),
                   reads=[B_kraw, B_const], writes=[B_pkp], incs=[("te", 1)])
            P.emit("vector", lambda e, s=s: e.tensor_tensor(out=xb[:, 0:512], in0=pkp[:],
                   in1=sinkb[:, s * 512:(s + 1) * 512], op=ALU.mult),
                   reads=[B_pkp, B_sk], writes=[B_xb], incs=[("ve", 1)])
            P.emit("vector", lambda e, s=s: e.tensor_tensor(out=Kr[:, s * 512:(s + 1) * 512],
                   in0=Kr[:, s * 512:(s + 1) * 512], in1=xb[:, 0:512], op=ALU.add),
                   reads=[B_Kr, B_xb], writes=[B_Kr], incs=[("ve", 1)])
        P.emit("scalar", lambda e: e.copy(out=kraw, in_=v_ps[:]),
               reads=[B_vps], writes=[B_kraw], incs=[("sc", 1)])
        for g in range(2):
            fns = [lambda e, t=t, g=g: e.transpose(vt_ps[:, t * 128:(t + 1) * 128],
                   kraw[:, (g * 6 + t) * 128:(g * 6 + t + 1) * 128], identb[:]) for t in range(6)]
            P.emit("tensor", fns, reads=[B_kraw, B_const], writes=[B_vtps], incs=[("te", 1)])
            P.emit("vector", lambda e, g=g: e.tensor_copy(out=V0[:, g * 768:(g + 1) * 768],
                   in_=vt_ps[:]), reads=[B_vtps], writes=[B_V0], incs=[("ve", 1)])

        P.join_all()
        psKV.close()
        st12.close()

        B_wqall = Buf()
        B_wqpart = [Buf() for _ in range(4)]
        for i in range(4):
            a0 = (24 + 4 * (i % 2)) * TOK
            Av = big2[:, a0: a0 + 4096].rearrange("p (h x) -> p h x", x=512)
            bbufs = [B_big[24 + 4 * (i % 2) + k] for k in range(4)]
            dma("gpsimd", "dg", big2[:, a0: a0 + 4096], ap["w1qsl"][i], writes=bbufs)
            P.emit("vector", lambda e, Av=Av: e.tensor_tensor(
                out=Av[:, :, 0:128], in0=Av[:, :, 0:128], in1=Av[:, :, 128:256], op=ALU.add),
                reads=bbufs, writes=bbufs, incs=[("ve", 1)])
            P.emit("vector", lambda e, Av=Av: e.tensor_tensor(
                out=Av[:, :, 256:384], in0=Av[:, :, 256:384], in1=Av[:, :, 384:512], op=ALU.add),
                reads=bbufs, writes=bbufs, incs=[("ve", 1)])
            P.emit("vector", lambda e, Av=Av: e.tensor_tensor(
                out=Av[:, :, 128:256], in0=Av[:, :, 0:128], in1=Av[:, :, 256:384], op=ALU.add),
                reads=bbufs, writes=bbufs, incs=[("ve", 1)])
            dma("sync", "ds", wqpart[i].rearrange("p (h j) -> p h j", j=128),
                Av[:, :, 128:256], reads=bbufs, writes=[B_wqpart[i]])
        P.emit("gpsimd", lambda e: e.collective_compute(
            "AllGather", ALU.bypass, replica_groups=[[0, 1, 2, 3, 4, 5, 6, 7]],
            ins=[wqpart], outs=[wqall]), reads=B_wqpart, writes=[B_wqall],
            incs=[("cc", 1)])

        if KDEBUG:
            for c in range(NCH):
                dma("sync", "ds", dbg["dxn"].rearrange("(c p) t -> c p t", p=128)[c],
                    xn[:, c * TOK:(c + 1) * TOK], reads=[B_xn[c]])
            dma("sync", "ds", dbg["dkr"], Kr[:], reads=[B_Kr])

        st3 = ExitStack()
        qsb = st3.enter_context(nc.sbuf_tensor("qsb", [128, 1024], BF16))
        qr2 = [st3.enter_context(nc.sbuf_tensor(f"qr{i}", [128, 1024], BF16)) for i in range(2)]
        qtmp = st3.enter_context(nc.sbuf_tensor("qtmp", [128, 1024], BF16))
        rtmp = st3.enter_context(nc.sbuf_tensor("rtmp", [128, 1024], BF16))
        sig = st3.enter_context(nc.sbuf_tensor("sig", [128, 4096], BF16))
        csb = st3.enter_context(nc.sbuf_tensor("csb", [128, 512], BF16))
        cosq = st3.enter_context(nc.sbuf_tensor("cosq_s", [128, 1024], BF16))
        sinq = st3.enter_context(nc.sbuf_tensor("sinq_s", [128, 1024], BF16))
        coef = st3.enter_context(nc.sbuf_tensor("coef_s", [8, 1024], F32))
        csr = st3.enter_context(nc.sbuf_tensor("csr", [8, 512], F32))
        cs = st3.enter_context(nc.sbuf_tensor("cs", [8, 1024], BF16))
        B_qsb, B_qtmp, B_rtmp, B_sig, B_csb = Buf(), Buf(), Buf(), Buf(), Buf()
        B_qr2 = [Buf(), Buf()]
        B_csr = Buf()
        B_csA, B_csB = Buf(), Buf()
        dma("sync", "ds", cosq[:], ap["cosq"], writes=[B_const])
        dma("sync", "ds", sinq[:], ap["sinq"], writes=[B_const])
        dma("sync", "ds", coef[:], ap["coef"], writes=[B_const])

        ps3 = ExitStack()
        scp = ps3.enter_context(nc.psum_tensor("scp", [128, 512], F32))
        qg = ps3.enter_context(nc.psum_tensor("qg", [128, 512], F32))
        wob = [ps3.enter_context(nc.psum_tensor(f"wob{i}", [128, 512], F32)) for i in range(3)]
        denA = ps3.enter_context(nc.psum_tensor("denA", [8, 512], F32))
        denB = ps3.enter_context(nc.psum_tensor("denB", [8, 512], F32))
        csb_ps = ps3.enter_context(nc.psum_tensor("csb_ps", [128, 512], F32))
        B_scp, B_qg = Buf(), Buf()
        B_wob = [Buf() for _ in range(3)]
        B_denA, B_denB, B_csbps = Buf(), Buf(), Buf()

        WGEO = [(0, 256, 256), (0, 512, 0), (256, 512, 0), (512, 512, 0), (768, 256, 0)]
        WBANK = {1: 0, 0: 1, 2: 2, 3: 0, 4: 1}
        ROW_OF = {1: 0, 0: 1, 2: 2, 3: 0, 4: 1}
        DGRP = {1: 0, 0: 0, 2: 0, 3: 1, 4: 1}

        def emit_qgemm_filler(h):
            if h >= H0:
                return []
            slot = h % 2
            par = h % 2
            th = []
            if h + 1 < H0:
                th.append(lambda h=h: load_panel("w0q", (h + 1) * 128, 128, (h + 1) % 2))
            for t in range(2):
                for g in range(8):
                    fns = []
                    for c in range(g * 4, g * 4 + 4):
                        fns.append(lambda e, c=c, t=t, slot=slot: e.matmul(
                            qg[:], wp[slot][:, c * 128:(c + 1) * 128],
                            xn[:, c * TOK + t * 512: c * TOK + (t + 1) * 512],
                            start=(c == 0), stop=(c == NCH - 1)))
                    th.append(lambda fns=fns, slot=slot: P.emit(
                        "tensor", fns, reads=[B_wp[slot], B_const] + B_xn,
                        writes=[B_qg], incs=[("te", 1)]))
                th.append(lambda t=t: P.emit("scalar",
                    lambda e, t=t: e.copy(out=qsb[:, t * 512:(t + 1) * 512], in_=qg[:]),
                    reads=[B_qg], writes=[B_qsb], incs=[("sc", 1)]))
            for t in range(2):
                th.append(lambda t=t: P.emit("tensor",
                    lambda e, t=t: e.matmul(qg[:], permT[:], qsb[:, t * 512:(t + 1) * 512],
                                            start=True, stop=True),
                    reads=[B_qsb, B_const], writes=[B_qg], incs=[("te", 1)]))
                th.append(lambda t=t, par=par: P.emit("vector",
                    lambda e, t=t, par=par: e.tensor_tensor(
                        out=qr2[par][:, t * 512:(t + 1) * 512], in0=qg[:],
                        in1=sinq[:, t * 512:(t + 1) * 512], op=ALU.mult),
                    reads=[B_qg, B_const], writes=[B_qr2[par]], incs=[("ve", 1)]))
            th.append(lambda par=par: P.emit("vector",
                lambda e: e.tensor_tensor(out=rtmp[:], in0=qsb[:], in1=cosq[:], op=ALU.mult),
                reads=[B_qsb, B_const], writes=[B_rtmp], incs=[("ve", 1)]))
            th.append(lambda par=par: P.emit("vector",
                lambda e, par=par: e.tensor_tensor(out=qr2[par][:], in0=qr2[par][:],
                                                   in1=rtmp[:], op=ALU.add),
                reads=[B_rtmp], writes=[B_qr2[par]], incs=[("ve", 1)]))
            return th

        def emit_unit(h, w, j, with_av, fill=None):
            par = h % 2
            qa, qn, g0 = WGEO[w]
            kc = 2 * w + j
            slotj = j if w not in (4, 0) else 4 + j
            dgrp = DGRP[w]
            dps, Bd = (denA, B_denA) if dgrp == 0 else (denB, B_denB)
            P.emit("tensor", lambda e, kc=kc, qa=qa, qn=qn, g0=g0, par=par: e.matmul(
                scp[:, g0:g0 + qn], Kr[:, kc * 128:(kc + 1) * 128],
                qr2[par][:, qa:qa + qn], start=True, stop=True),
                reads=[B_Kr, B_qr2[par]], writes=[B_scp], incs=[("te", 1)])
            P.emit("scalar", lambda e, slotj=slotj, kc=kc, g0=g0, qn=qn: e.activation(
                out=sig[:, slotj * 512 + g0: slotj * 512 + g0 + qn], in_=scp[:, g0:g0 + qn],
                func=AF.Sigmoid, bias=maskb[:, kc:kc + 1], scale=1.0),
                reads=[B_scp, B_const], writes=[B_sig], incs=[("sc", 1)])
            if fill is not None:
                fill(1)
            P.emit("tensor", lambda e, w=w, j=j, g0=g0, qn=qn, slotj=slotj, dps=dps: e.matmul(
                dps[:, g0:g0 + qn], basis[:, w * 8:(w + 1) * 8],
                sig[:, slotj * 512 + g0: slotj * 512 + g0 + qn],
                start=(w in (1, 3) and j == 0), stop=(w in (2, 4) and j == 3)),
                reads=[B_sig, B_const], writes=[Bd], incs=[("te", 1)])
            if with_av:
                emit_av(h, w, j)

        def emit_av(h, w, j):
            qa, qn, g0 = WGEO[w]
            kc = 2 * w + j
            slotj = j if w not in (4, 0) else 4 + j
            bk = WBANK[w]
            P.emit("tensor", lambda e, w=w, j=j, kc=kc, g0=g0, qn=qn, slotj=slotj, bk=bk: e.matmul(
                wob[bk][:, g0:g0 + qn], V0[:, kc * 128:(kc + 1) * 128],
                sig[:, slotj * 512 + g0: slotj * 512 + g0 + qn], start=(j == 0), stop=(j == 3)),
                reads=[B_sig, B_V0], writes=[B_wob[bk]], incs=[("te", 1)])

        def emit_recip(grp):
            dps, Bd = (denA, B_denA) if grp == 0 else (denB, B_denB)
            Bcs = B_csA if grp == 0 else B_csB
            o = grp * 512
            P.emit("vector", lambda e, dps=dps: e.tensor_scalar(out=csr[:], in0=dps[:],
                   scalar1=1e-30, scalar2=None, op0=ALU.max),
                   reads=[Bd], writes=[B_csr], incs=[("ve", 1)])
            P.emit("vector", lambda e: e.reciprocal(out=csr[:], in_=csr[:]),
                   reads=[B_csr], writes=[B_csr], incs=[("ve", 1)])
            P.emit("vector", lambda e, o=o: e.tensor_tensor(out=cs[:, o:o + 512],
                   in0=coef[:, o:o + 512], in1=csr[:], op=ALU.mult),
                   reads=[B_csr, B_const], writes=[Bcs], incs=[("ve", 1)])

        BROW = {0: ((0, 1), (0, 0)), 1: ((0, 0), (0, 2)),
                2: ((0, 2), (1, 0)), 3: ((1, 0), (1, 1))}

        def emit_blend(h, j):
            (fp, fr), (rp, rr) = BROW[j]
            Bf = B_csA if fp == 0 else B_csB
            Br = B_csA if rp == 0 else B_csB
            P.emit("tensor", [
                lambda e, fp=fp, fr=fr: e.matmul(csb_ps[:, 0:256],
                    rowselb[:, fr * 128:(fr + 1) * 128],
                    cs[0:8, fp * 512 + 256: fp * 512 + 512], start=True, stop=True),
                lambda e, rp=rp, rr=rr: e.matmul(csb_ps[:, 256:512],
                    rowselb[:, rr * 128:(rr + 1) * 128],
                    cs[0:8, rp * 512: rp * 512 + 256], start=True, stop=True),
            ], reads=[Bf, Br, B_const], writes=[B_csbps], incs=[("te", 1)])
            P.emit("scalar", lambda e: e.copy(out=csb[:], in_=csb_ps[:]),
                   reads=[B_csbps], writes=[B_csb], incs=[("sc", 1)])
            bf, br = WBANK[j], WBANK[j + 1]
            P.emit("vector", lambda e, bf=bf: e.tensor_tensor(out=qtmp[:, 0:256],
                   in0=wob[bf][:, 256:512], in1=csb[:, 0:256], op=ALU.mult),
                   reads=[B_wob[bf], B_csb], writes=[B_qtmp], incs=[("ve", 1)])
            P.emit("vector", lambda e, br=br: e.tensor_tensor(out=qtmp[:, 256:512],
                   in0=wob[br][:, 0:256], in1=csb[:, 256:512], op=ALU.mult),
                   reads=[B_wob[br], B_csb], writes=[B_qtmp], incs=[("ve", 1)])
            P.emit("vector", lambda e, j=j, h=h: e.tensor_tensor(
                   out=big2[:, h * TOK + j * 256: h * TOK + (j + 1) * 256],
                   in0=qtmp[:, 0:256], in1=qtmp[:, 256:512], op=ALU.add),
                   reads=[B_qtmp], writes=[B_big[h]], incs=[("ve", 1)])

        load_panel("w0q", 0, 128, 0)
        for th in emit_qgemm_filler(0):
            th()
        for h in range(H0):
            filler = list(emit_qgemm_filler(h + 1))
            fi = 0

            def fill(n=1):
                nonlocal fi
                for _ in range(n):
                    if fi < len(filler):
                        filler[fi]()
                        fi += 1

            for j in range(4):
                emit_unit(h, 1, j, with_av=False, fill=fill)
            if h > 0:
                emit_blend(h - 1, 2)
                fill(1)
                emit_blend(h - 1, 3)
                fill(1)
            for j in range(4):
                emit_av(h, 1, j)
            fill(1)
            for w in (0, 2):
                for j in range(4):
                    emit_unit(h, w, j, with_av=True, fill=fill)
            emit_recip(0)
            for j in range(4):
                emit_unit(h, 3, j, with_av=False, fill=fill)
            emit_blend(h, 0)
            fill(1)
            emit_blend(h, 1)
            fill(1)
            for j in range(4):
                emit_av(h, 3, j)
            fill(1)
            for j in range(4):
                emit_unit(h, 4, j, with_av=False, fill=fill)
            emit_recip(1)
            for j in range(4):
                emit_av(h, 4, j)
            fill(len(filler))
            if h == H0 - 1:
                emit_blend(h, 2)
                emit_blend(h, 3)

        P.join_all()
        ps3.close()
        st3.close()
        st23.close()

        if KDEBUG:
            for c in range(NCH):
                dma("sync", "ds", dbg["dao"].rearrange("(c p) t -> c p t", p=128)[c],
                    big2[:, c * TOK:(c + 1) * TOK], reads=[B_big[c]])

        def emit_gate_lora(layer):
            stg = ExitStack()
            g1 = stg.enter_context(nc.sbuf_tensor(f"g1_{layer}", [128, 2048], BF16))
            lg = stg.enter_context(nc.sbuf_tensor(f"lg_{layer}", [8, 1024], F32))
            lgT = stg.enter_context(nc.sbuf_tensor(f"lgT_{layer}", [128, 64], F32))
            tA = stg.enter_context(nc.sbuf_tensor(f"tA_{layer}", [128, 64], F32))
            tB = stg.enter_context(nc.sbuf_tensor(f"tB_{layer}", [128, 64], F32))
            tC = stg.enter_context(nc.sbuf_tensor(f"tC_{layer}", [128, 8], F32))
            tD = stg.enter_context(nc.sbuf_tensor(f"tD_{layer}", [128, 8], F32))
            wtk = stg.enter_context(nc.sbuf_tensor(f"wtk_{layer}", [128, 64], BF16))
            wfe = stg.enter_context(nc.sbuf_tensor(f"wfe_{layer}", [8, 1024], BF16))
            B_g1, B_lg, B_lgT = Buf(), Buf(), Buf()
            B_tA, B_tB, B_tC, B_tD, B_wtk, B_wfe = Buf(), Buf(), Buf(), Buf(), Buf(), Buf()

            psg = ExitStack()
            pg1 = psg.enter_context(nc.psum_tensor(f"pg1_{layer}", [128, 1024], F32))
            pg2 = psg.enter_context(nc.psum_tensor(f"pg2_{layer}", [128, 1024], F32))
            psc = psg.enter_context(nc.psum_tensor(f"psc_{layer}", [8, 1024], F32))
            B_psc = Buf()
            ptr = psg.enter_context(nc.psum_tensor(f"ptr_{layer}", [128, 64], F32))
            pwf = psg.enter_context(nc.psum_tensor(f"pwf_{layer}", [8, 1024], BF16))
            B_pg1, B_pg2, B_ptr, B_pwf = Buf(), Buf(), Buf(), Buf()

            def gemm(pst, Bp, slot, ncol=128, extra=None):
                fns = []
                for c in range(NCH):
                    for t in range(2):
                        fns.append(lambda e, c=c, t=t, slot=slot, ncol=ncol, pst=pst: e.matmul(
                            pst[:, t * 512:(t + 1) * 512] if ncol == 128 else
                            pst[0:ncol, t * 512:(t + 1) * 512],
                            wp[slot][:, c * ncol:(c + 1) * ncol],
                            xn[:, c * TOK + t * 512: c * TOK + (t + 1) * 512],
                            start=(c == 0), stop=(c == NCH - 1)))
                P.emit("tensor", fns, reads=[B_wp[slot], B_const] + B_xn,
                       writes=[Bp], incs=[("te", 1)])

            if layer == 0:
                for oc in range(2):
                    load_panel("w0fq", oc * 128, 128, 0)
                    gemm(pg1, B_pg1, 0)
                    P.emit("scalar", lambda e: e.copy(out=g1[:, 0:1024], in_=pg1[:]),
                           reads=[B_pg1], writes=[B_g1], incs=[("sc", 1)])
                    load_panel("w0fk", oc * 128, 128, 1)
                    gemm(pg2, B_pg2, 1)
                    P.emit("vector", lambda e: e.tensor_tensor(out=g1[:, 1024:2048],
                           in0=pg2[:], in1=g1[:, 0:1024], op=ALU.mult),
                           reads=[B_pg2, B_g1], writes=[B_g1], incs=[("ve", 1)])
                    fns = [lambda e, t=t, oc=oc: e.matmul(psc[0:8, t * 512:(t + 1) * 512],
                           blocksel[:, oc * 8:(oc + 1) * 8],
                           g1[:, 1024 + t * 512:1024 + (t + 1) * 512],
                           start=(oc == 0), stop=(oc == 1)) for t in range(2)]
                    P.emit("tensor", fns, reads=[B_g1, B_const], writes=[B_psc], incs=[("te", 1)])
                P.emit("scalar", lambda e: e.copy(out=lg[:], in_=psc[0:8, :]),
                       reads=[B_psc], writes=[B_lg], incs=[("sc", 1)])
            else:
                load_panel("gw1", 0, 8, 0)
                gemm(pg1, B_pg1, 0, ncol=8)
                P.emit("scalar", lambda e: e.copy(out=lg[:], in_=pg1[0:8, :]),
                       reads=[B_pg1], writes=[B_lg], incs=[("sc", 1)])

            fns = [lambda e, tc=tc: e.transpose(ptr[:, tc * 8:(tc + 1) * 8],
                   lg[0:8, tc * 128:(tc + 1) * 128], identf[:]) for tc in range(8)]
            P.emit("tensor", fns, reads=[B_lg, B_const], writes=[B_ptr], incs=[("te", 1)])
            P.emit("vector", lambda e: e.tensor_copy(out=lgT[:], in_=ptr[:]),
                   reads=[B_ptr], writes=[B_lgT], incs=[("ve", 1)])
            v3 = lgT[:].rearrange("p (a b) -> p a b", a=8)
            P.emit("vector", lambda e: e.reduce_max(out=tC[:], in_=v3, axis=AXX),
                   reads=[B_lgT], writes=[B_tC], incs=[("ve", 1)])
            if layer == 0:
                P.emit("vector", lambda e: e.tensor_tensor(out=tA[:], in0=lgT[:],
                       in1=tC[:].rearrange("p (a b) -> p a b", b=1).broadcast_to([128, 8, 8]), op=ALU.subtract),
                       reads=[B_lgT, B_tC], writes=[B_tA], incs=[("ve", 1)])
                P.emit("scalar", lambda e: e.activation(out=tA[:], in_=tA[:], func=AF.Exp,
                       scale=float(1.0 / np.sqrt(32.0))),
                       reads=[B_tA], writes=[B_tA], incs=[("sc", 1)])
                P.emit("vector", lambda e: e.reduce_sum(out=tC[:], in_=tA[:].rearrange(
                       "p (a b) -> p a b", a=8), axis=AXX),
                       reads=[B_tA], writes=[B_tC], incs=[("ve", 1)])
                P.emit("vector", lambda e: e.reciprocal(out=tD[:], in_=tC[:]),
                       reads=[B_tC], writes=[B_tD], incs=[("ve", 1)])
                P.emit("vector", lambda e: e.tensor_tensor(out=wtk[:], in0=tA[:],
                       in1=tD[:].rearrange("p (a b) -> p a b", b=1).broadcast_to([128, 8, 8]), op=ALU.mult),
                       reads=[B_tA, B_tD], writes=[B_wtk], incs=[("ve", 1)])
            else:
                P.emit("vector", lambda e: e.tensor_tensor(out=tA[:], in0=lgT[:],
                       in1=tC[:].rearrange("p (a b) -> p a b", b=1).broadcast_to([128, 8, 8]), op=ALU.is_equal),
                       reads=[B_lgT, B_tC], writes=[B_tA], incs=[("ve", 1)])
                P.emit("vector", lambda e: e.scalar_tensor_tensor(out=tA[:], in0=tA[:],
                       scalar=NEG, in1=lgT[:], op0=ALU.mult, op1=ALU.add),
                       reads=[B_tA, B_lgT], writes=[B_tA], incs=[("ve", 1)])
                P.emit("vector", lambda e: e.reduce_max(out=tD[:], in_=tA[:].rearrange(
                       "p (a b) -> p a b", a=8), axis=AXX),
                       reads=[B_tA], writes=[B_tD], incs=[("ve", 1)])
                P.emit("vector", lambda e: e.tensor_tensor(out=tA[:], in0=lgT[:],
                       in1=tC[:].rearrange("p (a b) -> p a b", b=1).broadcast_to([128, 8, 8]), op=ALU.subtract),
                       reads=[B_lgT, B_tC], writes=[B_tA], incs=[("ve", 1)])
                P.emit("scalar", lambda e: e.activation(out=tA[:], in_=tA[:], func=AF.Exp),
                       reads=[B_tA], writes=[B_tA], incs=[("sc", 1)])
                P.emit("vector", lambda e: e.tensor_tensor(out=tB[:], in0=lgT[:],
                       in1=tD[:].rearrange("p (a b) -> p a b", b=1).broadcast_to([128, 8, 8]), op=ALU.is_ge),
                       reads=[B_lgT, B_tD], writes=[B_tB], incs=[("ve", 1)])
                P.emit("vector", lambda e: e.tensor_tensor(out=tD[:], in0=tD[:], in1=tC[:],
                       op=ALU.subtract), reads=[B_tD, B_tC], writes=[B_tD], incs=[("ve", 1)])
                P.emit("scalar", lambda e: e.activation(out=tD[:], in_=tD[:], func=AF.Exp),
                       reads=[B_tD], writes=[B_tD], incs=[("sc", 1)])
                P.emit("vector", lambda e: e.tensor_scalar(out=tD[:], in0=tD[:], scalar1=1.0,
                       scalar2=None, op0=ALU.add), reads=[B_tD], writes=[B_tD], incs=[("ve", 1)])
                P.emit("vector", lambda e: e.reciprocal(out=tD[:], in_=tD[:]),
                       reads=[B_tD], writes=[B_tD], incs=[("ve", 1)])
                P.emit("vector", lambda e: e.tensor_tensor(out=tA[:], in0=tA[:], in1=tB[:],
                       op=ALU.mult), reads=[B_tA, B_tB], writes=[B_tA], incs=[("ve", 1)])
                P.emit("vector", lambda e: e.tensor_tensor(out=wtk[:], in0=tA[:],
                       in1=tD[:].rearrange("p (a b) -> p a b", b=1).broadcast_to([128, 8, 8]), op=ALU.mult),
                       reads=[B_tA, B_tD], writes=[B_wtk], incs=[("ve", 1)])

            fns = [lambda e, tc=tc: e.transpose(pwf[:, tc * 128:(tc + 1) * 128],
                   wtk[:, tc * 8:(tc + 1) * 8], identb[:]) for tc in range(8)]
            P.emit("tensor", fns, reads=[B_wtk, B_const], writes=[B_pwf], incs=[("te", 1)])
            P.emit("scalar", lambda e: e.copy(out=wfe[:], in_=pwf[:]),
                   reads=[B_pwf], writes=[B_wfe], incs=[("sc", 1)])
            exp = expand1 if layer == 0 else expand2
            fns = [lambda e, t=t, exp=exp: e.matmul(pg1[:, t * 512:(t + 1) * 512], exp[:],
                   wfe[:, t * 512:(t + 1) * 512], start=True, stop=True) for t in range(2)]
            P.emit("tensor", fns, reads=[B_wfe, B_const], writes=[B_pg1], incs=[("te", 1)])
            load_panel("a0" if layer == 0 else "la1", 0, 128, 0)
            gemm(pg2, B_pg2, 0)
            P.emit("scalar", lambda e: e.copy(out=g1[:, 0:1024], in_=pg2[:]),
                   reads=[B_pg2], writes=[B_g1], incs=[("sc", 1)])
            P.emit("vector", lambda e: e.tensor_tensor(out=hw[:], in0=pg1[:], in1=g1[:, 0:1024],
                   op=ALU.mult), reads=[B_pg1, B_g1], writes=[B_hw], incs=[("ve", 1)])
            P.join_all()
            psg.close()
            stg.close()

        emit_gate_lora(0)
        if KDEBUG:
            dma("sync", "ds", dbg["dhw"], hw[:], reads=[B_hw])

        x1d_t = x1d.rearrange("(c p) t -> c p t", p=128)
        x1b_t = x1b.rearrange("(c p) t -> c p t", p=128)
        y_t = y.rearrange("(c p) t -> c p t", p=128)

        def emit_epilogue(layer):
            is0 = layer == 0
            wo_name = "w0o" if is0 else "w1o"
            wf_name = "w0f" if is0 else "bw1"
            bias_sb = b0sb if is0 else b1sb
            nsc = NCH if is0 else 8
            ste = ExitStack()
            bml = ste.enter_context(nc.sbuf_tensor(f"bml_{layer}", [128, 4096], BF16))
            gsb = ste.enter_context(nc.sbuf_tensor(f"gsb_{layer}", [128, 1024], BF16))
            tsb = ste.enter_context(nc.sbuf_tensor(f"tsb_{layer}", [128, 1024], BF16))
            xob = [ste.enter_context(nc.sbuf_tensor(f"xob{layer}_{i}", [128, 1024], F32)) for i in range(2)]
            B_bml, B_gsb, B_tsb = Buf(), Buf(), Buf()
            B_xob = [Buf(), Buf()]
            ret = None
            pse = ExitStack()
            s_ps = pse.enter_context(nc.psum_tensor(f"s_ps_{layer}", [128, 1024], F32))
            g_ps = pse.enter_context(nc.psum_tensor(f"g_ps_{layer}", [128, 1024], F32))
            B_sps, B_gps = Buf(), Buf()
            if is0:
                xb1 = ste.enter_context(nc.sbuf_tensor(f"xb1_{layer}", [128, 1024], BF16))
                sq1 = ste.enter_context(nc.sbuf_tensor(f"sq1_{layer}", [128, 1024], BF16))
                B_xb1, B_sq1 = Buf(), Buf()
                sx1 = pse.enter_context(nc.psum_tensor(f"sx1_{layer}", [1, 1024], F32))
                sx21 = pse.enter_context(nc.psum_tensor(f"sx21_{layer}", [1, 1024], F32))
                B_sx1, B_sx21 = Buf(), Buf()
                rstdb1 = ste.enter_context(nc.sbuf_tensor(f"rstdb1_{layer}", [128, 1024], F32))
                negmrb1 = ste.enter_context(nc.sbuf_tensor(f"negmrb1_{layer}", [128, 1024], F32))
                xscr6 = ste.enter_context(nc.sbuf_tensor(f"xscr6_{layer}", [128, 1024], F32))
                ret = (ste, pse, sx1, sx21, B_sx1, B_sx21, s_ps, B_sps,
                       rstdb1, negmrb1, xscr6,
                       [xb1, sq1, gsb, tsb], [B_xb1, B_sq1, B_gsb, B_tsb])

            dma("gpsimd", "dg", bml[:], ap["bm0" if is0 else "lb1"], writes=[B_bml])
            load_panel(wo_name, 0, 128, 0, nch=nsc)
            if is0:
                load_panel(wf_name, 0, 128, 2)
            for oc in range(NCH):
                so, sf = oc % 2, 2 + (oc % 2)
                if oc + 1 < NCH:
                    load_panel(wo_name, (oc + 1) * 128, 128, (oc + 1) % 2, nch=nsc)
                    if is0 or oc + 1 >= 8:
                        load_panel(wf_name, (oc + 1) * 128, 128, 2 + ((oc + 1) % 2))
                po = oc % 2
                if is0:
                    dma("sync", "ds", xob[po][:], xloc_t[oc][:, 256:1280], writes=[B_xob[po]])
                else:
                    dma("sync", "ds", xob[po][:], x1d_t[oc], reads=[B_x1d[oc]], writes=[B_xob[po]])
                fns = []
                for c in range(nsc):
                    for t in range(2):
                        fns.append(lambda e, c=c, t=t, so=so: e.matmul(
                            s_ps[:, t * 512:(t + 1) * 512], wp[so][:, c * 128:(c + 1) * 128],
                            big2[:, c * TOK + t * 512: c * TOK + (t + 1) * 512],
                            start=(c == 0), stop=(c == nsc - 1)))
                P.emit("tensor", fns, reads=[B_wp[so], B_const] + B_big[:nsc],
                       writes=[B_sps], incs=[("te", 1)])
                gpre = (not is0) and oc < 8
                if not gpre:
                    fns = []
                    for c in range(NCH):
                        for t in range(2):
                            fns.append(lambda e, c=c, t=t, sf=sf: e.matmul(
                                g_ps[:, t * 512:(t + 1) * 512], wp[sf][:, c * 128:(c + 1) * 128],
                                xn[:, c * TOK + t * 512: c * TOK + (t + 1) * 512],
                                start=(c == 0), stop=False))
                    for t in range(2):
                        fns.append(lambda e, t=t, oc=oc: e.matmul(
                            g_ps[:, t * 512:(t + 1) * 512], bml[:, oc * 128:(oc + 1) * 128],
                            hw[:, t * 512:(t + 1) * 512], start=False, stop=True))
                    P.emit("tensor", fns, reads=[B_wp[sf], B_bml, B_hw, B_const] + B_xn,
                           writes=[B_gps], incs=[("te", 1)])
                    P.emit("scalar", lambda e, oc=oc: e.activation(out=gsb[:], in_=g_ps[:],
                           func=AF.Silu, bias=bias_sb[:, oc:oc + 1], scale=1.0),
                           reads=[B_gps, B_const], writes=[B_gsb], incs=[("sc", 1)])
                else:
                    P.emit("scalar", lambda e, oc=oc: e.activation(out=gsb[:],
                           in_=big2[:, (24 + oc) * TOK:(25 + oc) * TOK],
                           func=AF.Silu, bias=bias_sb[:, oc:oc + 1], scale=1.0),
                           reads=[B_big[24 + oc], B_const], writes=[B_gsb], incs=[("sc", 1)])
                P.emit("vector", lambda e: e.tensor_tensor(out=tsb[:], in0=s_ps[:], in1=gsb[:],
                       op=ALU.mult), reads=[B_sps, B_gsb], writes=[B_tsb], incs=[("ve", 1)])
                P.emit("gpsimd", lambda e, po=po: e.tensor_tensor(out=xob[po][:], in0=xob[po][:],
                       in1=tsb[:], op=ALU.add),
                       reads=[B_xob[po], B_tsb], writes=[B_xob[po]], incs=[("gp", 1)])
                if is0:
                    dma("sync", "ds", x1d_t[oc], xob[po][:], reads=[B_xob[po]], writes=[B_x1d[oc]])
                    P.emit("vector", lambda e, po=po: e.tensor_copy(out=xb1[:], in_=xob[po][:]),
                           reads=[B_xob[po]], writes=[B_xb1], incs=[("ve", 1)])
                    dma("sync", "ds", x1b_t[oc], xb1[:], reads=[B_xb1], writes=[B_x1b[oc]])
                    P.emit("vector", lambda e: e.tensor_tensor(out=sq1[:], in0=xb1[:], in1=xb1[:],
                           op=ALU.mult), reads=[B_xb1], writes=[B_sq1], incs=[("ve", 1)])
                    for (src, Bs, pst, Bp) in ((xb1, B_xb1, sx1, B_sx1), (sq1, B_sq1, sx21, B_sx21)):
                        fns = [lambda e, src=src, pst=pst, t=t, oc=oc: e.matmul(
                            pst[0:1, t * 512:(t + 1) * 512], ones1b[:], src[:, t * 512:(t + 1) * 512],
                            start=(oc == 0), stop=(oc == NCH - 1)) for t in range(2)]
                        P.emit("tensor", fns, reads=[Bs, B_const], writes=[Bp], incs=[("te", 1)])
                else:
                    dma("sync", "ds", y_t[oc], xob[po][:], reads=[B_xob[po]], writes=[B_y])
            if not is0:
                P.join_all()
                pse.close()
                ste.close()
            return ret

        (ste0, pse0, sx1, sx21, B_sx1, B_sx21, sps0, B_sps0,
         rstdb1, negmrb1, xscr6, x6, B_x6) = emit_epilogue(0)

        B_stat1, B_rstdb1, B_negmrb1 = Buf(), Buf(), Buf()
        B_xscr6 = Buf()
        emit_stats_finalize(B_stat1, sx1, sx21, B_sx1, B_sx21,
                            rstdb1, negmrb1, B_rstdb1, B_negmrb1, 1024,
                            sps0[:, 0:512], B_sps0)
        P.join("vector")
        P.join("tensor")
        P.join("scalar")
        for c in range(NCH):
            p = c % 4
            dma("sync", "ds", x6[p][:], x1b_t[c], reads=[B_x1b[c]], writes=[B_x6[p]])
            P.emit("vector", lambda e, p=p: e.tensor_tensor(out=xscr6[:], in0=x6[p][:],
                   in1=rstdb1[:], op=ALU.mult),
                   reads=[B_x6[p], B_rstdb1], writes=[B_xscr6], incs=[("ve", 1)])
            P.emit("vector", lambda e, c=c: e.tensor_tensor(out=xn[:, c * TOK:(c + 1) * TOK],
                   in0=xscr6[:], in1=negmrb1[:], op=ALU.add),
                   reads=[B_xscr6, B_negmrb1], writes=[B_xn[c]], incs=[("ve", 1)])
        P.join_all()
        pse0.close()
        ste0.close()
        if KDEBUG:
            for c in range(NCH):
                dma("sync", "ds", dbg["dxn1"].rearrange("(c p) t -> c p t", p=128)[c],
                    xn[:, c * TOK:(c + 1) * TOK], reads=[B_xn[c]])

        st7 = ExitStack()
        kevac = st7.enter_context(nc.sbuf_tensor("kevac", [128, 1024], BF16))
        B_kevac = Buf()
        ps7 = ExitStack()
        p7 = ps7.enter_context(nc.psum_tensor("p7", [128, 1024], F32))
        B_p7 = Buf()
        kvs_t = kvs.rearrange("(c p) t -> c p t", p=128)
        QS = 8 * TOK

        for (wname, base) in (("w1k", 0), ("w1v", 8)):
            for oc in range(8):
                slot = oc % 2
                load_panel(wname, oc * 128, 128, slot)
                fns = []
                for c in range(NCH):
                    for t in range(2):
                        fns.append(lambda e, c=c, t=t, slot=slot: e.matmul(
                            p7[:, t * 512:(t + 1) * 512], wp[slot][:, c * 128:(c + 1) * 128],
                            xn[:, c * TOK + t * 512: c * TOK + (t + 1) * 512],
                            start=(c == 0), stop=(c == NCH - 1)))
                P.emit("tensor", fns, reads=[B_wp[slot], B_const] + B_xn,
                       writes=[B_p7], incs=[("te", 1)])
                P.emit("scalar", lambda e: e.copy(out=kevac[:], in_=p7[:]),
                       reads=[B_p7], writes=[B_kevac], incs=[("sc", 1)])
                dma("sync", "ds", kvs_t[base + oc], kevac[:], reads=[B_kevac], writes=[B_kvs])
        _kvw = [(f"ds{i}", P.cnt[f"ds{i}"]) for i in range(NLANES["ds"])]
        P.emit("gpsimd", lambda e: e.collective_compute(
            "AllGather", ALU.bypass, replica_groups=[[0, 1], [2, 3], [4, 5], [6, 7]],
            ins=[kvs], outs=[kva]), reads=[B_kvs], writes=[B_kva], incs=[("cc", 1)],
            waits=_kvw)
        P.join_all()
        ps7.close()

        emit_gate_lora(1)

        ps7 = ExitStack()
        p7 = ps7.enter_context(nc.psum_tensor("p7b", [128, 1024], F32))
        B_p7 = Buf()

        def load_wqs_panel(oc, slot):
            src = wqall.rearrange("r i p o -> p (r i) o")[:, :, oc * 128:(oc + 1) * 128]
            dstv = wp[slot][:, 0:NCH * 128].rearrange("p (c o) -> p c o", c=NCH)
            dma("sync", "ds", dstv, src, reads=[B_wqall], writes=[B_wp[slot]])

        load_wqs_panel(0, 0)
        for oc in range(8):
            slot = oc % 2
            if oc + 1 < 8:
                load_wqs_panel(oc + 1, (oc + 1) % 2)
            fns = []
            for c in range(NCH):
                for t in range(2):
                    fns.append(lambda e, c=c, t=t, slot=slot: e.matmul(
                        p7[:, t * 512:(t + 1) * 512], wp[slot][:, c * 128:(c + 1) * 128],
                        xn[:, c * TOK + t * 512: c * TOK + (t + 1) * 512],
                        start=(c == 0), stop=(c == NCH - 1)))
            P.emit("tensor", fns, reads=[B_wp[slot], B_const] + B_xn,
                   writes=[B_p7], incs=[("te", 1)])
            P.emit("scalar", lambda e, oc=oc: e.copy(
                   out=big2[:, QS + oc * TOK: QS + (oc + 1) * TOK], in_=p7[:]),
                   reads=[B_p7], writes=[B_big[8 + oc]], incs=[("sc", 1)])
        P.join_all()
        ps7.close()
        st7.close()
        if KDEBUG:
            for c in range(8):
                dma("sync", "ds", dbg["dqs"].rearrange("(c p) t -> c p t", p=128)[c],
                    big2[:, QS + c * TOK: QS + (c + 1) * TOK], reads=[B_big[8 + c]])

        if KDEBUG:
            dma("sync", "ds", dbg["dhw1"], hw[:], reads=[B_hw])

        st9 = ExitStack()
        Kh2 = [st9.enter_context(nc.sbuf_tensor(f"Kh{i}", [128, 2048], BF16)) for i in range(2)]
        Vh2 = [st9.enter_context(nc.sbuf_tensor(f"Vh{i}", [128, 2048], BF16)) for i in range(2)]
        Vtm2 = [st9.enter_context(nc.sbuf_tensor(f"Vtm{i}", [128, 2048], BF16)) for i in range(2)]
        rdenS = st9.enter_context(nc.sbuf_tensor("rdenS", [1, 512], F32))
        bcS = st9.enter_context(nc.sbuf_tensor("bcS", [128, 512], F32))
        bm9 = st9.enter_context(nc.sbuf_tensor("bm9", [128, 4096], BF16))
        B_bm9 = Buf()
        B_Kh2 = [Buf(), Buf()]
        B_Vh2 = [Buf(), Buf()]
        B_Vtm2 = [Buf(), Buf()]
        B_rden, B_bcS = Buf(), Buf()
        PTB = 16 * TOK
        B_pt = [Buf() for _ in range(8)]
        ps9 = ExitStack()
        sc9 = [ps9.enter_context(nc.psum_tensor(f"sc9_{i}", [128, 512], F32)) for i in range(3)]
        av9 = ps9.enter_context(nc.psum_tensor("av9", [128, 512], F32))
        den9 = ps9.enter_context(nc.psum_tensor("den9", [1, 512], F32))
        bc9 = ps9.enter_context(nc.psum_tensor("bc9", [128, 512], F32))
        vt9 = ps9.enter_context(nc.psum_tensor("vt9", [128, 1024], BF16))
        g9 = ps9.enter_context(nc.psum_tensor("g9", [128, 512], F32))
        B_sc9 = [Buf() for _ in range(3)]
        B_av9, B_den9, B_bc9, B_vt9, B_g9 = Buf(), Buf(), Buf(), Buf(), Buf()

        def emit_g_filler():
            th = []
            th.append(lambda: dma("gpsimd", "dg", bm9[:], ap["lb1"], writes=[B_bm9]))
            th.append(lambda: load_panel("bw1", 0, 128, 0))
            for oc in range(8):
                slot = oc % 2
                if oc + 1 < 8:
                    th.append(lambda oc=oc: load_panel("bw1", (oc + 1) * 128, 128, (oc + 1) % 2))
                for t in range(2):
                    for g in range(8):
                        fns = []
                        for c in range(g * 4, g * 4 + 4):
                            fns.append(lambda e, c=c, t=t, slot=slot: e.matmul(
                                g9[:], wp[slot][:, c * 128:(c + 1) * 128],
                                xn[:, c * TOK + t * 512: c * TOK + (t + 1) * 512],
                                start=(c == 0), stop=False))
                        th.append(lambda fns=fns, slot=slot: P.emit("tensor", fns,
                            reads=[B_wp[slot], B_const] + B_xn, writes=[B_g9], incs=[("te", 1)]))
                    th.append(lambda t=t, oc=oc: P.emit("tensor",
                        lambda e, t=t, oc=oc: e.matmul(g9[:], bm9[:, oc * 128:(oc + 1) * 128],
                            hw[:, t * 512:(t + 1) * 512], start=False, stop=True),
                        reads=[B_bm9, B_hw, B_const], writes=[B_g9], incs=[("te", 1)]))
                    th.append(lambda t=t, oc=oc: P.emit("scalar", lambda e, t=t, oc=oc: e.copy(
                        out=big2[:, (24 + oc) * TOK + t * 512:(24 + oc) * TOK + (t + 1) * 512],
                        in_=g9[:]), reads=[B_g9], writes=[B_big[24 + oc]], incs=[("sc", 1)]))
            return th

        gfill = emit_g_filler()
        gfi = 0

        def gf(n=1):
            nonlocal gfi
            for _ in range(n):
                if gfi < len(gfill):
                    gfill[gfi]()
                    gfi += 1

        def emit_kv_load(h):
            hp = h % 2
            for r in range(2):
                dma("sync", "ds", Kh2[hp][:, r * 1024:(r + 1) * 1024],
                    kva[r, h * 128:(h + 1) * 128, :], reads=[B_kva], writes=[B_Kh2[hp]],
                    waits=[("cc", P.cnt["cc"])])
                dma("sync", "ds", Vh2[hp][:, r * 1024:(r + 1) * 1024],
                    kva[r, 1024 + h * 128: 1024 + (h + 1) * 128, :],
                    reads=[B_kva], writes=[B_Vh2[hp]], waits=[("cc", P.cnt["cc"])])

        pendfin = [None]

        def emit_fin():
            if pendfin[0] is None:
                return
            h, qgv = pendfin[0]
            pendfin[0] = None
            P.emit("vector", lambda e: e.reciprocal(out=rdenS[:], in_=den9[0:1, :]),
                   reads=[B_den9], writes=[B_rden], incs=[("ve", 1)])
            P.emit("tensor", lambda e: e.matmul(bc9[:], onesbf[:], rdenS[:],
                   start=True, stop=True), reads=[B_rden, B_const], writes=[B_bc9],
                   incs=[("te", 1)])
            P.emit("scalar", lambda e: e.copy(out=bcS[:], in_=bc9[:]),
                   reads=[B_bc9], writes=[B_bcS], incs=[("sc", 1)])
            P.emit("vector", lambda e, h=h, qgv=qgv: e.tensor_tensor(
                   out=big2[:, h * TOK + qgv * 512: h * TOK + (qgv + 1) * 512],
                   in0=av9[:], in1=bcS[:], op=ALU.mult),
                   reads=[B_av9, B_bcS], writes=[B_big[h]], incs=[("ve", 1)])

        emit_kv_load(0)
        for h in range(NKV):
            hp = h % 2
            Kh, Vh, Vtm = Kh2[hp], Vh2[hp], Vtm2[hp]
            B_Kh, B_Vh, B_Vtm = B_Kh2[hp], B_Vh2[hp], B_Vtm2[hp]
            if h + 1 < NKV:
                emit_kv_load(h + 1)
            for g in range(2):
                fns = [lambda e, g=g, t=t, Vh=Vh: e.transpose(vt9[:, t * 128:(t + 1) * 128],
                       Vh[:, (g * 8 + t) * 128:(g * 8 + t + 1) * 128], identb[:]) for t in range(8)]
                P.emit("tensor", fns, reads=[B_Vh, B_const], writes=[B_vt9], incs=[("te", 1)])
                P.emit("vector", lambda e, g=g, Vtm=Vtm: e.tensor_copy(
                       out=Vtm[:, g * 1024:(g + 1) * 1024],
                       in_=vt9[:]), reads=[B_vt9], writes=[B_Vtm], incs=[("ve", 1)])
                gf(1)
            for qgv in range(2):
                Qs = big2[:, QS + h * TOK + qgv * 512: QS + h * TOK + (qgv + 1) * 512]

                def s_mm(kc, Kh=Kh, B_Kh=B_Kh, Qs=Qs, h=h):
                    P.emit("tensor", lambda e, kc=kc, Kh=Kh, Qs=Qs: e.matmul(
                        sc9[kc % 3][:], Kh[:, kc * 128:(kc + 1) * 128], Qs,
                        start=True, stop=True),
                        reads=[B_Kh, B_big[8 + h]], writes=[B_sc9[kc % 3]], incs=[("te", 1)])

                def e_act(kc):
                    P.emit("scalar", lambda e, kc=kc: e.activation(
                        out=big2[:, PTB + (kc % 8) * 512: PTB + (kc % 8 + 1) * 512],
                        in_=sc9[kc % 3][:], func=AF.Exp, scale=SCALE),
                        reads=[B_sc9[kc % 3]], writes=[B_pt[kc % 8]], incs=[("sc", 1)])

                for kc in range(3):
                    s_mm(kc)
                    e_act(kc)
                emit_fin()
                for kc in range(16):
                    pt = big2[:, PTB + (kc % 8) * 512: PTB + (kc % 8 + 1) * 512]
                    P.emit("tensor", lambda e, pt=pt, kc=kc, Vtm=Vtm: e.matmul(
                        av9[:], Vtm[:, kc * 128:(kc + 1) * 128], pt,
                        start=(kc == 0), stop=(kc == 15)),
                        reads=[B_pt[kc % 8], B_Vtm], writes=[B_av9], incs=[("te", 1)])
                    P.emit("tensor", lambda e, pt=pt, kc=kc: e.matmul(
                        den9[:], ones1b[:], pt, start=(kc == 0), stop=(kc == 15)),
                        reads=[B_pt[kc % 8], B_const], writes=[B_den9], incs=[("te", 1)])
                    if kc + 3 < 16:
                        s_mm(kc + 3)
                        e_act(kc + 3)
                    gf(1)
                pendfin[0] = (h, qgv)
        emit_fin()
        gf(len(gfill))

        P.join_all()
        ps9.close()
        st9.close()
        if KDEBUG:
            for c in range(8):
                dma("sync", "ds", dbg["dval"].rearrange("(c p) t -> c p t", p=128)[c],
                    big2[:, c * TOK:(c + 1) * TOK], reads=[B_big[c]])
            for c in range(NCH):
                dma("sync", "ds", dbg["dx1"].rearrange("(c p) t -> c p t", p=128)[c],
                    x1d_t[c], reads=[B_x1d[c]])

        emit_epilogue(1)

        P.join("sync")

        with nc.Block() as block:
            prog.replay(block)
    return nc


def kernel(**inputs):
    f32 = np.float32
    x = np.asarray(inputs["L0_attn_x"], f32)
    W = {}
    W["w0q"] = np.ascontiguousarray(np.asarray(inputs["L0_attn_Wq"], f32).T)
    W["w0k"] = np.ascontiguousarray(np.asarray(inputs["L0_attn_Wk"], f32).T)
    W["w0v"] = np.ascontiguousarray(np.asarray(inputs["L0_attn_Wv"], f32).T)
    W["w0o"] = np.ascontiguousarray(np.asarray(inputs["L0_attn_Wo"], f32).T)
    W["w0f"] = np.ascontiguousarray(np.asarray(inputs["L0_ffn_W"], f32).T)
    W["w0fq"] = np.ascontiguousarray(np.asarray(inputs["L0_ffn_Wq"], f32).T)
    W["w0fk"] = np.ascontiguousarray(np.asarray(inputs["L0_ffn_Wk"], f32).T)
    W["a0"] = np.ascontiguousarray(np.asarray(inputs["L0_ffn_A"], f32).reshape(128, D).T)
    W["bm0"] = np.ascontiguousarray(np.asarray(inputs["L0_ffn_Bm"], f32).transpose(0, 2, 1).reshape(128, D))
    w1qf = np.asarray(inputs["L1_attn_wq"], f32)
    W["w1k"] = np.ascontiguousarray(np.asarray(inputs["L1_attn_wk"], f32))
    W["w1v"] = np.ascontiguousarray(np.asarray(inputs["L1_attn_wv"], f32))
    W["w1o"] = np.ascontiguousarray(np.asarray(inputs["L1_attn_wo"], f32))
    W["gw1"] = np.ascontiguousarray(np.asarray(inputs["L1_ffn_gate_w"], f32).T)
    W["bw1"] = np.ascontiguousarray(np.asarray(inputs["L1_ffn_base_w"], f32).T)
    W["la1"] = np.ascontiguousarray(np.asarray(inputs["L1_ffn_lora_A"], f32).reshape(128, D).T)
    W["lb1"] = np.ascontiguousarray(np.asarray(inputs["L1_ffn_lora_B"], f32).transpose(0, 2, 1).reshape(128, D))
    b0 = np.ascontiguousarray(np.asarray(inputs["L0_ffn_b"], f32).reshape(32, 128).T)
    b1 = np.ascontiguousarray(np.asarray(inputs["L1_ffn_base_b"], f32).reshape(32, 128).T)

    consts = [_consts_for(0), _consts_for(1)]
    in_maps = []
    for core in range(8):
        b, half = core // 2, core % 2
        off = 1024 * half
        lo, hi = off - 256, off + 1280
        s0, s1 = max(lo, 0), min(hi, 2048)
        xl = np.zeros((LOC, D), f32)
        xl[s0 - lo:s1 - lo] = x[b, s0:s1]
        m = {"xloc": np.ascontiguousarray(xl.T), "b0": b0, "b1": b1,
             "w1qsl": np.ascontiguousarray(
                 w1qf[core * 512:(core + 1) * 512].reshape(4, 128, 4096))}
        m.update(W)
        m.update(consts[half])
        in_maps.append(m)

    nc = build_nc()
    res = run_bass_kernel_spmd(nc, in_maps, core_ids=list(range(8)))
    out = np.zeros((4, 2048, 4096), np.float32)
    for core in range(8):
        b, half = core // 2, core % 2
        out[b, 1024 * half:1024 * (half + 1)] = res.results[core]["y"].T
    return out


# version 15
# speedup vs baseline: 1.0262x; 1.0161x over previous
import numpy as np
import ml_dtypes
import concourse.bass as bass
import concourse.mybir as mybir
from concourse.bass_utils import run_bass_kernel_spmd
from contextlib import ExitStack

F32 = mybir.dt.float32
BF16 = mybir.dt.bfloat16
AF = mybir.ActivationFunctionType
ALU = mybir.AluOpType
AXX = mybir.AxisListType.X

KDEBUG = False

D = 4096
NCH = 32
TOK = 1024
LOC = 1536
H0 = 32
NKV = 8
SCALE = 1.0 / float(np.sqrt(128.0))
NEG = -1e9


class Buf:
    __slots__ = ("w", "r")

    def __init__(self):
        self.w = None
        self.r = {}

    def rd_waits(self):
        return [self.w] if self.w is not None else []

    def wr_waits(self):
        ws = dict(self.r)
        if self.w is not None:
            s, v = self.w
            ws[s] = max(ws.get(s, -1), v)
        return list(ws.items())


class Prog:
    ENGS = ("tensor", "vector", "scalar", "gpsimd", "sync")

    def __init__(self, nc):
        self.nc = nc
        self.ev = {e: [] for e in self.ENGS}
        self.sems = {}
        self.cnt = {}
        self.lastwait = {e: {} for e in self.ENGS}

    def sem(self, name):
        self.sems[name] = self.nc.alloc_semaphore(name)
        self.cnt[name] = 0

    def emit(self, eng, fns, reads=(), writes=(), waits=(), incs=()):
        if callable(fns):
            fns = [fns]
        wset = {}
        for s, v in waits:
            wset[s] = max(wset.get(s, -1), v)
        for b in reads:
            for s, v in b.rd_waits():
                wset[s] = max(wset.get(s, -1), v)
        for b in writes:
            for s, v in b.wr_waits():
                wset[s] = max(wset.get(s, -1), v)
        fw = []
        for s, v in wset.items():
            if self.lastwait[eng].get(s, -1) < v:
                fw.append((self.sems[s], v))
                self.lastwait[eng][s] = v
        marks = []
        for s, n in incs:
            self.cnt[s] += n
            marks.append((s, self.cnt[s]))
        mark = marks[0] if marks else None
        self.ev[eng].append((fns, fw, [(self.sems[s], n) for s, n in incs]))
        if mark is not None:
            for b in reads:
                b.r[mark[0]] = max(b.r.get(mark[0], -1), mark[1])
            for b in writes:
                b.w = mark
                b.r = {}
        return mark

    def join(self, eng, sems=None):
        if sems is None:
            sems = list(self.sems)
        fw = []
        for s in sems:
            v = self.cnt[s]
            if v > 0 and self.lastwait[eng].get(s, -1) < v:
                fw.append((self.sems[s], v))
                self.lastwait[eng][s] = v
        if fw:
            self.ev[eng].append(([], fw, []))

    def join_all(self):
        for e in self.ENGS:
            self.join(e)

    def replay(self, block):
        def mk(engname):
            def body(e):
                for fns, fw, incs in self.ev[engname]:
                    for h, v in fw:
                        e.wait_ge(h, v)
                    ins = None
                    for f in fns:
                        ins = f(e)
                    for h, n in incs:
                        assert ins is not None
                        ins = ins.then_inc(h, n)
            return body
        block.tensor(mk("tensor"))
        block.vector(mk("vector"))
        block.scalar(mk("scalar"))
        block.gpsimd(mk("gpsimd"))
        block.sync(mk("sync"))


def _rope_cache(S=2048, Dh=128, base=10000.0):
    inv = 1.0 / base ** (np.arange(0, Dh, 2, dtype=np.float32) / Dh)
    t = np.arange(S, dtype=np.float32)
    f = np.outer(t, inv)
    cos = np.zeros((S, Dh), np.float32)
    sin = np.zeros((S, Dh), np.float32)
    cos[:, 0::2] = np.cos(f); cos[:, 1::2] = np.cos(f)
    sin[:, 0::2] = np.sin(f); sin[:, 1::2] = np.sin(f)
    return cos, sin


def _consts_for(half):
    off = 1024 * half
    cosF, sinF = _rope_cache()
    qpos = np.arange(off, off + TOK)
    kpos = np.clip(np.arange(off - 256, off + 1280), 0, 2047)
    c = {}
    c["cosq"] = np.ascontiguousarray((cosF[qpos] * SCALE).T).astype(ml_dtypes.bfloat16)
    c["sinq"] = np.ascontiguousarray((sinF[qpos] * SCALE).T).astype(ml_dtypes.bfloat16)
    c["cosk"] = np.ascontiguousarray(cosF[kpos].T).astype(ml_dtypes.bfloat16)
    c["sink"] = np.ascontiguousarray(sinF[kpos].T).astype(ml_dtypes.bfloat16)
    P = np.zeros((128, 128), np.float32)
    for d in range(64):
        P[d, d + 64] = -1.0
        P[d + 64, d] = 1.0
    c["permT"] = np.ascontiguousarray(P.T).astype(ml_dtypes.bfloat16)
    c["identb"] = np.eye(128, dtype=ml_dtypes.bfloat16)
    c["identf"] = np.eye(8, dtype=np.float32)
    mask = np.zeros((128, 12), np.float32)
    if half == 0:
        mask[:, 0] = NEG; mask[:, 1] = NEG
    else:
        mask[:, 10] = NEG; mask[:, 11] = NEG
    c["maskb"] = mask
    ramp = np.arange(256, dtype=np.float32) / 255.0
    coef = np.zeros((8, 512), np.float32)
    for w in range(5):
        if w >= 1:
            coef[w, 0:256] = ramp
        if w <= 3:
            coef[w, 256:512] = 1.0 - ramp
    if half == 0:
        coef[0, 256:512] = 0.0
        coef[1, 0:256] = 1.0
    coef2 = np.zeros((8, 1024), np.float32)
    coef2[0, 0:512] = coef[1]; coef2[1, 0:512] = coef[0]; coef2[2, 0:512] = coef[2]
    coef2[0, 512:1024] = coef[3]; coef2[1, 512:1024] = coef[4]
    c["coef"] = coef2
    ROW_OF = {1: 0, 0: 1, 2: 2, 3: 0, 4: 1}
    basis = np.zeros((128, 40), ml_dtypes.bfloat16)
    for w in range(5):
        basis[:, w * 8 + ROW_OF[w]] = 1.0
    c["basis"] = basis
    c["ones1b"] = np.ones((128, 1), ml_dtypes.bfloat16)
    c["onesbf"] = np.ones((1, 128), np.float32)
    rs = np.zeros((8, 6 * 128), ml_dtypes.bfloat16)
    for w in range(6):
        rs[w, w * 128:(w + 1) * 128] = 1.0
    c["rowselb"] = rs
    blk = np.zeros((128, 16), ml_dtypes.bfloat16)
    for oc in range(2):
        for p in range(128):
            blk[p, oc * 8 + oc * 4 + p // 32] = 1.0
    c["blocksel"] = blk
    ex1 = np.zeros((8, 128), ml_dtypes.bfloat16)
    ex2 = np.zeros((8, 128), ml_dtypes.bfloat16)
    for m in range(128):
        ex1[m // 16, m] = 1.0
        ex2[m // 16, m] = 2.0
    c["expand1"] = ex1
    c["expand2"] = ex2
    c["ones128"] = np.ones((128, 128), ml_dtypes.bfloat16)
    return c


_CONST_SHAPES = {
    "cosq": ([128, 1024], BF16), "sinq": ([128, 1024], BF16),
    "cosk": ([128, 1536], BF16), "sink": ([128, 1536], BF16),
    "permT": ([128, 128], BF16), "identb": ([128, 128], BF16),
    "identf": ([8, 8], F32), "maskb": ([128, 12], F32),
    "coef": ([8, 1024], F32), "basis": ([128, 40], BF16),
    "ones1b": ([128, 1], BF16), "rowselb": ([8, 768], BF16),
    "onesbf": ([1, 128], F32), "blocksel": ([128, 16], BF16),
    "expand1": ([8, 128], BF16), "expand2": ([8, 128], BF16),
    "ones128": ([128, 128], BF16),
}

_WEIGHT_SHAPES = {
    "w0q": [32, 128, D], "w0k": [1, 128, D], "w0v": [1, 128, D],
    "w0o": [32, 128, D], "w0f": [32, 128, D],
    "w0fq": [2, 128, D], "w0fk": [2, 128, D],
    "a0": [1, 128, D], "bm0": [128, D],
    "w1k": [8, 128, D], "w1v": [8, 128, D], "w1o": [32, 128, 1024],
    "gw1": [1, 128, 256], "bw1": [32, 128, D], "la1": [1, 128, D],
    "lb1": [128, D],
}


def build_nc():
    nc = bass.Bass("TRN2", target_bir_lowering=False)
    ap = {}
    ap["xloc"] = nc.dram_tensor("xloc", [D, LOC], F32, kind="ExternalInput").ap()
    for n, shp in _WEIGHT_SHAPES.items():
        ap[n] = nc.dram_tensor(n, shp, F32, kind="ExternalInput").ap()
    ap["b0"] = nc.dram_tensor("b0", [128, 32], F32, kind="ExternalInput").ap()
    ap["b1"] = nc.dram_tensor("b1", [128, 32], F32, kind="ExternalInput").ap()
    for n, (shp, dt) in _CONST_SHAPES.items():
        ap[n] = nc.dram_tensor(n, shp, dt, kind="ExternalInput").ap()
    y = nc.dram_tensor("y", [D, TOK], F32, kind="ExternalOutput").ap()
    dbg = {}
    if KDEBUG:
        for n, shp, dt in [("dxn", [D, TOK], BF16), ("dao", [D, TOK], BF16),
                           ("dx1", [D, TOK], F32), ("dxn1", [D, TOK], BF16),
                           ("dqs", [1024, TOK], BF16), ("dval", [1024, TOK], BF16),
                           ("dhw", [128, TOK], BF16), ("dhw1", [128, TOK], BF16),
                           ("dkr", [128, LOC], BF16)]:
            dbg[n] = nc.dram_tensor(n, shp, dt, kind="ExternalOutput").ap()
    ap["w1qsl"] = nc.dram_tensor("w1qsl", [4, 128, D], F32, kind="ExternalInput").ap()
    x1d = nc.dram_tensor("x1d", [D, TOK], F32).ap()
    x1b = nc.dram_tensor("x1b", [D, TOK], BF16).ap()
    kvs = nc.dram_tensor("kvs", [2048, TOK], BF16).ap()
    kva = nc.dram_tensor("kva", [2, 2048, TOK], BF16).ap()
    wqpart = nc.dram_tensor("wqpart", [8, 128, 4, 128], BF16).ap()
    wqall = nc.dram_tensor("wqall", [8, 8, 128, 4, 128], BF16, addr_space="Shared").ap()

    st = ExitStack()
    with st:
        xn = st.enter_context(nc.sbuf_tensor("xn", [128, NCH * TOK], BF16))
        big2 = st.enter_context(nc.sbuf_tensor("big2", [128, NCH * TOK], BF16))
        wp = [st.enter_context(nc.sbuf_tensor(f"wp{i}", [128, 4096], BF16)) for i in range(4)]
        hw = st.enter_context(nc.sbuf_tensor("hw", [128, 1024], BF16))
        permT = st.enter_context(nc.sbuf_tensor("permT_s", [128, 128], BF16))
        identb = st.enter_context(nc.sbuf_tensor("identb_s", [128, 128], BF16))
        identf = st.enter_context(nc.sbuf_tensor("identf_s", [8, 8], F32))
        maskb = st.enter_context(nc.sbuf_tensor("maskb_s", [128, 12], F32))
        basis = st.enter_context(nc.sbuf_tensor("basis_s", [128, 40], BF16))
        ones1b = st.enter_context(nc.sbuf_tensor("ones1b_s", [128, 1], BF16))
        rowselb = st.enter_context(nc.sbuf_tensor("rowselb_s", [8, 768], BF16))
        onesbf = st.enter_context(nc.sbuf_tensor("onesbf_s", [1, 128], F32))
        blocksel = st.enter_context(nc.sbuf_tensor("blocksel_s", [128, 16], BF16))
        expand1 = st.enter_context(nc.sbuf_tensor("expand1_s", [8, 128], BF16))
        expand2 = st.enter_context(nc.sbuf_tensor("expand2_s", [8, 128], BF16))
        ones128 = st.enter_context(nc.sbuf_tensor("ones128_s", [128, 128], BF16))
        b0sb = st.enter_context(nc.sbuf_tensor("b0sb", [128, 32], F32))
        b1sb = st.enter_context(nc.sbuf_tensor("b1sb", [128, 32], F32))

        prog = Prog(nc)
        for s in ("te", "ve", "sc", "gp", "cc"):
            prog.sem(s)
        NLANES = {"ds": 8, "dg": 6}
        for q, nl in NLANES.items():
            for i in range(nl):
                prog.sem(f"{q}{i}")
        _rr = {"ds": 0, "dg": 0}
        P = prog

        B_xn = [Buf() for _ in range(NCH)]
        B_big = [Buf() for _ in range(NCH)]
        B_wp = [Buf() for _ in range(4)]
        B_hw = Buf()
        B_const = Buf()
        B_x1d = [Buf() for _ in range(NCH)]
        B_x1b = [Buf() for _ in range(NCH)]
        B_kvs = Buf()
        B_kva = Buf()
        B_y = Buf()

        def dma(eng, q, out, in_, reads=(), writes=(), waits=(), **kw):
            lane = f"{q}{_rr[q]}"
            _rr[q] = (_rr[q] + 1) % NLANES[q]
            w = list(waits) + [(lane, P.cnt[lane])]
            return P.emit(eng, lambda e: e.dma_start(out=out, in_=in_, **kw),
                          reads=reads, writes=writes, waits=w, incs=[(lane, 16)])

        for sb, n in [(permT, "permT"), (identb, "identb"), (identf, "identf"),
                      (maskb, "maskb"), (basis, "basis"), (ones1b, "ones1b"),
                      (rowselb, "rowselb"), (onesbf, "onesbf"), (blocksel, "blocksel"),
                      (expand1, "expand1"), (expand2, "expand2"), (ones128, "ones128"),
                      (b0sb, "b0"), (b1sb, "b1")]:
            dma("sync", "ds", sb[:], ap[n], writes=[B_const])

        def load_panel(wname, pan, ncol, slot, nch=NCH):
            return dma("gpsimd", "dg", wp[slot][:, 0:nch * ncol], ap[wname][pan],
                       writes=[B_wp[slot]])

        def emit_stats_finalize(B_stat, sx_ps, sx2_ps, B_sxps, B_sx2ps,
                                rstdb, negmrb, B_rstdb, B_negmrb, ncols, bc_ps, B_bcps):
            inv = 1.0 / float(D)
            mrow = negmrb[0:1, 0:ncols]
            scr = rstdb[0:1, 0:ncols]
            P.emit("vector", lambda e: e.tensor_scalar(out=mrow, in0=sx_ps[0:1, 0:ncols],
                   scalar1=inv, scalar2=None, op0=ALU.mult),
                   reads=[B_sxps], writes=[B_stat], incs=[("ve", 1)])
            P.emit("vector", lambda e: e.tensor_tensor(out=scr, in0=mrow, in1=mrow, op=ALU.mult),
                   reads=[B_stat], writes=[B_stat], incs=[("ve", 1)])
            P.emit("vector", lambda e: e.scalar_tensor_tensor(out=scr, in0=sx2_ps[0:1, 0:ncols],
                   scalar=inv, in1=scr, op0=ALU.mult, op1=ALU.subtract),
                   reads=[B_sx2ps, B_stat], writes=[B_stat], incs=[("ve", 1)])
            P.emit("vector", lambda e: e.tensor_scalar(out=scr, in0=scr,
                   scalar1=1e-5, scalar2=None, op0=ALU.add),
                   reads=[B_stat], writes=[B_stat], incs=[("ve", 1)])
            P.emit("scalar", lambda e: e.activation(out=scr, in_=scr, func=AF.Sqrt),
                   reads=[B_stat], writes=[B_stat], incs=[("sc", 1)])
            P.emit("vector", lambda e: e.reciprocal(out=scr, in_=scr),
                   reads=[B_stat], writes=[B_stat], incs=[("ve", 1)])
            P.emit("vector", lambda e: e.tensor_tensor(out=mrow, in0=mrow, in1=scr, op=ALU.mult),
                   reads=[B_stat], writes=[B_stat], incs=[("ve", 1)])
            P.emit("vector", lambda e: e.tensor_scalar(out=mrow, in0=mrow,
                   scalar1=-1.0, scalar2=None, op0=ALU.mult),
                   reads=[B_stat], writes=[B_stat], incs=[("ve", 1)])
            for (srcr, dst, Bd) in ((mrow, negmrb, B_negmrb), (scr, rstdb, B_rstdb)):
                for s in range(0, ncols, 512):
                    n = min(512, ncols - s)
                    P.emit("tensor", lambda e, srcr=srcr, s=s, n=n: e.matmul(
                        bc_ps[:, 0:n], onesbf[:], srcr[0:1, s:s + n],
                        start=True, stop=True),
                        reads=[B_stat, B_const], writes=[B_bcps], incs=[("te", 1)])
                    P.emit("scalar", lambda e, dst=dst, s=s, n=n: e.copy(
                        out=dst[:, s:s + n], in_=bc_ps[:, 0:n]),
                        reads=[B_bcps, B_stat], writes=[Bd], incs=[("sc", 1)])

        xb = big2[:, 0:1536]
        sqv = big2[:, 1536:3072]
        coskb = big2[:, 3072:4608]
        sinkb = big2[:, 4608:6144]
        kraw = big2[:, 6144:7680]
        xnh = big2[:, 7680:8192]
        B_xb, B_sq, B_ck, B_sk, B_kraw, B_xnh = Buf(), Buf(), Buf(), Buf(), Buf(), Buf()

        st23 = ExitStack()
        Kr = st23.enter_context(nc.sbuf_tensor("Kr", [128, LOC], BF16))
        V0 = st23.enter_context(nc.sbuf_tensor("V0", [128, LOC], BF16))
        B_Kr, B_V0 = Buf(), Buf()

        st12 = ExitStack()
        xf = [st12.enter_context(nc.sbuf_tensor(f"xf{i}", [128, LOC], F32)) for i in range(2)]
        rstdb = st12.enter_context(nc.sbuf_tensor("rstdb", [128, LOC], F32))
        negmrb = st12.enter_context(nc.sbuf_tensor("negmrb", [128, LOC], F32))
        xscr = st12.enter_context(nc.sbuf_tensor("xscr", [128, LOC], F32))
        B_xf = [Buf(), Buf()]
        B_stat, B_rstdb, B_negmrb, B_xscr = Buf(), Buf(), Buf(), Buf()

        ps12 = ExitStack()
        sx_ps = ps12.enter_context(nc.psum_tensor("sx_ps", [1, LOC], F32))
        sx2_ps = ps12.enter_context(nc.psum_tensor("sx2_ps", [1, LOC], F32))
        bc_ps = ps12.enter_context(nc.psum_tensor("bc_ps", [128, 512], F32))
        B_sxps, B_sx2ps, B_bcps = Buf(), Buf(), Buf()

        xloc_t = ap["xloc"].rearrange("(c p) t -> c p t", p=128)

        for c in range(NCH):
            p = c % 2
            dma("sync", "ds", xf[p][:], xloc_t[c], writes=[B_xf[p]])
            P.emit("vector", lambda e, p=p: e.tensor_copy(out=xb, in_=xf[p][:]),
                   reads=[B_xf[p]], writes=[B_xb], incs=[("ve", 1)])
            P.emit("vector", lambda e: e.tensor_tensor(out=sqv, in0=xb, in1=xb, op=ALU.mult),
                   reads=[B_xb], writes=[B_sq], incs=[("ve", 1)])
            for (src, Bs, pst, Bp) in ((xb, B_xb, sx_ps, B_sxps), (sqv, B_sq, sx2_ps, B_sx2ps)):
                fns = [lambda e, src=src, pst=pst, s=s, c=c: e.matmul(
                    pst[0:1, s * 512:(s + 1) * 512], ones1b[:], src[:, s * 512:(s + 1) * 512],
                    start=(c == 0), stop=(c == NCH - 1)) for s in range(3)]
                P.emit("tensor", fns, reads=[Bs, B_const], writes=[Bp], incs=[("te", 1)])

        emit_stats_finalize(B_stat, sx_ps, sx2_ps, B_sxps, B_sx2ps,
                            rstdb, negmrb, B_rstdb, B_negmrb, LOC, bc_ps, B_bcps)
        P.join_all()
        ps12.close()

        psKV = ExitStack()
        k_ps = psKV.enter_context(nc.psum_tensor("k_ps", [128, LOC], F32))
        v_ps = psKV.enter_context(nc.psum_tensor("v_ps", [128, LOC], F32))
        pkp = psKV.enter_context(nc.psum_tensor("pkp", [128, 512], F32))
        vt_ps = psKV.enter_context(nc.psum_tensor("vt_ps", [128, 768], BF16))
        B_kps, B_vps, B_pkp, B_vtps = Buf(), Buf(), Buf(), Buf()

        load_panel("w0k", 0, 128, 2)
        load_panel("w0v", 0, 128, 3)

        for c in range(NCH):
            p = c % 2
            dma("sync", "ds", xf[p][:], xloc_t[c], writes=[B_xf[p]])
            P.emit("vector", lambda e, p=p: e.tensor_tensor(out=xscr[:], in0=xf[p][:],
                   in1=rstdb[:], op=ALU.mult),
                   reads=[B_xf[p], B_rstdb], writes=[B_xscr], incs=[("ve", 1)])
            P.emit("vector", lambda e, c=c: e.tensor_tensor(out=xn[:, c * TOK:(c + 1) * TOK],
                   in0=xscr[:, 256:1280], in1=negmrb[:, 256:1280], op=ALU.add),
                   reads=[B_xscr, B_negmrb], writes=[B_xn[c]], incs=[("ve", 1)])
            P.emit("vector", lambda e: e.tensor_tensor(out=xnh[:, 0:256],
                   in0=xscr[:, 0:256], in1=negmrb[:, 0:256], op=ALU.add),
                   reads=[B_xscr, B_negmrb], writes=[B_xnh], incs=[("ve", 1)])
            P.emit("vector", lambda e: e.tensor_tensor(out=xnh[:, 256:512],
                   in0=xscr[:, 1280:1536], in1=negmrb[:, 1280:1536], op=ALU.add),
                   reads=[B_xscr, B_negmrb], writes=[B_xnh], incs=[("ve", 1)])
            segs = [(0, 256, "h", 0, True, False), (256, 512, "o", 0, False, True),
                    (512, 1024, "o", 256, True, True),
                    (1024, 1280, "o", 768, True, False), (1280, 1536, "h", 256, False, True)]
            for (pst, Bp, slot) in ((k_ps, B_kps, 2), (v_ps, B_vps, 3)):
                fns = []
                for (a, b, kind, rc, bf, bl) in segs:
                    if kind == "o":
                        rhs = xn[:, c * TOK + rc: c * TOK + rc + (b - a)]
                    else:
                        rhs = xnh[:, rc: rc + (b - a)]
                    fns.append(lambda e, pst=pst, a=a, b=b, rhs=rhs, slot=slot, c=c,
                               bf=bf, bl=bl: e.matmul(
                        pst[:, a:b], wp[slot][:, c * 128:(c + 1) * 128], rhs,
                        start=(c == 0 and bf), stop=(c == NCH - 1 and bl)))
                P.emit("tensor", fns, reads=[B_xn[c], B_xnh, B_wp[slot], B_const],
                       writes=[Bp], incs=[("te", 1)])

        dma("sync", "ds", coskb, ap["cosk"], writes=[B_ck])
        dma("sync", "ds", sinkb, ap["sink"], writes=[B_sk])
        P.emit("scalar", lambda e: e.copy(out=kraw, in_=k_ps[:]),
               reads=[B_kps], writes=[B_kraw], incs=[("sc", 1)])
        P.emit("vector", lambda e: e.tensor_tensor(out=Kr[:], in0=kraw, in1=coskb, op=ALU.mult),
               reads=[B_kraw, B_ck], writes=[B_Kr], incs=[("ve", 1)])
        for s in range(3):
            P.emit("tensor", lambda e, s=s: e.matmul(pkp[:], permT[:],
                   kraw[:, s * 512:(s + 1) * 512], start=True, stop=True),
                   reads=[B_kraw, B_const], writes=[B_pkp], incs=[("te", 1)])
            P.emit("vector", lambda e, s=s: e.tensor_tensor(out=xb[:, 0:512], in0=pkp[:],
                   in1=sinkb[:, s * 512:(s + 1) * 512], op=ALU.mult),
                   reads=[B_pkp, B_sk], writes=[B_xb], incs=[("ve", 1)])
            P.emit("vector", lambda e, s=s: e.tensor_tensor(out=Kr[:, s * 512:(s + 1) * 512],
                   in0=Kr[:, s * 512:(s + 1) * 512], in1=xb[:, 0:512], op=ALU.add),
                   reads=[B_Kr, B_xb], writes=[B_Kr], incs=[("ve", 1)])
        P.emit("scalar", lambda e: e.copy(out=kraw, in_=v_ps[:]),
               reads=[B_vps], writes=[B_kraw], incs=[("sc", 1)])
        for g in range(2):
            fns = [lambda e, t=t, g=g: e.transpose(vt_ps[:, t * 128:(t + 1) * 128],
                   kraw[:, (g * 6 + t) * 128:(g * 6 + t + 1) * 128], identb[:]) for t in range(6)]
            P.emit("tensor", fns, reads=[B_kraw, B_const], writes=[B_vtps], incs=[("te", 1)])
            P.emit("vector", lambda e, g=g: e.tensor_copy(out=V0[:, g * 768:(g + 1) * 768],
                   in_=vt_ps[:]), reads=[B_vtps], writes=[B_V0], incs=[("ve", 1)])

        P.join_all()
        psKV.close()
        st12.close()

        B_wqall = Buf()
        B_wqpart = [Buf() for _ in range(4)]
        for i in range(4):
            a0 = (24 + 4 * (i % 2)) * TOK
            Av = big2[:, a0: a0 + 4096].rearrange("p (h x) -> p h x", x=512)
            bbufs = [B_big[24 + 4 * (i % 2) + k] for k in range(4)]
            dma("gpsimd", "dg", big2[:, a0: a0 + 4096], ap["w1qsl"][i], writes=bbufs)
            P.emit("vector", lambda e, Av=Av: e.tensor_tensor(
                out=Av[:, :, 0:128], in0=Av[:, :, 0:128], in1=Av[:, :, 128:256], op=ALU.add),
                reads=bbufs, writes=bbufs, incs=[("ve", 1)])
            P.emit("vector", lambda e, Av=Av: e.tensor_tensor(
                out=Av[:, :, 256:384], in0=Av[:, :, 256:384], in1=Av[:, :, 384:512], op=ALU.add),
                reads=bbufs, writes=bbufs, incs=[("ve", 1)])
            P.emit("vector", lambda e, Av=Av: e.tensor_tensor(
                out=Av[:, :, 128:256], in0=Av[:, :, 0:128], in1=Av[:, :, 256:384], op=ALU.add),
                reads=bbufs, writes=bbufs, incs=[("ve", 1)])
            dma("sync", "ds", wqpart[:, :, i, :].rearrange("h p j -> p h j"),
                Av[:, :, 128:256], reads=bbufs, writes=[B_wqpart[i]])
        P.emit("gpsimd", lambda e: e.collective_compute(
            "AllGather", ALU.bypass, replica_groups=[[0, 1, 2, 3, 4, 5, 6, 7]],
            ins=[wqpart], outs=[wqall]), reads=B_wqpart, writes=[B_wqall],
            incs=[("cc", 1)])

        if KDEBUG:
            for c in range(NCH):
                dma("sync", "ds", dbg["dxn"].rearrange("(c p) t -> c p t", p=128)[c],
                    xn[:, c * TOK:(c + 1) * TOK], reads=[B_xn[c]])
            dma("sync", "ds", dbg["dkr"], Kr[:], reads=[B_Kr])

        st3 = ExitStack()
        qsb = st3.enter_context(nc.sbuf_tensor("qsb", [128, 1024], BF16))
        qr2 = [st3.enter_context(nc.sbuf_tensor(f"qr{i}", [128, 1024], BF16)) for i in range(2)]
        qtmp = st3.enter_context(nc.sbuf_tensor("qtmp", [128, 1024], BF16))
        rtmp = st3.enter_context(nc.sbuf_tensor("rtmp", [128, 1024], BF16))
        sig = st3.enter_context(nc.sbuf_tensor("sig", [128, 4096], BF16))
        csb = st3.enter_context(nc.sbuf_tensor("csb", [128, 512], BF16))
        cosq = st3.enter_context(nc.sbuf_tensor("cosq_s", [128, 1024], BF16))
        sinq = st3.enter_context(nc.sbuf_tensor("sinq_s", [128, 1024], BF16))
        coef = st3.enter_context(nc.sbuf_tensor("coef_s", [8, 1024], F32))
        csr = st3.enter_context(nc.sbuf_tensor("csr", [8, 512], F32))
        cs = st3.enter_context(nc.sbuf_tensor("cs", [8, 1024], BF16))
        B_qsb, B_qtmp, B_rtmp, B_sig, B_csb = Buf(), Buf(), Buf(), Buf(), Buf()
        B_qr2 = [Buf(), Buf()]
        B_csr = Buf()
        B_csA, B_csB = Buf(), Buf()
        dma("sync", "ds", cosq[:], ap["cosq"], writes=[B_const])
        dma("sync", "ds", sinq[:], ap["sinq"], writes=[B_const])
        dma("sync", "ds", coef[:], ap["coef"], writes=[B_const])

        ps3 = ExitStack()
        scp = ps3.enter_context(nc.psum_tensor("scp", [128, 512], F32))
        qg = ps3.enter_context(nc.psum_tensor("qg", [128, 512], F32))
        wob = [ps3.enter_context(nc.psum_tensor(f"wob{i}", [128, 512], F32)) for i in range(3)]
        denA = ps3.enter_context(nc.psum_tensor("denA", [8, 512], F32))
        denB = ps3.enter_context(nc.psum_tensor("denB", [8, 512], F32))
        csb_ps = ps3.enter_context(nc.psum_tensor("csb_ps", [128, 512], F32))
        B_scp, B_qg = Buf(), Buf()
        B_wob = [Buf() for _ in range(3)]
        B_denA, B_denB, B_csbps = Buf(), Buf(), Buf()

        WGEO = [(0, 256, 256), (0, 512, 0), (256, 512, 0), (512, 512, 0), (768, 256, 0)]
        WBANK = {1: 0, 0: 1, 2: 2, 3: 0, 4: 1}
        ROW_OF = {1: 0, 0: 1, 2: 2, 3: 0, 4: 1}
        DGRP = {1: 0, 0: 0, 2: 0, 3: 1, 4: 1}

        def emit_qgemm_filler(h):
            if h >= H0:
                return []
            slot = h % 2
            par = h % 2
            th = []
            if h + 1 < H0:
                th.append(lambda h=h: load_panel("w0q", h + 1, 128, (h + 1) % 2))
            for t in range(2):
                for g in range(8):
                    fns = []
                    for c in range(g * 4, g * 4 + 4):
                        fns.append(lambda e, c=c, t=t, slot=slot: e.matmul(
                            qg[:], wp[slot][:, c * 128:(c + 1) * 128],
                            xn[:, c * TOK + t * 512: c * TOK + (t + 1) * 512],
                            start=(c == 0), stop=(c == NCH - 1)))
                    th.append(lambda fns=fns, slot=slot: P.emit(
                        "tensor", fns, reads=[B_wp[slot], B_const] + B_xn,
                        writes=[B_qg], incs=[("te", 1)]))
                th.append(lambda t=t: P.emit("scalar",
                    lambda e, t=t: e.copy(out=qsb[:, t * 512:(t + 1) * 512], in_=qg[:]),
                    reads=[B_qg], writes=[B_qsb], incs=[("sc", 1)]))
            for t in range(2):
                th.append(lambda t=t: P.emit("tensor",
                    lambda e, t=t: e.matmul(qg[:], permT[:], qsb[:, t * 512:(t + 1) * 512],
                                            start=True, stop=True),
                    reads=[B_qsb, B_const], writes=[B_qg], incs=[("te", 1)]))
                th.append(lambda t=t, par=par: P.emit("vector",
                    lambda e, t=t, par=par: e.tensor_tensor(
                        out=qr2[par][:, t * 512:(t + 1) * 512], in0=qg[:],
                        in1=sinq[:, t * 512:(t + 1) * 512], op=ALU.mult),
                    reads=[B_qg, B_const], writes=[B_qr2[par]], incs=[("ve", 1)]))
            th.append(lambda par=par: P.emit("vector",
                lambda e: e.tensor_tensor(out=rtmp[:], in0=qsb[:], in1=cosq[:], op=ALU.mult),
                reads=[B_qsb, B_const], writes=[B_rtmp], incs=[("ve", 1)]))
            th.append(lambda par=par: P.emit("vector",
                lambda e, par=par: e.tensor_tensor(out=qr2[par][:], in0=qr2[par][:],
                                                   in1=rtmp[:], op=ALU.add),
                reads=[B_rtmp], writes=[B_qr2[par]], incs=[("ve", 1)]))
            return th

        def emit_unit(h, w, j, with_av, fill=None):
            par = h % 2
            qa, qn, g0 = WGEO[w]
            kc = 2 * w + j
            slotj = j if w not in (4, 0) else 4 + j
            dgrp = DGRP[w]
            dps, Bd = (denA, B_denA) if dgrp == 0 else (denB, B_denB)
            P.emit("tensor", lambda e, kc=kc, qa=qa, qn=qn, g0=g0, par=par: e.matmul(
                scp[:, g0:g0 + qn], Kr[:, kc * 128:(kc + 1) * 128],
                qr2[par][:, qa:qa + qn], start=True, stop=True),
                reads=[B_Kr, B_qr2[par]], writes=[B_scp], incs=[("te", 1)])
            P.emit("scalar", lambda e, slotj=slotj, kc=kc, g0=g0, qn=qn: e.activation(
                out=sig[:, slotj * 512 + g0: slotj * 512 + g0 + qn], in_=scp[:, g0:g0 + qn],
                func=AF.Sigmoid, bias=maskb[:, kc:kc + 1], scale=1.0),
                reads=[B_scp, B_const], writes=[B_sig], incs=[("sc", 1)])
            if fill is not None:
                fill(1)
            P.emit("tensor", lambda e, w=w, j=j, g0=g0, qn=qn, slotj=slotj, dps=dps: e.matmul(
                dps[:, g0:g0 + qn], basis[:, w * 8:(w + 1) * 8],
                sig[:, slotj * 512 + g0: slotj * 512 + g0 + qn],
                start=(w in (1, 3) and j == 0), stop=(w in (2, 4) and j == 3)),
                reads=[B_sig, B_const], writes=[Bd], incs=[("te", 1)])
            if with_av:
                emit_av(h, w, j)

        def emit_av(h, w, j):
            qa, qn, g0 = WGEO[w]
            kc = 2 * w + j
            slotj = j if w not in (4, 0) else 4 + j
            bk = WBANK[w]
            P.emit("tensor", lambda e, w=w, j=j, kc=kc, g0=g0, qn=qn, slotj=slotj, bk=bk: e.matmul(
                wob[bk][:, g0:g0 + qn], V0[:, kc * 128:(kc + 1) * 128],
                sig[:, slotj * 512 + g0: slotj * 512 + g0 + qn], start=(j == 0), stop=(j == 3)),
                reads=[B_sig, B_V0], writes=[B_wob[bk]], incs=[("te", 1)])

        def emit_recip(grp):
            dps, Bd = (denA, B_denA) if grp == 0 else (denB, B_denB)
            Bcs = B_csA if grp == 0 else B_csB
            o = grp * 512
            P.emit("vector", lambda e, dps=dps: e.tensor_scalar(out=csr[:], in0=dps[:],
                   scalar1=1e-30, scalar2=None, op0=ALU.max),
                   reads=[Bd], writes=[B_csr], incs=[("ve", 1)])
            P.emit("vector", lambda e: e.reciprocal(out=csr[:], in_=csr[:]),
                   reads=[B_csr], writes=[B_csr], incs=[("ve", 1)])
            P.emit("vector", lambda e, o=o: e.tensor_tensor(out=cs[:, o:o + 512],
                   in0=coef[:, o:o + 512], in1=csr[:], op=ALU.mult),
                   reads=[B_csr, B_const], writes=[Bcs], incs=[("ve", 1)])

        BROW = {0: ((0, 1), (0, 0)), 1: ((0, 0), (0, 2)),
                2: ((0, 2), (1, 0)), 3: ((1, 0), (1, 1))}

        def emit_blend(h, j):
            (fp, fr), (rp, rr) = BROW[j]
            Bf = B_csA if fp == 0 else B_csB
            Br = B_csA if rp == 0 else B_csB
            P.emit("tensor", [
                lambda e, fp=fp, fr=fr: e.matmul(csb_ps[:, 0:256],
                    rowselb[:, fr * 128:(fr + 1) * 128],
                    cs[0:8, fp * 512 + 256: fp * 512 + 512], start=True, stop=True),
                lambda e, rp=rp, rr=rr: e.matmul(csb_ps[:, 256:512],
                    rowselb[:, rr * 128:(rr + 1) * 128],
                    cs[0:8, rp * 512: rp * 512 + 256], start=True, stop=True),
            ], reads=[Bf, Br, B_const], writes=[B_csbps], incs=[("te", 1)])
            P.emit("scalar", lambda e: e.copy(out=csb[:], in_=csb_ps[:]),
                   reads=[B_csbps], writes=[B_csb], incs=[("sc", 1)])
            bf, br = WBANK[j], WBANK[j + 1]
            P.emit("vector", lambda e, bf=bf: e.tensor_tensor(out=qtmp[:, 0:256],
                   in0=wob[bf][:, 256:512], in1=csb[:, 0:256], op=ALU.mult),
                   reads=[B_wob[bf], B_csb], writes=[B_qtmp], incs=[("ve", 1)])
            P.emit("vector", lambda e, br=br: e.tensor_tensor(out=qtmp[:, 256:512],
                   in0=wob[br][:, 0:256], in1=csb[:, 256:512], op=ALU.mult),
                   reads=[B_wob[br], B_csb], writes=[B_qtmp], incs=[("ve", 1)])
            P.emit("vector", lambda e, j=j, h=h: e.tensor_tensor(
                   out=big2[:, h * TOK + j * 256: h * TOK + (j + 1) * 256],
                   in0=qtmp[:, 0:256], in1=qtmp[:, 256:512], op=ALU.add),
                   reads=[B_qtmp], writes=[B_big[h]], incs=[("ve", 1)])

        load_panel("w0q", 0, 128, 0)
        for th in emit_qgemm_filler(0):
            th()
        for h in range(H0):
            filler = list(emit_qgemm_filler(h + 1))
            fi = 0

            def fill(n=1):
                nonlocal fi
                for _ in range(n):
                    if fi < len(filler):
                        filler[fi]()
                        fi += 1

            for j in range(4):
                emit_unit(h, 1, j, with_av=False, fill=fill)
            if h > 0:
                emit_blend(h - 1, 2)
                fill(1)
                emit_blend(h - 1, 3)
                fill(1)
            for j in range(4):
                emit_av(h, 1, j)
            fill(1)
            for w in (0, 2):
                for j in range(4):
                    emit_unit(h, w, j, with_av=True, fill=fill)
            emit_recip(0)
            for j in range(4):
                emit_unit(h, 3, j, with_av=False, fill=fill)
            emit_blend(h, 0)
            fill(1)
            emit_blend(h, 1)
            fill(1)
            for j in range(4):
                emit_av(h, 3, j)
            fill(1)
            for j in range(4):
                emit_unit(h, 4, j, with_av=False, fill=fill)
            emit_recip(1)
            for j in range(4):
                emit_av(h, 4, j)
            fill(len(filler))
            if h == H0 - 1:
                emit_blend(h, 2)
                emit_blend(h, 3)

        P.join_all()
        ps3.close()
        st3.close()
        st23.close()

        if KDEBUG:
            for c in range(NCH):
                dma("sync", "ds", dbg["dao"].rearrange("(c p) t -> c p t", p=128)[c],
                    big2[:, c * TOK:(c + 1) * TOK], reads=[B_big[c]])

        def emit_gate_lora(layer):
            stg = ExitStack()
            g1 = stg.enter_context(nc.sbuf_tensor(f"g1_{layer}", [128, 2048], BF16))
            lg = stg.enter_context(nc.sbuf_tensor(f"lg_{layer}", [8, 1024], F32))
            lgT = stg.enter_context(nc.sbuf_tensor(f"lgT_{layer}", [128, 64], F32))
            tA = stg.enter_context(nc.sbuf_tensor(f"tA_{layer}", [128, 64], F32))
            tB = stg.enter_context(nc.sbuf_tensor(f"tB_{layer}", [128, 64], F32))
            tC = stg.enter_context(nc.sbuf_tensor(f"tC_{layer}", [128, 8], F32))
            tD = stg.enter_context(nc.sbuf_tensor(f"tD_{layer}", [128, 8], F32))
            wtk = stg.enter_context(nc.sbuf_tensor(f"wtk_{layer}", [128, 64], BF16))
            wfe = stg.enter_context(nc.sbuf_tensor(f"wfe_{layer}", [8, 1024], BF16))
            B_g1, B_lg, B_lgT = Buf(), Buf(), Buf()
            B_tA, B_tB, B_tC, B_tD, B_wtk, B_wfe = Buf(), Buf(), Buf(), Buf(), Buf(), Buf()

            psg = ExitStack()
            pg1 = psg.enter_context(nc.psum_tensor(f"pg1_{layer}", [128, 1024], F32))
            pg2 = psg.enter_context(nc.psum_tensor(f"pg2_{layer}", [128, 1024], F32))
            psc = psg.enter_context(nc.psum_tensor(f"psc_{layer}", [8, 1024], F32))
            B_psc = Buf()
            ptr = psg.enter_context(nc.psum_tensor(f"ptr_{layer}", [128, 64], F32))
            pwf = psg.enter_context(nc.psum_tensor(f"pwf_{layer}", [8, 1024], BF16))
            B_pg1, B_pg2, B_ptr, B_pwf = Buf(), Buf(), Buf(), Buf()

            def gemm(pst, Bp, slot, ncol=128, extra=None):
                fns = []
                for c in range(NCH):
                    for t in range(2):
                        fns.append(lambda e, c=c, t=t, slot=slot, ncol=ncol, pst=pst: e.matmul(
                            pst[:, t * 512:(t + 1) * 512] if ncol == 128 else
                            pst[0:ncol, t * 512:(t + 1) * 512],
                            wp[slot][:, c * ncol:(c + 1) * ncol],
                            xn[:, c * TOK + t * 512: c * TOK + (t + 1) * 512],
                            start=(c == 0), stop=(c == NCH - 1)))
                P.emit("tensor", fns, reads=[B_wp[slot], B_const] + B_xn,
                       writes=[Bp], incs=[("te", 1)])

            if layer == 0:
                for oc in range(2):
                    load_panel("w0fq", oc, 128, 0)
                    gemm(pg1, B_pg1, 0)
                    P.emit("scalar", lambda e: e.copy(out=g1[:, 0:1024], in_=pg1[:]),
                           reads=[B_pg1], writes=[B_g1], incs=[("sc", 1)])
                    load_panel("w0fk", oc, 128, 1)
                    gemm(pg2, B_pg2, 1)
                    P.emit("vector", lambda e: e.tensor_tensor(out=g1[:, 1024:2048],
                           in0=pg2[:], in1=g1[:, 0:1024], op=ALU.mult),
                           reads=[B_pg2, B_g1], writes=[B_g1], incs=[("ve", 1)])
                    fns = [lambda e, t=t, oc=oc: e.matmul(psc[0:8, t * 512:(t + 1) * 512],
                           blocksel[:, oc * 8:(oc + 1) * 8],
                           g1[:, 1024 + t * 512:1024 + (t + 1) * 512],
                           start=(oc == 0), stop=(oc == 1)) for t in range(2)]
                    P.emit("tensor", fns, reads=[B_g1, B_const], writes=[B_psc], incs=[("te", 1)])
                P.emit("scalar", lambda e: e.copy(out=lg[:], in_=psc[0:8, :]),
                       reads=[B_psc], writes=[B_lg], incs=[("sc", 1)])
            else:
                load_panel("gw1", 0, 8, 0)
                gemm(pg1, B_pg1, 0, ncol=8)
                P.emit("scalar", lambda e: e.copy(out=lg[:], in_=pg1[0:8, :]),
                       reads=[B_pg1], writes=[B_lg], incs=[("sc", 1)])

            fns = [lambda e, tc=tc: e.transpose(ptr[:, tc * 8:(tc + 1) * 8],
                   lg[0:8, tc * 128:(tc + 1) * 128], identf[:]) for tc in range(8)]
            P.emit("tensor", fns, reads=[B_lg, B_const], writes=[B_ptr], incs=[("te", 1)])
            P.emit("vector", lambda e: e.tensor_copy(out=lgT[:], in_=ptr[:]),
                   reads=[B_ptr], writes=[B_lgT], incs=[("ve", 1)])
            v3 = lgT[:].rearrange("p (a b) -> p a b", a=8)
            P.emit("vector", lambda e: e.reduce_max(out=tC[:], in_=v3, axis=AXX),
                   reads=[B_lgT], writes=[B_tC], incs=[("ve", 1)])
            if layer == 0:
                P.emit("vector", lambda e: e.tensor_tensor(out=tA[:], in0=lgT[:],
                       in1=tC[:].rearrange("p (a b) -> p a b", b=1).broadcast_to([128, 8, 8]), op=ALU.subtract),
                       reads=[B_lgT, B_tC], writes=[B_tA], incs=[("ve", 1)])
                P.emit("scalar", lambda e: e.activation(out=tA[:], in_=tA[:], func=AF.Exp,
                       scale=float(1.0 / np.sqrt(32.0))),
                       reads=[B_tA], writes=[B_tA], incs=[("sc", 1)])
                P.emit("vector", lambda e: e.reduce_sum(out=tC[:], in_=tA[:].rearrange(
                       "p (a b) -> p a b", a=8), axis=AXX),
                       reads=[B_tA], writes=[B_tC], incs=[("ve", 1)])
                P.emit("vector", lambda e: e.reciprocal(out=tD[:], in_=tC[:]),
                       reads=[B_tC], writes=[B_tD], incs=[("ve", 1)])
                P.emit("vector", lambda e: e.tensor_tensor(out=wtk[:], in0=tA[:],
                       in1=tD[:].rearrange("p (a b) -> p a b", b=1).broadcast_to([128, 8, 8]), op=ALU.mult),
                       reads=[B_tA, B_tD], writes=[B_wtk], incs=[("ve", 1)])
            else:
                P.emit("vector", lambda e: e.tensor_tensor(out=tA[:], in0=lgT[:],
                       in1=tC[:].rearrange("p (a b) -> p a b", b=1).broadcast_to([128, 8, 8]), op=ALU.is_equal),
                       reads=[B_lgT, B_tC], writes=[B_tA], incs=[("ve", 1)])
                P.emit("vector", lambda e: e.scalar_tensor_tensor(out=tA[:], in0=tA[:],
                       scalar=NEG, in1=lgT[:], op0=ALU.mult, op1=ALU.add),
                       reads=[B_tA, B_lgT], writes=[B_tA], incs=[("ve", 1)])
                P.emit("vector", lambda e: e.reduce_max(out=tD[:], in_=tA[:].rearrange(
                       "p (a b) -> p a b", a=8), axis=AXX),
                       reads=[B_tA], writes=[B_tD], incs=[("ve", 1)])
                P.emit("vector", lambda e: e.tensor_tensor(out=tA[:], in0=lgT[:],
                       in1=tC[:].rearrange("p (a b) -> p a b", b=1).broadcast_to([128, 8, 8]), op=ALU.subtract),
                       reads=[B_lgT, B_tC], writes=[B_tA], incs=[("ve", 1)])
                P.emit("scalar", lambda e: e.activation(out=tA[:], in_=tA[:], func=AF.Exp),
                       reads=[B_tA], writes=[B_tA], incs=[("sc", 1)])
                P.emit("vector", lambda e: e.tensor_tensor(out=tB[:], in0=lgT[:],
                       in1=tD[:].rearrange("p (a b) -> p a b", b=1).broadcast_to([128, 8, 8]), op=ALU.is_ge),
                       reads=[B_lgT, B_tD], writes=[B_tB], incs=[("ve", 1)])
                P.emit("vector", lambda e: e.tensor_tensor(out=tD[:], in0=tD[:], in1=tC[:],
                       op=ALU.subtract), reads=[B_tD, B_tC], writes=[B_tD], incs=[("ve", 1)])
                P.emit("scalar", lambda e: e.activation(out=tD[:], in_=tD[:], func=AF.Exp),
                       reads=[B_tD], writes=[B_tD], incs=[("sc", 1)])
                P.emit("vector", lambda e: e.tensor_scalar(out=tD[:], in0=tD[:], scalar1=1.0,
                       scalar2=None, op0=ALU.add), reads=[B_tD], writes=[B_tD], incs=[("ve", 1)])
                P.emit("vector", lambda e: e.reciprocal(out=tD[:], in_=tD[:]),
                       reads=[B_tD], writes=[B_tD], incs=[("ve", 1)])
                P.emit("vector", lambda e: e.tensor_tensor(out=tA[:], in0=tA[:], in1=tB[:],
                       op=ALU.mult), reads=[B_tA, B_tB], writes=[B_tA], incs=[("ve", 1)])
                P.emit("vector", lambda e: e.tensor_tensor(out=wtk[:], in0=tA[:],
                       in1=tD[:].rearrange("p (a b) -> p a b", b=1).broadcast_to([128, 8, 8]), op=ALU.mult),
                       reads=[B_tA, B_tD], writes=[B_wtk], incs=[("ve", 1)])

            fns = [lambda e, tc=tc: e.transpose(pwf[:, tc * 128:(tc + 1) * 128],
                   wtk[:, tc * 8:(tc + 1) * 8], identb[:]) for tc in range(8)]
            P.emit("tensor", fns, reads=[B_wtk, B_const], writes=[B_pwf], incs=[("te", 1)])
            P.emit("scalar", lambda e: e.copy(out=wfe[:], in_=pwf[:]),
                   reads=[B_pwf], writes=[B_wfe], incs=[("sc", 1)])
            exp = expand1 if layer == 0 else expand2
            fns = [lambda e, t=t, exp=exp: e.matmul(pg1[:, t * 512:(t + 1) * 512], exp[:],
                   wfe[:, t * 512:(t + 1) * 512], start=True, stop=True) for t in range(2)]
            P.emit("tensor", fns, reads=[B_wfe, B_const], writes=[B_pg1], incs=[("te", 1)])
            load_panel("a0" if layer == 0 else "la1", 0, 128, 0)
            gemm(pg2, B_pg2, 0)
            P.emit("scalar", lambda e: e.copy(out=g1[:, 0:1024], in_=pg2[:]),
                   reads=[B_pg2], writes=[B_g1], incs=[("sc", 1)])
            P.emit("vector", lambda e: e.tensor_tensor(out=hw[:], in0=pg1[:], in1=g1[:, 0:1024],
                   op=ALU.mult), reads=[B_pg1, B_g1], writes=[B_hw], incs=[("ve", 1)])
            P.join_all()
            psg.close()
            stg.close()

        emit_gate_lora(0)
        if KDEBUG:
            dma("sync", "ds", dbg["dhw"], hw[:], reads=[B_hw])

        x1d_t = x1d.rearrange("(c p) t -> c p t", p=128)
        x1b_t = x1b.rearrange("(c p) t -> c p t", p=128)
        y_t = y.rearrange("(c p) t -> c p t", p=128)

        def emit_epilogue(layer):
            is0 = layer == 0
            wo_name = "w0o" if is0 else "w1o"
            wf_name = "w0f" if is0 else "bw1"
            bias_sb = b0sb if is0 else b1sb
            nsc = NCH if is0 else 8
            ste = ExitStack()
            bml = ste.enter_context(nc.sbuf_tensor(f"bml_{layer}", [128, 4096], BF16))
            gsb = ste.enter_context(nc.sbuf_tensor(f"gsb_{layer}", [128, 1024], BF16))
            tsb = ste.enter_context(nc.sbuf_tensor(f"tsb_{layer}", [128, 1024], BF16))
            xob = [ste.enter_context(nc.sbuf_tensor(f"xob{layer}_{i}", [128, 1024], F32)) for i in range(2)]
            B_bml, B_gsb, B_tsb = Buf(), Buf(), Buf()
            B_xob = [Buf(), Buf()]
            ret = None
            pse = ExitStack()
            s_ps = pse.enter_context(nc.psum_tensor(f"s_ps_{layer}", [128, 1024], F32))
            g_ps = pse.enter_context(nc.psum_tensor(f"g_ps_{layer}", [128, 1024], F32))
            B_sps, B_gps = Buf(), Buf()
            if is0:
                xb1 = ste.enter_context(nc.sbuf_tensor(f"xb1_{layer}", [128, 1024], BF16))
                sq1 = ste.enter_context(nc.sbuf_tensor(f"sq1_{layer}", [128, 1024], BF16))
                B_xb1, B_sq1 = Buf(), Buf()
                sx1 = pse.enter_context(nc.psum_tensor(f"sx1_{layer}", [1, 1024], F32))
                sx21 = pse.enter_context(nc.psum_tensor(f"sx21_{layer}", [1, 1024], F32))
                B_sx1, B_sx21 = Buf(), Buf()
                ret = (ste, pse, sx1, sx21, B_sx1, B_sx21, s_ps, B_sps)

            dma("gpsimd", "dg", bml[:], ap["bm0" if is0 else "lb1"], writes=[B_bml])
            load_panel(wo_name, 0, 128, 0, nch=nsc)
            if is0:
                load_panel(wf_name, 0, 128, 2)
            for oc in range(NCH):
                so, sf = oc % 2, 2 + (oc % 2)
                if oc + 1 < NCH:
                    load_panel(wo_name, oc + 1, 128, (oc + 1) % 2, nch=nsc)
                    if is0 or oc + 1 >= 8:
                        load_panel(wf_name, oc + 1, 128, 2 + ((oc + 1) % 2))
                po = oc % 2
                if is0:
                    dma("sync", "ds", xob[po][:], xloc_t[oc][:, 256:1280], writes=[B_xob[po]])
                else:
                    dma("sync", "ds", xob[po][:], x1d_t[oc], reads=[B_x1d[oc]], writes=[B_xob[po]])
                fns = []
                for c in range(nsc):
                    for t in range(2):
                        fns.append(lambda e, c=c, t=t, so=so: e.matmul(
                            s_ps[:, t * 512:(t + 1) * 512], wp[so][:, c * 128:(c + 1) * 128],
                            big2[:, c * TOK + t * 512: c * TOK + (t + 1) * 512],
                            start=(c == 0), stop=(c == nsc - 1)))
                P.emit("tensor", fns, reads=[B_wp[so], B_const] + B_big[:nsc],
                       writes=[B_sps], incs=[("te", 1)])
                gpre = (not is0) and oc < 8
                if not gpre:
                    fns = []
                    for c in range(NCH):
                        for t in range(2):
                            fns.append(lambda e, c=c, t=t, sf=sf: e.matmul(
                                g_ps[:, t * 512:(t + 1) * 512], wp[sf][:, c * 128:(c + 1) * 128],
                                xn[:, c * TOK + t * 512: c * TOK + (t + 1) * 512],
                                start=(c == 0), stop=False))
                    for t in range(2):
                        fns.append(lambda e, t=t, oc=oc: e.matmul(
                            g_ps[:, t * 512:(t + 1) * 512], bml[:, oc * 128:(oc + 1) * 128],
                            hw[:, t * 512:(t + 1) * 512], start=False, stop=True))
                    P.emit("tensor", fns, reads=[B_wp[sf], B_bml, B_hw, B_const] + B_xn,
                           writes=[B_gps], incs=[("te", 1)])
                    P.emit("scalar", lambda e, oc=oc: e.activation(out=gsb[:], in_=g_ps[:],
                           func=AF.Silu, bias=bias_sb[:, oc:oc + 1], scale=1.0),
                           reads=[B_gps, B_const], writes=[B_gsb], incs=[("sc", 1)])
                else:
                    P.emit("scalar", lambda e, oc=oc: e.activation(out=gsb[:],
                           in_=big2[:, (24 + oc) * TOK:(25 + oc) * TOK],
                           func=AF.Silu, bias=bias_sb[:, oc:oc + 1], scale=1.0),
                           reads=[B_big[24 + oc], B_const], writes=[B_gsb], incs=[("sc", 1)])
                P.emit("vector", lambda e: e.tensor_tensor(out=tsb[:], in0=s_ps[:], in1=gsb[:],
                       op=ALU.mult), reads=[B_sps, B_gsb], writes=[B_tsb], incs=[("ve", 1)])
                P.emit("gpsimd", lambda e, po=po: e.tensor_tensor(out=xob[po][:], in0=xob[po][:],
                       in1=tsb[:], op=ALU.add),
                       reads=[B_xob[po], B_tsb], writes=[B_xob[po]], incs=[("gp", 1)])
                if is0:
                    dma("sync", "ds", x1d_t[oc], xob[po][:], reads=[B_xob[po]], writes=[B_x1d[oc]])
                    P.emit("vector", lambda e, po=po: e.tensor_copy(out=xb1[:], in_=xob[po][:]),
                           reads=[B_xob[po]], writes=[B_xb1], incs=[("ve", 1)])
                    dma("sync", "ds", x1b_t[oc], xb1[:], reads=[B_xb1], writes=[B_x1b[oc]])
                    P.emit("vector", lambda e: e.tensor_tensor(out=sq1[:], in0=xb1[:], in1=xb1[:],
                           op=ALU.mult), reads=[B_xb1], writes=[B_sq1], incs=[("ve", 1)])
                    for (src, Bs, pst, Bp) in ((xb1, B_xb1, sx1, B_sx1), (sq1, B_sq1, sx21, B_sx21)):
                        fns = [lambda e, src=src, pst=pst, t=t, oc=oc: e.matmul(
                            pst[0:1, t * 512:(t + 1) * 512], ones1b[:], src[:, t * 512:(t + 1) * 512],
                            start=(oc == 0), stop=(oc == NCH - 1)) for t in range(2)]
                        P.emit("tensor", fns, reads=[Bs, B_const], writes=[Bp], incs=[("te", 1)])
                else:
                    dma("sync", "ds", y_t[oc], xob[po][:], reads=[B_xob[po]], writes=[B_y])
            if not is0:
                P.join_all()
                pse.close()
                ste.close()
            return ret

        st6 = ExitStack()
        rstdb1 = st6.enter_context(nc.sbuf_tensor("rstdb1", [128, 1024], F32))
        negmrb1 = st6.enter_context(nc.sbuf_tensor("negmrb1", [128, 1024], F32))

        (ste0, pse0, sx1, sx21, B_sx1, B_sx21, sps0, B_sps0) = emit_epilogue(0)

        B_stat1, B_rstdb1, B_negmrb1 = Buf(), Buf(), Buf()
        B_xscr6, B_xscr6b = Buf(), Buf()
        emit_stats_finalize(B_stat1, sx1, sx21, B_sx1, B_sx21,
                            rstdb1, negmrb1, B_rstdb1, B_negmrb1, 1024,
                            sps0[:, 0:512], B_sps0)
        snap6 = [(s, P.cnt[s]) for s in P.cnt if P.cnt[s] > 0]
        pse0.close()
        ste0.close()

        st6b = ExitStack()
        xscr6 = st6b.enter_context(nc.sbuf_tensor("xscr6", [128, 1024], F32))
        xscr6b = st6b.enter_context(nc.sbuf_tensor("xscr6b", [128, 1024], F32))
        x6 = [st6b.enter_context(nc.sbuf_tensor(f"x6_{i}", [128, 1024], BF16)) for i in range(4)]
        B_x6 = [Buf() for _ in range(4)]

        st7 = ExitStack()
        kevac = st7.enter_context(nc.sbuf_tensor("kevac", [128, 1024], BF16))
        B_kevac = Buf()
        ps7 = ExitStack()
        p7 = ps7.enter_context(nc.psum_tensor("p7", [128, 1024], F32))
        B_p7 = Buf()
        kvs_t = kvs.rearrange("(c p) t -> c p t", p=128)
        QS = 8 * TOK

        jobs = [("w1k", oc) for oc in range(8)] + [("w1v", oc) for oc in range(8)]

        def load_job(j):
            wname, oc = jobs[j]
            dma("gpsimd", "dg", wp[j % 4][:, 0:NCH * 128], ap[wname][oc],
                writes=[B_wp[j % 4]])

        for j in range(3):
            load_job(j)

        for c in range(NCH):
            p = c % 4
            eng, sem = ("vector", "ve") if c % 2 == 0 else ("gpsimd", "gp")
            scr, Bscr = (xscr6, B_xscr6) if c % 2 == 0 else (xscr6b, B_xscr6b)
            w6 = snap6 if c < 4 else ()
            dma("sync", "ds", x6[p][:], x1b_t[c], reads=[B_x1b[c]], writes=[B_x6[p]],
                waits=w6)
            P.emit(eng, lambda e, p=p, scr=scr: e.tensor_tensor(out=scr[:], in0=x6[p][:],
                   in1=rstdb1[:], op=ALU.mult),
                   reads=[B_x6[p], B_rstdb1], writes=[Bscr], incs=[(sem, 1)], waits=w6)
            P.emit(eng, lambda e, c=c, scr=scr: e.tensor_tensor(out=xn[:, c * TOK:(c + 1) * TOK],
                   in0=scr[:], in1=negmrb1[:], op=ALU.add),
                   reads=[Bscr, B_negmrb1], writes=[B_xn[c]], incs=[(sem, 1)])
        if KDEBUG:
            for c in range(NCH):
                dma("sync", "ds", dbg["dxn1"].rearrange("(c p) t -> c p t", p=128)[c],
                    xn[:, c * TOK:(c + 1) * TOK], reads=[B_xn[c]])

        for j, (wname, oc) in enumerate(jobs):
            slot = j % 4
            if j + 3 < len(jobs):
                load_job(j + 3)
            if j < 2:
                for c in range(NCH):
                    fns = [lambda e, c=c, t=t, slot=slot: e.matmul(
                        p7[:, t * 512:(t + 1) * 512], wp[slot][:, c * 128:(c + 1) * 128],
                        xn[:, c * TOK + t * 512: c * TOK + (t + 1) * 512],
                        start=(c == 0), stop=(c == NCH - 1)) for t in range(2)]
                    wb = [B_p7] if c in (0, NCH - 1) else []
                    P.emit("tensor", fns, reads=[B_wp[slot], B_xn[c]], writes=wb,
                           incs=[("te", 1)],
                           waits=(snap6 if j == 0 and c == 0 else ()))
            else:
                fns = []
                for c in range(NCH):
                    for t in range(2):
                        fns.append(lambda e, c=c, t=t, slot=slot: e.matmul(
                            p7[:, t * 512:(t + 1) * 512], wp[slot][:, c * 128:(c + 1) * 128],
                            xn[:, c * TOK + t * 512: c * TOK + (t + 1) * 512],
                            start=(c == 0), stop=(c == NCH - 1)))
                P.emit("tensor", fns, reads=[B_wp[slot], B_const] + B_xn,
                       writes=[B_p7], incs=[("te", 1)])
            P.emit("scalar", lambda e: e.copy(out=kevac[:], in_=p7[:]),
                   reads=[B_p7], writes=[B_kevac], incs=[("sc", 1)],
                   waits=(snap6 if j == 0 else ()))
            dma("sync", "ds", kvs_t[j], kevac[:], reads=[B_kevac], writes=[B_kvs])
        _kvw = [(f"ds{i}", P.cnt[f"ds{i}"]) for i in range(NLANES["ds"])]
        P.emit("gpsimd", lambda e: e.collective_compute(
            "AllGather", ALU.bypass, replica_groups=[[0, 1], [2, 3], [4, 5], [6, 7]],
            ins=[kvs], outs=[kva]), reads=[B_kvs], writes=[B_kva], incs=[("cc", 1)],
            waits=_kvw)
        P.join_all()
        ps7.close()

        emit_gate_lora(1)

        ps7 = ExitStack()
        p7 = ps7.enter_context(nc.psum_tensor("p7b", [128, 1024], F32))
        B_p7 = Buf()

        def load_wqs_panel(oc, slot):
            src = wqall[:, oc].rearrange("r p i j -> p r (i j)")
            dstv = wp[slot][:, 0:NCH * 128].rearrange("p (r y) -> p r y", y=512)
            dma("sync", "ds", dstv, src, reads=[B_wqall], writes=[B_wp[slot]])

        load_wqs_panel(0, 0)
        for oc in range(8):
            slot = oc % 2
            if oc + 1 < 8:
                load_wqs_panel(oc + 1, (oc + 1) % 2)
            fns = []
            for c in range(NCH):
                for t in range(2):
                    fns.append(lambda e, c=c, t=t, slot=slot: e.matmul(
                        p7[:, t * 512:(t + 1) * 512], wp[slot][:, c * 128:(c + 1) * 128],
                        xn[:, c * TOK + t * 512: c * TOK + (t + 1) * 512],
                        start=(c == 0), stop=(c == NCH - 1)))
            P.emit("tensor", fns, reads=[B_wp[slot], B_const] + B_xn,
                   writes=[B_p7], incs=[("te", 1)])
            P.emit("scalar", lambda e, oc=oc: e.copy(
                   out=big2[:, QS + oc * TOK: QS + (oc + 1) * TOK], in_=p7[:]),
                   reads=[B_p7], writes=[B_big[8 + oc]], incs=[("sc", 1)])
        P.join_all()
        ps7.close()
        st7.close()
        st6b.close()
        st6.close()
        if KDEBUG:
            for c in range(8):
                dma("sync", "ds", dbg["dqs"].rearrange("(c p) t -> c p t", p=128)[c],
                    big2[:, QS + c * TOK: QS + (c + 1) * TOK], reads=[B_big[8 + c]])

        if KDEBUG:
            dma("sync", "ds", dbg["dhw1"], hw[:], reads=[B_hw])

        st9 = ExitStack()
        Kh2 = [st9.enter_context(nc.sbuf_tensor(f"Kh{i}", [128, 2048], BF16)) for i in range(2)]
        Vh2 = [st9.enter_context(nc.sbuf_tensor(f"Vh{i}", [128, 2048], BF16)) for i in range(2)]
        Vtm2 = [st9.enter_context(nc.sbuf_tensor(f"Vtm{i}", [128, 2048], BF16)) for i in range(2)]
        rdenS = st9.enter_context(nc.sbuf_tensor("rdenS", [128, 512], F32))
        bm9 = st9.enter_context(nc.sbuf_tensor("bm9", [128, 4096], BF16))
        B_bm9 = Buf()
        B_Kh2 = [Buf(), Buf()]
        B_Vh2 = [Buf(), Buf()]
        B_Vtm2 = [Buf(), Buf()]
        B_rden = Buf()
        PTB = 16 * TOK
        B_pt = [Buf() for _ in range(8)]
        ps9 = ExitStack()
        sc9 = [ps9.enter_context(nc.psum_tensor(f"sc9_{i}", [128, 512], F32)) for i in range(3)]
        av9 = ps9.enter_context(nc.psum_tensor("av9", [128, 512], F32))
        den9 = ps9.enter_context(nc.psum_tensor("den9", [128, 512], F32))
        vt9 = ps9.enter_context(nc.psum_tensor("vt9", [128, 1024], BF16))
        g9 = ps9.enter_context(nc.psum_tensor("g9", [128, 512], F32))
        B_sc9 = [Buf() for _ in range(3)]
        B_av9, B_den9, B_vt9, B_g9 = Buf(), Buf(), Buf(), Buf()

        def emit_g_filler():
            th = []
            th.append(lambda: dma("gpsimd", "dg", bm9[:], ap["lb1"], writes=[B_bm9]))
            th.append(lambda: load_panel("bw1", 0, 128, 0))
            for oc in range(8):
                slot = oc % 2
                if oc + 1 < 8:
                    th.append(lambda oc=oc: load_panel("bw1", oc + 1, 128, (oc + 1) % 2))
                for t in range(2):
                    for g in range(8):
                        fns = []
                        for c in range(g * 4, g * 4 + 4):
                            fns.append(lambda e, c=c, t=t, slot=slot: e.matmul(
                                g9[:], wp[slot][:, c * 128:(c + 1) * 128],
                                xn[:, c * TOK + t * 512: c * TOK + (t + 1) * 512],
                                start=(c == 0), stop=False))
                        th.append(lambda fns=fns, slot=slot: P.emit("tensor", fns,
                            reads=[B_wp[slot], B_const] + B_xn, writes=[B_g9], incs=[("te", 1)]))
                    th.append(lambda t=t, oc=oc: P.emit("tensor",
                        lambda e, t=t, oc=oc: e.matmul(g9[:], bm9[:, oc * 128:(oc + 1) * 128],
                            hw[:, t * 512:(t + 1) * 512], start=False, stop=True),
                        reads=[B_bm9, B_hw, B_const], writes=[B_g9], incs=[("te", 1)]))
                    th.append(lambda t=t, oc=oc: P.emit("scalar", lambda e, t=t, oc=oc: e.copy(
                        out=big2[:, (24 + oc) * TOK + t * 512:(24 + oc) * TOK + (t + 1) * 512],
                        in_=g9[:]), reads=[B_g9], writes=[B_big[24 + oc]], incs=[("sc", 1)]))
            return th

        gfill = emit_g_filler()
        gfi = 0

        def gf(n=1):
            nonlocal gfi
            for _ in range(n):
                if gfi < len(gfill):
                    gfill[gfi]()
                    gfi += 1

        def emit_kv_load(h):
            hp = h % 2
            for r in range(2):
                dma("sync", "ds", Kh2[hp][:, r * 1024:(r + 1) * 1024],
                    kva[r, h * 128:(h + 1) * 128, :], reads=[B_kva], writes=[B_Kh2[hp]],
                    waits=[("cc", P.cnt["cc"])])
                dma("sync", "ds", Vh2[hp][:, r * 1024:(r + 1) * 1024],
                    kva[r, 1024 + h * 128: 1024 + (h + 1) * 128, :],
                    reads=[B_kva], writes=[B_Vh2[hp]], waits=[("cc", P.cnt["cc"])])

        pendfin = [None]

        def emit_fin():
            if pendfin[0] is None:
                return
            h, qgv = pendfin[0]
            pendfin[0] = None
            P.emit("vector", lambda e: e.reciprocal(out=rdenS[:], in_=den9[:]),
                   reads=[B_den9], writes=[B_rden], incs=[("ve", 1)])
            P.emit("vector", lambda e, h=h, qgv=qgv: e.tensor_tensor(
                   out=big2[:, h * TOK + qgv * 512: h * TOK + (qgv + 1) * 512],
                   in0=av9[:], in1=rdenS[:], op=ALU.mult),
                   reads=[B_av9, B_rden], writes=[B_big[h]], incs=[("ve", 1)])

        emit_kv_load(0)
        for h in range(NKV):
            hp = h % 2
            Kh, Vh, Vtm = Kh2[hp], Vh2[hp], Vtm2[hp]
            B_Kh, B_Vh, B_Vtm = B_Kh2[hp], B_Vh2[hp], B_Vtm2[hp]
            if h + 1 < NKV:
                emit_kv_load(h + 1)
            for g in range(2):
                fns = [lambda e, g=g, t=t, Vh=Vh: e.transpose(vt9[:, t * 128:(t + 1) * 128],
                       Vh[:, (g * 8 + t) * 128:(g * 8 + t + 1) * 128], identb[:]) for t in range(8)]
                P.emit("tensor", fns, reads=[B_Vh, B_const], writes=[B_vt9], incs=[("te", 1)])
                P.emit("vector", lambda e, g=g, Vtm=Vtm: e.tensor_copy(
                       out=Vtm[:, g * 1024:(g + 1) * 1024],
                       in_=vt9[:]), reads=[B_vt9], writes=[B_Vtm], incs=[("ve", 1)])
                gf(1)
            for qgv in range(2):
                Qs = big2[:, QS + h * TOK + qgv * 512: QS + h * TOK + (qgv + 1) * 512]

                def s_mm(kc, Kh=Kh, B_Kh=B_Kh, Qs=Qs, h=h):
                    P.emit("tensor", lambda e, kc=kc, Kh=Kh, Qs=Qs: e.matmul(
                        sc9[kc % 3][:], Kh[:, kc * 128:(kc + 1) * 128], Qs,
                        start=True, stop=True),
                        reads=[B_Kh, B_big[8 + h]], writes=[B_sc9[kc % 3]], incs=[("te", 1)])

                def e_act(kc):
                    P.emit("scalar", lambda e, kc=kc: e.activation(
                        out=big2[:, PTB + (kc % 8) * 512: PTB + (kc % 8 + 1) * 512],
                        in_=sc9[kc % 3][:], func=AF.Exp, scale=SCALE),
                        reads=[B_sc9[kc % 3]], writes=[B_pt[kc % 8]], incs=[("sc", 1)])

                for kc in range(3):
                    s_mm(kc)
                    e_act(kc)
                emit_fin()
                for kc in range(16):
                    pt = big2[:, PTB + (kc % 8) * 512: PTB + (kc % 8 + 1) * 512]
                    P.emit("tensor", lambda e, pt=pt, kc=kc, Vtm=Vtm: e.matmul(
                        av9[:], Vtm[:, kc * 128:(kc + 1) * 128], pt,
                        start=(kc == 0), stop=(kc == 15)),
                        reads=[B_pt[kc % 8], B_Vtm], writes=[B_av9], incs=[("te", 1)])
                    P.emit("tensor", lambda e, pt=pt, kc=kc: e.matmul(
                        den9[:], ones128[:], pt, start=(kc == 0), stop=(kc == 15)),
                        reads=[B_pt[kc % 8], B_const], writes=[B_den9], incs=[("te", 1)])
                    if kc + 3 < 16:
                        s_mm(kc + 3)
                        e_act(kc + 3)
                    gf(1)
                pendfin[0] = (h, qgv)
        emit_fin()
        gf(len(gfill))

        P.join_all()
        ps9.close()
        st9.close()
        if KDEBUG:
            for c in range(8):
                dma("sync", "ds", dbg["dval"].rearrange("(c p) t -> c p t", p=128)[c],
                    big2[:, c * TOK:(c + 1) * TOK], reads=[B_big[c]])
            for c in range(NCH):
                dma("sync", "ds", dbg["dx1"].rearrange("(c p) t -> c p t", p=128)[c],
                    x1d_t[c], reads=[B_x1d[c]])

        emit_epilogue(1)

        P.join("sync")

        with nc.Block() as block:
            prog.replay(block)
    return nc


def _panelize(wt, ncol=128):
    din, o = wt.shape
    nch, npan = din // 128, o // ncol
    return np.ascontiguousarray(
        wt.reshape(nch, 128, npan, ncol).transpose(2, 1, 0, 3).reshape(npan, 128, nch * ncol))


def kernel(**inputs):
    f32 = np.float32
    x = np.asarray(inputs["L0_attn_x"], f32)
    W = {}
    W["w0q"] = _panelize(np.asarray(inputs["L0_attn_Wq"], f32).T)
    W["w0k"] = _panelize(np.asarray(inputs["L0_attn_Wk"], f32).T)
    W["w0v"] = _panelize(np.asarray(inputs["L0_attn_Wv"], f32).T)
    W["w0o"] = _panelize(np.asarray(inputs["L0_attn_Wo"], f32).T)
    W["w0f"] = _panelize(np.asarray(inputs["L0_ffn_W"], f32).T)
    W["w0fq"] = _panelize(np.asarray(inputs["L0_ffn_Wq"], f32).T)
    W["w0fk"] = _panelize(np.asarray(inputs["L0_ffn_Wk"], f32).T)
    W["a0"] = _panelize(np.asarray(inputs["L0_ffn_A"], f32).reshape(128, D).T)
    W["bm0"] = np.ascontiguousarray(np.asarray(inputs["L0_ffn_Bm"], f32).transpose(0, 2, 1).reshape(128, D))
    w1qf = np.asarray(inputs["L1_attn_wq"], f32)
    W["w1k"] = _panelize(np.asarray(inputs["L1_attn_wk"], f32))
    W["w1v"] = _panelize(np.asarray(inputs["L1_attn_wv"], f32))
    W["w1o"] = _panelize(np.asarray(inputs["L1_attn_wo"], f32))
    W["gw1"] = _panelize(np.asarray(inputs["L1_ffn_gate_w"], f32).T, ncol=8)
    W["bw1"] = _panelize(np.asarray(inputs["L1_ffn_base_w"], f32).T)
    W["la1"] = _panelize(np.asarray(inputs["L1_ffn_lora_A"], f32).reshape(128, D).T)
    W["lb1"] = np.ascontiguousarray(np.asarray(inputs["L1_ffn_lora_B"], f32).transpose(0, 2, 1).reshape(128, D))
    b0 = np.ascontiguousarray(np.asarray(inputs["L0_ffn_b"], f32).reshape(32, 128).T)
    b1 = np.ascontiguousarray(np.asarray(inputs["L1_ffn_base_b"], f32).reshape(32, 128).T)

    consts = [_consts_for(0), _consts_for(1)]
    in_maps = []
    for core in range(8):
        b, half = core // 2, core % 2
        off = 1024 * half
        lo, hi = off - 256, off + 1280
        s0, s1 = max(lo, 0), min(hi, 2048)
        xl = np.zeros((LOC, D), f32)
        xl[s0 - lo:s1 - lo] = x[b, s0:s1]
        m = {"xloc": np.ascontiguousarray(xl.T), "b0": b0, "b1": b1,
             "w1qsl": np.ascontiguousarray(
                 w1qf[core * 512:(core + 1) * 512].reshape(4, 128, 4096))}
        m.update(W)
        m.update(consts[half])
        in_maps.append(m)

    nc = build_nc()
    res = run_bass_kernel_spmd(nc, in_maps, core_ids=list(range(8)))
    out = np.zeros((4, 2048, 4096), np.float32)
    for core in range(8):
        b, half = core // 2, core % 2
        out[b, 1024 * half:1024 * (half + 1)] = res.results[core]["y"].T
    return out


# version 16 (final)
# speedup vs baseline: 1.0351x; 1.0087x over previous
import numpy as np
import ml_dtypes
import concourse.bass as bass
import concourse.mybir as mybir
from concourse.bass_utils import run_bass_kernel_spmd
from contextlib import ExitStack

F32 = mybir.dt.float32
BF16 = mybir.dt.bfloat16
AF = mybir.ActivationFunctionType
ALU = mybir.AluOpType
AXX = mybir.AxisListType.X

KDEBUG = False

D = 4096
NCH = 32
TOK = 1024
LOC = 1536
H0 = 32
NKV = 8
SCALE = 1.0 / float(np.sqrt(128.0))
NEG = -1e9


class Buf:
    __slots__ = ("w", "r")

    def __init__(self):
        self.w = None
        self.r = {}

    def rd_waits(self):
        return [self.w] if self.w is not None else []

    def wr_waits(self):
        ws = dict(self.r)
        if self.w is not None:
            s, v = self.w
            ws[s] = max(ws.get(s, -1), v)
        return list(ws.items())


class Prog:
    ENGS = ("tensor", "vector", "scalar", "gpsimd", "sync")

    def __init__(self, nc):
        self.nc = nc
        self.ev = {e: [] for e in self.ENGS}
        self.sems = {}
        self.cnt = {}
        self.lastwait = {e: {} for e in self.ENGS}

    def sem(self, name):
        self.sems[name] = self.nc.alloc_semaphore(name)
        self.cnt[name] = 0

    def emit(self, eng, fns, reads=(), writes=(), waits=(), incs=()):
        if callable(fns):
            fns = [fns]
        wset = {}
        for s, v in waits:
            wset[s] = max(wset.get(s, -1), v)
        for b in reads:
            for s, v in b.rd_waits():
                wset[s] = max(wset.get(s, -1), v)
        for b in writes:
            for s, v in b.wr_waits():
                wset[s] = max(wset.get(s, -1), v)
        fw = []
        for s, v in wset.items():
            if self.lastwait[eng].get(s, -1) < v:
                fw.append((self.sems[s], v))
                self.lastwait[eng][s] = v
        marks = []
        for s, n in incs:
            self.cnt[s] += n
            marks.append((s, self.cnt[s]))
        mark = marks[0] if marks else None
        self.ev[eng].append((fns, fw, [(self.sems[s], n) for s, n in incs]))
        if mark is not None:
            for b in reads:
                b.r[mark[0]] = max(b.r.get(mark[0], -1), mark[1])
            for b in writes:
                b.w = mark
                b.r = {}
        return mark

    def join(self, eng, sems=None):
        if sems is None:
            sems = list(self.sems)
        fw = []
        for s in sems:
            v = self.cnt[s]
            if v > 0 and self.lastwait[eng].get(s, -1) < v:
                fw.append((self.sems[s], v))
                self.lastwait[eng][s] = v
        if fw:
            self.ev[eng].append(([], fw, []))

    def join_all(self, exclude=()):
        for e in self.ENGS:
            self.join(e, [s for s in self.sems if s not in exclude])

    def replay(self, block):
        def mk(engname):
            def body(e):
                for fns, fw, incs in self.ev[engname]:
                    for h, v in fw:
                        e.wait_ge(h, v)
                    ins = None
                    for f in fns:
                        ins = f(e)
                    for h, n in incs:
                        assert ins is not None
                        ins = ins.then_inc(h, n)
            return body
        block.tensor(mk("tensor"))
        block.vector(mk("vector"))
        block.scalar(mk("scalar"))
        block.gpsimd(mk("gpsimd"))
        block.sync(mk("sync"))


def _rope_cache(S=2048, Dh=128, base=10000.0):
    inv = 1.0 / base ** (np.arange(0, Dh, 2, dtype=np.float32) / Dh)
    t = np.arange(S, dtype=np.float32)
    f = np.outer(t, inv)
    cos = np.zeros((S, Dh), np.float32)
    sin = np.zeros((S, Dh), np.float32)
    cos[:, 0::2] = np.cos(f); cos[:, 1::2] = np.cos(f)
    sin[:, 0::2] = np.sin(f); sin[:, 1::2] = np.sin(f)
    return cos, sin


def _consts_for(half):
    off = 1024 * half
    cosF, sinF = _rope_cache()
    qpos = np.arange(off, off + TOK)
    kpos = np.clip(np.arange(off - 256, off + 1280), 0, 2047)
    c = {}
    c["cosq"] = np.ascontiguousarray((cosF[qpos] * SCALE).T).astype(ml_dtypes.bfloat16)
    c["sinq"] = np.ascontiguousarray((sinF[qpos] * SCALE).T).astype(ml_dtypes.bfloat16)
    c["cosk"] = np.ascontiguousarray(cosF[kpos].T).astype(ml_dtypes.bfloat16)
    c["sink"] = np.ascontiguousarray(sinF[kpos].T).astype(ml_dtypes.bfloat16)
    P = np.zeros((128, 128), np.float32)
    for d in range(64):
        P[d, d + 64] = -1.0
        P[d + 64, d] = 1.0
    c["permT"] = np.ascontiguousarray(P.T).astype(ml_dtypes.bfloat16)
    c["identb"] = np.eye(128, dtype=ml_dtypes.bfloat16)
    c["identf"] = np.eye(8, dtype=np.float32)
    mask = np.zeros((128, 12), np.float32)
    if half == 0:
        mask[:, 0] = NEG; mask[:, 1] = NEG
    else:
        mask[:, 10] = NEG; mask[:, 11] = NEG
    c["maskb"] = mask
    ramp = np.arange(256, dtype=np.float32) / 255.0
    coef = np.zeros((8, 512), np.float32)
    for w in range(5):
        if w >= 1:
            coef[w, 0:256] = ramp
        if w <= 3:
            coef[w, 256:512] = 1.0 - ramp
    if half == 0:
        coef[0, 256:512] = 0.0
        coef[1, 0:256] = 1.0
    coef2 = np.zeros((8, 1024), np.float32)
    coef2[0, 0:512] = coef[1]; coef2[1, 0:512] = coef[0]; coef2[2, 0:512] = coef[2]
    coef2[0, 512:1024] = coef[3]; coef2[1, 512:1024] = coef[4]
    c["coef"] = coef2
    ROW_OF = {1: 0, 0: 1, 2: 2, 3: 0, 4: 1}
    basis = np.zeros((128, 40), ml_dtypes.bfloat16)
    for w in range(5):
        basis[:, w * 8 + ROW_OF[w]] = 1.0
    c["basis"] = basis
    c["ones1b"] = np.ones((128, 1), ml_dtypes.bfloat16)
    c["onesbf"] = np.ones((1, 128), np.float32)
    rs = np.zeros((8, 6 * 128), ml_dtypes.bfloat16)
    for w in range(6):
        rs[w, w * 128:(w + 1) * 128] = 1.0
    c["rowselb"] = rs
    blk = np.zeros((128, 16), ml_dtypes.bfloat16)
    for oc in range(2):
        for p in range(128):
            blk[p, oc * 8 + oc * 4 + p // 32] = 1.0
    c["blocksel"] = blk
    ex1 = np.zeros((8, 128), ml_dtypes.bfloat16)
    ex2 = np.zeros((8, 128), ml_dtypes.bfloat16)
    for m in range(128):
        ex1[m // 16, m] = 1.0
        ex2[m // 16, m] = 2.0
    c["expand1"] = ex1
    c["expand2"] = ex2
    c["ones128"] = np.ones((128, 128), ml_dtypes.bfloat16)
    return c


_CONST_SHAPES = {
    "cosq": ([128, 1024], BF16), "sinq": ([128, 1024], BF16),
    "cosk": ([128, 1536], BF16), "sink": ([128, 1536], BF16),
    "permT": ([128, 128], BF16), "identb": ([128, 128], BF16),
    "identf": ([8, 8], F32), "maskb": ([128, 12], F32),
    "coef": ([8, 1024], F32), "basis": ([128, 40], BF16),
    "ones1b": ([128, 1], BF16), "rowselb": ([8, 768], BF16),
    "onesbf": ([1, 128], F32), "blocksel": ([128, 16], BF16),
    "expand1": ([8, 128], BF16), "expand2": ([8, 128], BF16),
    "ones128": ([128, 128], BF16),
}

_WEIGHT_SHAPES = {
    "w0q": [32, 128, D], "w0k": [1, 128, D], "w0v": [1, 128, D],
    "w0o": [32, 128, D], "w0f": [32, 128, D],
    "w0fq": [2, 128, D], "w0fk": [2, 128, D],
    "a0": [1, 128, D], "bm0": [128, D],
    "w1k": [8, 128, D], "w1v": [8, 128, D], "w1o": [32, 128, 1024],
    "gw1": [1, 128, 256], "bw1": [32, 128, D], "la1": [1, 128, D],
    "lb1": [128, D],
}


def build_nc():
    nc = bass.Bass("TRN2", target_bir_lowering=False)
    ap = {}
    ap["xloc"] = nc.dram_tensor("xloc", [D, LOC], F32, kind="ExternalInput").ap()
    for n, shp in _WEIGHT_SHAPES.items():
        ap[n] = nc.dram_tensor(n, shp, F32, kind="ExternalInput").ap()
    ap["b0"] = nc.dram_tensor("b0", [128, 32], F32, kind="ExternalInput").ap()
    ap["b1"] = nc.dram_tensor("b1", [128, 32], F32, kind="ExternalInput").ap()
    for n, (shp, dt) in _CONST_SHAPES.items():
        ap[n] = nc.dram_tensor(n, shp, dt, kind="ExternalInput").ap()
    y = nc.dram_tensor("y", [D, TOK], F32, kind="ExternalOutput").ap()
    dbg = {}
    if KDEBUG:
        for n, shp, dt in [("dxn", [D, TOK], BF16), ("dao", [D, TOK], BF16),
                           ("dx1", [D, TOK], F32), ("dxn1", [D, TOK], BF16),
                           ("dqs", [1024, TOK], BF16), ("dval", [1024, TOK], BF16),
                           ("dhw", [128, TOK], BF16), ("dhw1", [128, TOK], BF16),
                           ("dkr", [128, LOC], BF16)]:
            dbg[n] = nc.dram_tensor(n, shp, dt, kind="ExternalOutput").ap()
    ap["w1qsl"] = nc.dram_tensor("w1qsl", [4, 128, D], F32, kind="ExternalInput").ap()
    x1d = nc.dram_tensor("x1d", [D, TOK], F32).ap()
    x1b = nc.dram_tensor("x1b", [D, TOK], BF16).ap()
    kvs = nc.dram_tensor("kvs", [2048, TOK], BF16).ap()
    kva = nc.dram_tensor("kva", [2, 2048, TOK], BF16).ap()
    wqpart = nc.dram_tensor("wqpart", [8, 128, 4, 128], BF16).ap()
    wqall = nc.dram_tensor("wqall", [8, 8, 128, 4, 128], BF16, addr_space="Shared").ap()

    st = ExitStack()
    with st:
        xn = st.enter_context(nc.sbuf_tensor("xn", [128, NCH * TOK], BF16))
        big2 = st.enter_context(nc.sbuf_tensor("big2", [128, NCH * TOK], BF16))
        wp = [st.enter_context(nc.sbuf_tensor(f"wp{i}", [128, 4096], BF16)) for i in range(4)]
        hw = st.enter_context(nc.sbuf_tensor("hw", [128, 1024], BF16))
        permT = st.enter_context(nc.sbuf_tensor("permT_s", [128, 128], BF16))
        identb = st.enter_context(nc.sbuf_tensor("identb_s", [128, 128], BF16))
        identf = st.enter_context(nc.sbuf_tensor("identf_s", [8, 8], F32))
        maskb = st.enter_context(nc.sbuf_tensor("maskb_s", [128, 12], F32))
        basis = st.enter_context(nc.sbuf_tensor("basis_s", [128, 40], BF16))
        ones1b = st.enter_context(nc.sbuf_tensor("ones1b_s", [128, 1], BF16))
        rowselb = st.enter_context(nc.sbuf_tensor("rowselb_s", [8, 768], BF16))
        onesbf = st.enter_context(nc.sbuf_tensor("onesbf_s", [1, 128], F32))
        blocksel = st.enter_context(nc.sbuf_tensor("blocksel_s", [128, 16], BF16))
        expand1 = st.enter_context(nc.sbuf_tensor("expand1_s", [8, 128], BF16))
        expand2 = st.enter_context(nc.sbuf_tensor("expand2_s", [8, 128], BF16))
        ones128 = st.enter_context(nc.sbuf_tensor("ones128_s", [128, 128], BF16))
        b0sb = st.enter_context(nc.sbuf_tensor("b0sb", [128, 32], F32))
        b1sb = st.enter_context(nc.sbuf_tensor("b1sb", [128, 32], F32))

        prog = Prog(nc)
        for s in ("te", "ve", "sc", "gp", "cc"):
            prog.sem(s)
        NLANES = {"ds": 8, "dg": 6}
        for q, nl in NLANES.items():
            for i in range(nl):
                prog.sem(f"{q}{i}")
        _rr = {"ds": 0, "dg": 0}
        P = prog

        B_xn = [Buf() for _ in range(NCH)]
        B_big = [Buf() for _ in range(NCH)]
        B_wp = [Buf() for _ in range(4)]
        B_hw = Buf()
        B_const = Buf()
        B_x1d = [Buf() for _ in range(NCH)]
        B_x1b = [Buf() for _ in range(NCH)]
        B_kvs = Buf()
        B_kva = Buf()
        B_y = Buf()

        def dma(eng, q, out, in_, reads=(), writes=(), waits=(), **kw):
            lane = f"{q}{_rr[q]}"
            _rr[q] = (_rr[q] + 1) % NLANES[q]
            w = list(waits) + [(lane, P.cnt[lane])]
            return P.emit(eng, lambda e: e.dma_start(out=out, in_=in_, **kw),
                          reads=reads, writes=writes, waits=w, incs=[(lane, 16)])

        for sb, n in [(permT, "permT"), (identb, "identb"), (identf, "identf"),
                      (maskb, "maskb"), (basis, "basis"), (ones1b, "ones1b"),
                      (rowselb, "rowselb"), (onesbf, "onesbf"), (blocksel, "blocksel"),
                      (expand1, "expand1"), (expand2, "expand2"), (ones128, "ones128"),
                      (b0sb, "b0"), (b1sb, "b1")]:
            dma("sync", "ds", sb[:], ap[n], writes=[B_const])

        def load_panel(wname, pan, ncol, slot, nch=NCH):
            return dma("gpsimd", "dg", wp[slot][:, 0:nch * ncol], ap[wname][pan],
                       writes=[B_wp[slot]])

        def emit_stats_finalize(B_stat, sx_ps, sx2_ps, B_sxps, B_sx2ps,
                                rstdb, negmrb, B_rstdb, B_negmrb, ncols, bc_ps, B_bcps):
            inv = 1.0 / float(D)
            mrow = negmrb[0:1, 0:ncols]
            scr = rstdb[0:1, 0:ncols]
            P.emit("vector", lambda e: e.tensor_scalar(out=mrow, in0=sx_ps[0:1, 0:ncols],
                   scalar1=inv, scalar2=None, op0=ALU.mult),
                   reads=[B_sxps], writes=[B_stat], incs=[("ve", 1)])
            P.emit("vector", lambda e: e.tensor_tensor(out=scr, in0=mrow, in1=mrow, op=ALU.mult),
                   reads=[B_stat], writes=[B_stat], incs=[("ve", 1)])
            P.emit("vector", lambda e: e.scalar_tensor_tensor(out=scr, in0=sx2_ps[0:1, 0:ncols],
                   scalar=inv, in1=scr, op0=ALU.mult, op1=ALU.subtract),
                   reads=[B_sx2ps, B_stat], writes=[B_stat], incs=[("ve", 1)])
            P.emit("vector", lambda e: e.tensor_scalar(out=scr, in0=scr,
                   scalar1=1e-5, scalar2=None, op0=ALU.add),
                   reads=[B_stat], writes=[B_stat], incs=[("ve", 1)])
            P.emit("scalar", lambda e: e.activation(out=scr, in_=scr, func=AF.Sqrt),
                   reads=[B_stat], writes=[B_stat], incs=[("sc", 1)])
            P.emit("vector", lambda e: e.reciprocal(out=scr, in_=scr),
                   reads=[B_stat], writes=[B_stat], incs=[("ve", 1)])
            P.emit("vector", lambda e: e.tensor_tensor(out=mrow, in0=mrow, in1=scr, op=ALU.mult),
                   reads=[B_stat], writes=[B_stat], incs=[("ve", 1)])
            P.emit("vector", lambda e: e.tensor_scalar(out=mrow, in0=mrow,
                   scalar1=-1.0, scalar2=None, op0=ALU.mult),
                   reads=[B_stat], writes=[B_stat], incs=[("ve", 1)])
            for (srcr, dst, Bd) in ((mrow, negmrb, B_negmrb), (scr, rstdb, B_rstdb)):
                for s in range(0, ncols, 512):
                    n = min(512, ncols - s)
                    P.emit("tensor", lambda e, srcr=srcr, s=s, n=n: e.matmul(
                        bc_ps[:, 0:n], onesbf[:], srcr[0:1, s:s + n],
                        start=True, stop=True),
                        reads=[B_stat, B_const], writes=[B_bcps], incs=[("te", 1)])
                    P.emit("scalar", lambda e, dst=dst, s=s, n=n: e.copy(
                        out=dst[:, s:s + n], in_=bc_ps[:, 0:n]),
                        reads=[B_bcps, B_stat], writes=[Bd], incs=[("sc", 1)])

        xb = big2[:, 0:1536]
        sqv = big2[:, 1536:3072]
        coskb = big2[:, 3072:4608]
        sinkb = big2[:, 4608:6144]
        kraw = big2[:, 6144:7680]
        xnh = big2[:, 7680:8192]
        B_xb, B_sq, B_ck, B_sk, B_kraw, B_xnh = Buf(), Buf(), Buf(), Buf(), Buf(), Buf()

        st23 = ExitStack()
        Kr = st23.enter_context(nc.sbuf_tensor("Kr", [128, LOC], BF16))
        V0 = st23.enter_context(nc.sbuf_tensor("V0", [128, LOC], BF16))
        B_Kr, B_V0 = Buf(), Buf()

        st12 = ExitStack()
        xf = [st12.enter_context(nc.sbuf_tensor(f"xf{i}", [128, LOC], F32)) for i in range(2)]
        rstdb = st12.enter_context(nc.sbuf_tensor("rstdb", [128, LOC], F32))
        negmrb = st12.enter_context(nc.sbuf_tensor("negmrb", [128, LOC], F32))
        xscr = st12.enter_context(nc.sbuf_tensor("xscr", [128, LOC], F32))
        xscrb = st12.enter_context(nc.sbuf_tensor("xscrb", [128, 1024], F32))
        B_xf = [Buf(), Buf()]
        B_stat, B_rstdb, B_negmrb, B_xscr = Buf(), Buf(), Buf(), Buf()

        ps12 = ExitStack()
        sx_ps = ps12.enter_context(nc.psum_tensor("sx_ps", [1, LOC], F32))
        sx2_ps = ps12.enter_context(nc.psum_tensor("sx2_ps", [1, LOC], F32))
        bc_ps = ps12.enter_context(nc.psum_tensor("bc_ps", [128, 512], F32))
        B_sxps, B_sx2ps, B_bcps = Buf(), Buf(), Buf()

        xloc_t = ap["xloc"].rearrange("(c p) t -> c p t", p=128)

        for c in range(NCH):
            p = c % 2
            dma("sync", "ds", xf[p][:], xloc_t[c], writes=[B_xf[p]])
            P.emit("vector", lambda e, p=p: e.tensor_copy(out=xb, in_=xf[p][:]),
                   reads=[B_xf[p]], writes=[B_xb], incs=[("ve", 1)])
            P.emit("scalar", lambda e, c=c: e.copy(out=xn[:, c * TOK:(c + 1) * TOK],
                   in_=xb[:, 256:1280]),
                   reads=[B_xb], writes=[B_xn[c]], incs=[("sc", 1)])
            P.emit("vector", lambda e: e.tensor_tensor(out=sqv, in0=xb, in1=xb, op=ALU.mult),
                   reads=[B_xb], writes=[B_sq], incs=[("ve", 1)])
            for (src, Bs, pst, Bp) in ((xb, B_xb, sx_ps, B_sxps), (sqv, B_sq, sx2_ps, B_sx2ps)):
                fns = [lambda e, src=src, pst=pst, s=s, c=c: e.matmul(
                    pst[0:1, s * 512:(s + 1) * 512], ones1b[:], src[:, s * 512:(s + 1) * 512],
                    start=(c == 0), stop=(c == NCH - 1)) for s in range(3)]
                P.emit("tensor", fns, reads=[Bs, B_const], writes=[Bp], incs=[("te", 1)])

        emit_stats_finalize(B_stat, sx_ps, sx2_ps, B_sxps, B_sx2ps,
                            rstdb, negmrb, B_rstdb, B_negmrb, LOC, bc_ps, B_bcps)
        P.join_all()
        ps12.close()

        psKV = ExitStack()
        k_ps = psKV.enter_context(nc.psum_tensor("k_ps", [128, LOC], F32))
        v_ps = psKV.enter_context(nc.psum_tensor("v_ps", [128, LOC], F32))
        pkp = psKV.enter_context(nc.psum_tensor("pkp", [128, 512], F32))
        vt_ps = psKV.enter_context(nc.psum_tensor("vt_ps", [128, 768], BF16))
        B_kps, B_vps, B_pkp, B_vtps = Buf(), Buf(), Buf(), Buf()

        load_panel("w0k", 0, 128, 2)
        load_panel("w0v", 0, 128, 3)
        load_panel("w0q", 0, 128, 0)

        B_xfa = [Buf(), Buf()]
        B_xfb = [Buf(), Buf()]
        B_xso, B_xsob, B_xsch = Buf(), Buf(), Buf()
        for c in range(NCH):
            p = c % 2
            dma("sync", "ds", xf[p][:, 0:256], xloc_t[c][:, 0:256], writes=[B_xfa[p]])
            dma("sync", "ds", xf[p][:, 256:512], xloc_t[c][:, 1280:1536], writes=[B_xfb[p]])
            eng, sem = ("vector", "ve") if c % 2 == 0 else ("gpsimd", "gp")
            so, Bso = (xscr[:, 0:1024], B_xso) if c % 2 == 0 else (xscrb[:], B_xsob)
            P.emit(eng, lambda e, c=c, so=so: e.tensor_tensor(
                   out=so, in0=xn[:, c * TOK:(c + 1) * TOK],
                   in1=rstdb[:, 256:1280], op=ALU.mult),
                   reads=[B_xn[c], B_rstdb], writes=[Bso], incs=[(sem, 1)])
            P.emit(eng, lambda e, c=c, so=so: e.tensor_tensor(
                   out=xn[:, c * TOK:(c + 1) * TOK], in0=so,
                   in1=negmrb[:, 256:1280], op=ALU.add),
                   reads=[Bso, B_negmrb], writes=[B_xn[c]], incs=[(sem, 1)])
            P.emit("vector", lambda e, p=p: e.tensor_tensor(out=xscr[:, 1024:1280],
                   in0=xf[p][:, 0:256], in1=rstdb[:, 0:256], op=ALU.mult),
                   reads=[B_xfa[p], B_rstdb], writes=[B_xsch], incs=[("ve", 1)])
            P.emit("vector", lambda e: e.tensor_tensor(out=xnh[:, 0:256],
                   in0=xscr[:, 1024:1280], in1=negmrb[:, 0:256], op=ALU.add),
                   reads=[B_xsch, B_negmrb], writes=[B_xnh], incs=[("ve", 1)])
            P.emit("vector", lambda e, p=p: e.tensor_tensor(out=xscr[:, 1280:1536],
                   in0=xf[p][:, 256:512], in1=rstdb[:, 1280:1536], op=ALU.mult),
                   reads=[B_xfb[p], B_rstdb], writes=[B_xsch], incs=[("ve", 1)])
            P.emit("vector", lambda e: e.tensor_tensor(out=xnh[:, 256:512],
                   in0=xscr[:, 1280:1536], in1=negmrb[:, 1280:1536], op=ALU.add),
                   reads=[B_xsch, B_negmrb], writes=[B_xnh], incs=[("ve", 1)])
            segs = [(0, 256, "h", 0, True, False), (256, 512, "o", 0, False, True),
                    (512, 1024, "o", 256, True, True),
                    (1024, 1280, "o", 768, True, False), (1280, 1536, "h", 256, False, True)]
            for (pst, Bp, slot) in ((k_ps, B_kps, 2), (v_ps, B_vps, 3)):
                fns = []
                for (a, b, kind, rc, bf, bl) in segs:
                    if kind == "o":
                        rhs = xn[:, c * TOK + rc: c * TOK + rc + (b - a)]
                    else:
                        rhs = xnh[:, rc: rc + (b - a)]
                    fns.append(lambda e, pst=pst, a=a, b=b, rhs=rhs, slot=slot, c=c,
                               bf=bf, bl=bl: e.matmul(
                        pst[:, a:b], wp[slot][:, c * 128:(c + 1) * 128], rhs,
                        start=(c == 0 and bf), stop=(c == NCH - 1 and bl)))
                P.emit("tensor", fns, reads=[B_xn[c], B_xnh, B_wp[slot], B_const],
                       writes=[Bp], incs=[("te", 1)])

        dma("sync", "ds", coskb, ap["cosk"], writes=[B_ck])
        dma("sync", "ds", sinkb, ap["sink"], writes=[B_sk])
        P.emit("scalar", lambda e: e.copy(out=kraw, in_=k_ps[:]),
               reads=[B_kps], writes=[B_kraw], incs=[("sc", 1)])
        P.emit("vector", lambda e: e.tensor_tensor(out=Kr[:], in0=kraw, in1=coskb, op=ALU.mult),
               reads=[B_kraw, B_ck], writes=[B_Kr], incs=[("ve", 1)])
        for s in range(3):
            P.emit("tensor", lambda e, s=s: e.matmul(pkp[:], permT[:],
                   kraw[:, s * 512:(s + 1) * 512], start=True, stop=True),
                   reads=[B_kraw, B_const], writes=[B_pkp], incs=[("te", 1)])
            P.emit("vector", lambda e, s=s: e.tensor_tensor(out=xb[:, 0:512], in0=pkp[:],
                   in1=sinkb[:, s * 512:(s + 1) * 512], op=ALU.mult),
                   reads=[B_pkp, B_sk], writes=[B_xb], incs=[("ve", 1)])
            P.emit("vector", lambda e, s=s: e.tensor_tensor(out=Kr[:, s * 512:(s + 1) * 512],
                   in0=Kr[:, s * 512:(s + 1) * 512], in1=xb[:, 0:512], op=ALU.add),
                   reads=[B_Kr, B_xb], writes=[B_Kr], incs=[("ve", 1)])
        P.emit("scalar", lambda e: e.copy(out=kraw, in_=v_ps[:]),
               reads=[B_vps], writes=[B_kraw], incs=[("sc", 1)])
        for g in range(2):
            fns = [lambda e, t=t, g=g: e.transpose(vt_ps[:, t * 128:(t + 1) * 128],
                   kraw[:, (g * 6 + t) * 128:(g * 6 + t + 1) * 128], identb[:]) for t in range(6)]
            P.emit("tensor", fns, reads=[B_kraw, B_const], writes=[B_vtps], incs=[("te", 1)])
            P.emit("vector", lambda e, g=g: e.tensor_copy(out=V0[:, g * 768:(g + 1) * 768],
                   in_=vt_ps[:]), reads=[B_vtps], writes=[B_V0], incs=[("ve", 1)])

        P.join_all()
        psKV.close()
        st12.close()

        B_wqall = Buf()
        B_wqpart = [Buf() for _ in range(4)]
        for i in range(4):
            a0 = (24 + 4 * (i % 2)) * TOK
            Av = big2[:, a0: a0 + 4096].rearrange("p (h x) -> p h x", x=512)
            bbufs = [B_big[24 + 4 * (i % 2) + k] for k in range(4)]
            dma("gpsimd", "dg", big2[:, a0: a0 + 4096], ap["w1qsl"][i], writes=bbufs)
            P.emit("vector", lambda e, Av=Av: e.tensor_tensor(
                out=Av[:, :, 0:128], in0=Av[:, :, 0:128], in1=Av[:, :, 128:256], op=ALU.add),
                reads=bbufs, writes=bbufs, incs=[("ve", 1)])
            P.emit("vector", lambda e, Av=Av: e.tensor_tensor(
                out=Av[:, :, 256:384], in0=Av[:, :, 256:384], in1=Av[:, :, 384:512], op=ALU.add),
                reads=bbufs, writes=bbufs, incs=[("ve", 1)])
            P.emit("vector", lambda e, Av=Av: e.tensor_tensor(
                out=Av[:, :, 128:256], in0=Av[:, :, 0:128], in1=Av[:, :, 256:384], op=ALU.add),
                reads=bbufs, writes=bbufs, incs=[("ve", 1)])
            dma("sync", "ds", wqpart[:, :, i, :].rearrange("h p j -> p h j"),
                Av[:, :, 128:256], reads=bbufs, writes=[B_wqpart[i]])
        P.emit("gpsimd", lambda e: e.collective_compute(
            "AllGather", ALU.bypass, replica_groups=[[0, 1, 2, 3, 4, 5, 6, 7]],
            ins=[wqpart], outs=[wqall]), reads=B_wqpart, writes=[B_wqall],
            incs=[("cc", 1)])

        if KDEBUG:
            for c in range(NCH):
                dma("sync", "ds", dbg["dxn"].rearrange("(c p) t -> c p t", p=128)[c],
                    xn[:, c * TOK:(c + 1) * TOK], reads=[B_xn[c]])
            dma("sync", "ds", dbg["dkr"], Kr[:], reads=[B_Kr])

        st3 = ExitStack()
        qsb = st3.enter_context(nc.sbuf_tensor("qsb", [128, 1024], BF16))
        qr2 = [st3.enter_context(nc.sbuf_tensor(f"qr{i}", [128, 1024], BF16)) for i in range(2)]
        qtmp = st3.enter_context(nc.sbuf_tensor("qtmp", [128, 1024], BF16))
        rtmp = st3.enter_context(nc.sbuf_tensor("rtmp", [128, 1024], BF16))
        sig = st3.enter_context(nc.sbuf_tensor("sig", [128, 4096], BF16))
        csb = st3.enter_context(nc.sbuf_tensor("csb", [128, 512], BF16))
        cosq = st3.enter_context(nc.sbuf_tensor("cosq_s", [128, 1024], BF16))
        sinq = st3.enter_context(nc.sbuf_tensor("sinq_s", [128, 1024], BF16))
        coef = st3.enter_context(nc.sbuf_tensor("coef_s", [8, 1024], F32))
        csr = st3.enter_context(nc.sbuf_tensor("csr", [8, 512], F32))
        cs = st3.enter_context(nc.sbuf_tensor("cs", [8, 1024], BF16))
        B_qsb, B_qtmp, B_rtmp, B_sig, B_csb = Buf(), Buf(), Buf(), Buf(), Buf()
        B_qr2 = [Buf(), Buf()]
        B_csr = Buf()
        B_csA, B_csB = Buf(), Buf()
        dma("sync", "ds", cosq[:], ap["cosq"], writes=[B_const])
        dma("sync", "ds", sinq[:], ap["sinq"], writes=[B_const])
        dma("sync", "ds", coef[:], ap["coef"], writes=[B_const])

        ps3 = ExitStack()
        scp = ps3.enter_context(nc.psum_tensor("scp", [128, 512], F32))
        qg = ps3.enter_context(nc.psum_tensor("qg", [128, 512], F32))
        wob = [ps3.enter_context(nc.psum_tensor(f"wob{i}", [128, 512], F32)) for i in range(3)]
        denA = ps3.enter_context(nc.psum_tensor("denA", [8, 512], F32))
        denB = ps3.enter_context(nc.psum_tensor("denB", [8, 512], F32))
        csb_ps = ps3.enter_context(nc.psum_tensor("csb_ps", [128, 512], F32))
        B_scp, B_qg = Buf(), Buf()
        B_wob = [Buf() for _ in range(3)]
        B_denA, B_denB, B_csbps = Buf(), Buf(), Buf()

        WGEO = [(0, 256, 256), (0, 512, 0), (256, 512, 0), (512, 512, 0), (768, 256, 0)]
        WBANK = {1: 0, 0: 1, 2: 2, 3: 0, 4: 1}
        ROW_OF = {1: 0, 0: 1, 2: 2, 3: 0, 4: 1}
        DGRP = {1: 0, 0: 0, 2: 0, 3: 1, 4: 1}

        def emit_qgemm_filler(h):
            if h >= H0:
                return []
            slot = h % 2
            par = h % 2
            th = []
            if h + 1 < H0:
                th.append(lambda h=h: load_panel("w0q", h + 1, 128, (h + 1) % 2))
            for t in range(2):
                for g in range(8):
                    fns = []
                    for c in range(g * 4, g * 4 + 4):
                        fns.append(lambda e, c=c, t=t, slot=slot: e.matmul(
                            qg[:], wp[slot][:, c * 128:(c + 1) * 128],
                            xn[:, c * TOK + t * 512: c * TOK + (t + 1) * 512],
                            start=(c == 0), stop=(c == NCH - 1)))
                    th.append(lambda fns=fns, slot=slot: P.emit(
                        "tensor", fns, reads=[B_wp[slot], B_const] + B_xn,
                        writes=[B_qg], incs=[("te", 1)]))
                th.append(lambda t=t: P.emit("scalar",
                    lambda e, t=t: e.copy(out=qsb[:, t * 512:(t + 1) * 512], in_=qg[:]),
                    reads=[B_qg], writes=[B_qsb], incs=[("sc", 1)]))
            for t in range(2):
                th.append(lambda t=t: P.emit("tensor",
                    lambda e, t=t: e.matmul(qg[:], permT[:], qsb[:, t * 512:(t + 1) * 512],
                                            start=True, stop=True),
                    reads=[B_qsb, B_const], writes=[B_qg], incs=[("te", 1)]))
                th.append(lambda t=t, par=par: P.emit("vector",
                    lambda e, t=t, par=par: e.tensor_tensor(
                        out=qr2[par][:, t * 512:(t + 1) * 512], in0=qg[:],
                        in1=sinq[:, t * 512:(t + 1) * 512], op=ALU.mult),
                    reads=[B_qg, B_const], writes=[B_qr2[par]], incs=[("ve", 1)]))
            th.append(lambda par=par: P.emit("vector",
                lambda e: e.tensor_tensor(out=rtmp[:], in0=qsb[:], in1=cosq[:], op=ALU.mult),
                reads=[B_qsb, B_const], writes=[B_rtmp], incs=[("ve", 1)]))
            th.append(lambda par=par: P.emit("vector",
                lambda e, par=par: e.tensor_tensor(out=qr2[par][:], in0=qr2[par][:],
                                                   in1=rtmp[:], op=ALU.add),
                reads=[B_rtmp], writes=[B_qr2[par]], incs=[("ve", 1)]))
            return th

        def emit_unit(h, w, j, with_av, fill=None):
            par = h % 2
            qa, qn, g0 = WGEO[w]
            kc = 2 * w + j
            slotj = j if w not in (4, 0) else 4 + j
            dgrp = DGRP[w]
            dps, Bd = (denA, B_denA) if dgrp == 0 else (denB, B_denB)
            P.emit("tensor", lambda e, kc=kc, qa=qa, qn=qn, g0=g0, par=par: e.matmul(
                scp[:, g0:g0 + qn], Kr[:, kc * 128:(kc + 1) * 128],
                qr2[par][:, qa:qa + qn], start=True, stop=True),
                reads=[B_Kr, B_qr2[par]], writes=[B_scp], incs=[("te", 1)])
            P.emit("scalar", lambda e, slotj=slotj, kc=kc, g0=g0, qn=qn: e.activation(
                out=sig[:, slotj * 512 + g0: slotj * 512 + g0 + qn], in_=scp[:, g0:g0 + qn],
                func=AF.Sigmoid, bias=maskb[:, kc:kc + 1], scale=1.0),
                reads=[B_scp, B_const], writes=[B_sig], incs=[("sc", 1)])
            if fill is not None:
                fill(1)
            P.emit("tensor", lambda e, w=w, j=j, g0=g0, qn=qn, slotj=slotj, dps=dps: e.matmul(
                dps[:, g0:g0 + qn], basis[:, w * 8:(w + 1) * 8],
                sig[:, slotj * 512 + g0: slotj * 512 + g0 + qn],
                start=(w in (1, 3) and j == 0), stop=(w in (2, 4) and j == 3)),
                reads=[B_sig, B_const], writes=[Bd], incs=[("te", 1)])
            if with_av:
                emit_av(h, w, j)

        def emit_av(h, w, j):
            qa, qn, g0 = WGEO[w]
            kc = 2 * w + j
            slotj = j if w not in (4, 0) else 4 + j
            bk = WBANK[w]
            P.emit("tensor", lambda e, w=w, j=j, kc=kc, g0=g0, qn=qn, slotj=slotj, bk=bk: e.matmul(
                wob[bk][:, g0:g0 + qn], V0[:, kc * 128:(kc + 1) * 128],
                sig[:, slotj * 512 + g0: slotj * 512 + g0 + qn], start=(j == 0), stop=(j == 3)),
                reads=[B_sig, B_V0], writes=[B_wob[bk]], incs=[("te", 1)])

        def emit_recip(grp):
            dps, Bd = (denA, B_denA) if grp == 0 else (denB, B_denB)
            Bcs = B_csA if grp == 0 else B_csB
            o = grp * 512
            P.emit("vector", lambda e, dps=dps: e.tensor_scalar(out=csr[:], in0=dps[:],
                   scalar1=1e-30, scalar2=None, op0=ALU.max),
                   reads=[Bd], writes=[B_csr], incs=[("ve", 1)])
            P.emit("vector", lambda e: e.reciprocal(out=csr[:], in_=csr[:]),
                   reads=[B_csr], writes=[B_csr], incs=[("ve", 1)])
            P.emit("vector", lambda e, o=o: e.tensor_tensor(out=cs[:, o:o + 512],
                   in0=coef[:, o:o + 512], in1=csr[:], op=ALU.mult),
                   reads=[B_csr, B_const], writes=[Bcs], incs=[("ve", 1)])

        BROW = {0: ((0, 1), (0, 0)), 1: ((0, 0), (0, 2)),
                2: ((0, 2), (1, 0)), 3: ((1, 0), (1, 1))}

        def emit_blend(h, j):
            (fp, fr), (rp, rr) = BROW[j]
            Bf = B_csA if fp == 0 else B_csB
            Br = B_csA if rp == 0 else B_csB
            P.emit("tensor", [
                lambda e, fp=fp, fr=fr: e.matmul(csb_ps[:, 0:256],
                    rowselb[:, fr * 128:(fr + 1) * 128],
                    cs[0:8, fp * 512 + 256: fp * 512 + 512], start=True, stop=True),
                lambda e, rp=rp, rr=rr: e.matmul(csb_ps[:, 256:512],
                    rowselb[:, rr * 128:(rr + 1) * 128],
                    cs[0:8, rp * 512: rp * 512 + 256], start=True, stop=True),
            ], reads=[Bf, Br, B_const], writes=[B_csbps], incs=[("te", 1)])
            P.emit("scalar", lambda e: e.copy(out=csb[:], in_=csb_ps[:]),
                   reads=[B_csbps], writes=[B_csb], incs=[("sc", 1)])
            bf, br = WBANK[j], WBANK[j + 1]
            P.emit("vector", lambda e, bf=bf: e.tensor_tensor(out=qtmp[:, 0:256],
                   in0=wob[bf][:, 256:512], in1=csb[:, 0:256], op=ALU.mult),
                   reads=[B_wob[bf], B_csb], writes=[B_qtmp], incs=[("ve", 1)])
            P.emit("vector", lambda e, br=br: e.tensor_tensor(out=qtmp[:, 256:512],
                   in0=wob[br][:, 0:256], in1=csb[:, 256:512], op=ALU.mult),
                   reads=[B_wob[br], B_csb], writes=[B_qtmp], incs=[("ve", 1)])
            P.emit("vector", lambda e, j=j, h=h: e.tensor_tensor(
                   out=big2[:, h * TOK + j * 256: h * TOK + (j + 1) * 256],
                   in0=qtmp[:, 0:256], in1=qtmp[:, 256:512], op=ALU.add),
                   reads=[B_qtmp], writes=[B_big[h]], incs=[("ve", 1)])

        for th in emit_qgemm_filler(0):
            th()
        for h in range(H0):
            filler = list(emit_qgemm_filler(h + 1))
            fi = 0

            def fill(n=1):
                nonlocal fi
                for _ in range(n):
                    if fi < len(filler):
                        filler[fi]()
                        fi += 1

            for j in range(4):
                emit_unit(h, 1, j, with_av=False, fill=fill)
            if h > 0:
                emit_blend(h - 1, 2)
                fill(1)
                emit_blend(h - 1, 3)
                fill(1)
            for j in range(4):
                emit_av(h, 1, j)
            fill(1)
            for w in (0, 2):
                for j in range(4):
                    emit_unit(h, w, j, with_av=True, fill=fill)
            emit_recip(0)
            for j in range(4):
                emit_unit(h, 3, j, with_av=False, fill=fill)
            emit_blend(h, 0)
            fill(1)
            emit_blend(h, 1)
            fill(1)
            for j in range(4):
                emit_av(h, 3, j)
            fill(1)
            for j in range(4):
                emit_unit(h, 4, j, with_av=False, fill=fill)
            emit_recip(1)
            for j in range(4):
                emit_av(h, 4, j)
            fill(len(filler))
            if h == H0 - 1:
                emit_blend(h, 2)
                emit_blend(h, 3)

        P.join_all()
        ps3.close()
        st3.close()
        st23.close()

        if KDEBUG:
            for c in range(NCH):
                dma("sync", "ds", dbg["dao"].rearrange("(c p) t -> c p t", p=128)[c],
                    big2[:, c * TOK:(c + 1) * TOK], reads=[B_big[c]])

        def emit_gate_lora(layer):
            stg = ExitStack()
            g1 = stg.enter_context(nc.sbuf_tensor(f"g1_{layer}", [128, 2048], BF16))
            lg = stg.enter_context(nc.sbuf_tensor(f"lg_{layer}", [8, 1024], F32))
            lgT = stg.enter_context(nc.sbuf_tensor(f"lgT_{layer}", [128, 64], F32))
            tA = stg.enter_context(nc.sbuf_tensor(f"tA_{layer}", [128, 64], F32))
            tB = stg.enter_context(nc.sbuf_tensor(f"tB_{layer}", [128, 64], F32))
            tC = stg.enter_context(nc.sbuf_tensor(f"tC_{layer}", [128, 8], F32))
            tD = stg.enter_context(nc.sbuf_tensor(f"tD_{layer}", [128, 8], F32))
            wtk = stg.enter_context(nc.sbuf_tensor(f"wtk_{layer}", [128, 64], BF16))
            wfe = stg.enter_context(nc.sbuf_tensor(f"wfe_{layer}", [8, 1024], BF16))
            B_g1, B_lg, B_lgT = Buf(), Buf(), Buf()
            B_tA, B_tB, B_tC, B_tD, B_wtk, B_wfe = Buf(), Buf(), Buf(), Buf(), Buf(), Buf()

            psg = ExitStack()
            pg1 = psg.enter_context(nc.psum_tensor(f"pg1_{layer}", [128, 1024], F32))
            pg2 = psg.enter_context(nc.psum_tensor(f"pg2_{layer}", [128, 1024], F32))
            psc = psg.enter_context(nc.psum_tensor(f"psc_{layer}", [8, 1024], F32))
            B_psc = Buf()
            ptr = psg.enter_context(nc.psum_tensor(f"ptr_{layer}", [128, 64], F32))
            pwf = psg.enter_context(nc.psum_tensor(f"pwf_{layer}", [8, 1024], BF16))
            B_pg1, B_pg2, B_ptr, B_pwf = Buf(), Buf(), Buf(), Buf()

            def gemm(pst, Bp, slot, ncol=128, extra=None):
                fns = []
                for c in range(NCH):
                    for t in range(2):
                        fns.append(lambda e, c=c, t=t, slot=slot, ncol=ncol, pst=pst: e.matmul(
                            pst[:, t * 512:(t + 1) * 512] if ncol == 128 else
                            pst[0:ncol, t * 512:(t + 1) * 512],
                            wp[slot][:, c * ncol:(c + 1) * ncol],
                            xn[:, c * TOK + t * 512: c * TOK + (t + 1) * 512],
                            start=(c == 0), stop=(c == NCH - 1)))
                P.emit("tensor", fns, reads=[B_wp[slot], B_const] + B_xn,
                       writes=[Bp], incs=[("te", 1)])

            if layer == 0:
                for oc in range(2):
                    load_panel("w0fq", oc, 128, 0)
                    gemm(pg1, B_pg1, 0)
                    P.emit("scalar", lambda e: e.copy(out=g1[:, 0:1024], in_=pg1[:]),
                           reads=[B_pg1], writes=[B_g1], incs=[("sc", 1)])
                    load_panel("w0fk", oc, 128, 1)
                    gemm(pg2, B_pg2, 1)
                    P.emit("vector", lambda e: e.tensor_tensor(out=g1[:, 1024:2048],
                           in0=pg2[:], in1=g1[:, 0:1024], op=ALU.mult),
                           reads=[B_pg2, B_g1], writes=[B_g1], incs=[("ve", 1)])
                    fns = [lambda e, t=t, oc=oc: e.matmul(psc[0:8, t * 512:(t + 1) * 512],
                           blocksel[:, oc * 8:(oc + 1) * 8],
                           g1[:, 1024 + t * 512:1024 + (t + 1) * 512],
                           start=(oc == 0), stop=(oc == 1)) for t in range(2)]
                    P.emit("tensor", fns, reads=[B_g1, B_const], writes=[B_psc], incs=[("te", 1)])
                P.emit("scalar", lambda e: e.copy(out=lg[:], in_=psc[0:8, :]),
                       reads=[B_psc], writes=[B_lg], incs=[("sc", 1)])
            else:
                load_panel("gw1", 0, 8, 0)
                gemm(pg1, B_pg1, 0, ncol=8)
                P.emit("scalar", lambda e: e.copy(out=lg[:], in_=pg1[0:8, :]),
                       reads=[B_pg1], writes=[B_lg], incs=[("sc", 1)])

            fns = [lambda e, tc=tc: e.transpose(ptr[:, tc * 8:(tc + 1) * 8],
                   lg[0:8, tc * 128:(tc + 1) * 128], identf[:]) for tc in range(8)]
            P.emit("tensor", fns, reads=[B_lg, B_const], writes=[B_ptr], incs=[("te", 1)])
            P.emit("vector", lambda e: e.tensor_copy(out=lgT[:], in_=ptr[:]),
                   reads=[B_ptr], writes=[B_lgT], incs=[("ve", 1)])
            v3 = lgT[:].rearrange("p (a b) -> p a b", a=8)
            P.emit("vector", lambda e: e.reduce_max(out=tC[:], in_=v3, axis=AXX),
                   reads=[B_lgT], writes=[B_tC], incs=[("ve", 1)])
            if layer == 0:
                P.emit("vector", lambda e: e.tensor_tensor(out=tA[:], in0=lgT[:],
                       in1=tC[:].rearrange("p (a b) -> p a b", b=1).broadcast_to([128, 8, 8]), op=ALU.subtract),
                       reads=[B_lgT, B_tC], writes=[B_tA], incs=[("ve", 1)])
                P.emit("scalar", lambda e: e.activation(out=tA[:], in_=tA[:], func=AF.Exp,
                       scale=float(1.0 / np.sqrt(32.0))),
                       reads=[B_tA], writes=[B_tA], incs=[("sc", 1)])
                P.emit("vector", lambda e: e.reduce_sum(out=tC[:], in_=tA[:].rearrange(
                       "p (a b) -> p a b", a=8), axis=AXX),
                       reads=[B_tA], writes=[B_tC], incs=[("ve", 1)])
                P.emit("vector", lambda e: e.reciprocal(out=tD[:], in_=tC[:]),
                       reads=[B_tC], writes=[B_tD], incs=[("ve", 1)])
                P.emit("vector", lambda e: e.tensor_tensor(out=wtk[:], in0=tA[:],
                       in1=tD[:].rearrange("p (a b) -> p a b", b=1).broadcast_to([128, 8, 8]), op=ALU.mult),
                       reads=[B_tA, B_tD], writes=[B_wtk], incs=[("ve", 1)])
            else:
                P.emit("vector", lambda e: e.tensor_tensor(out=tA[:], in0=lgT[:],
                       in1=tC[:].rearrange("p (a b) -> p a b", b=1).broadcast_to([128, 8, 8]), op=ALU.is_equal),
                       reads=[B_lgT, B_tC], writes=[B_tA], incs=[("ve", 1)])
                P.emit("vector", lambda e: e.scalar_tensor_tensor(out=tA[:], in0=tA[:],
                       scalar=NEG, in1=lgT[:], op0=ALU.mult, op1=ALU.add),
                       reads=[B_tA, B_lgT], writes=[B_tA], incs=[("ve", 1)])
                P.emit("vector", lambda e: e.reduce_max(out=tD[:], in_=tA[:].rearrange(
                       "p (a b) -> p a b", a=8), axis=AXX),
                       reads=[B_tA], writes=[B_tD], incs=[("ve", 1)])
                P.emit("vector", lambda e: e.tensor_tensor(out=tA[:], in0=lgT[:],
                       in1=tC[:].rearrange("p (a b) -> p a b", b=1).broadcast_to([128, 8, 8]), op=ALU.subtract),
                       reads=[B_lgT, B_tC], writes=[B_tA], incs=[("ve", 1)])
                P.emit("scalar", lambda e: e.activation(out=tA[:], in_=tA[:], func=AF.Exp),
                       reads=[B_tA], writes=[B_tA], incs=[("sc", 1)])
                P.emit("vector", lambda e: e.tensor_tensor(out=tB[:], in0=lgT[:],
                       in1=tD[:].rearrange("p (a b) -> p a b", b=1).broadcast_to([128, 8, 8]), op=ALU.is_ge),
                       reads=[B_lgT, B_tD], writes=[B_tB], incs=[("ve", 1)])
                P.emit("vector", lambda e: e.tensor_tensor(out=tD[:], in0=tD[:], in1=tC[:],
                       op=ALU.subtract), reads=[B_tD, B_tC], writes=[B_tD], incs=[("ve", 1)])
                P.emit("scalar", lambda e: e.activation(out=tD[:], in_=tD[:], func=AF.Exp),
                       reads=[B_tD], writes=[B_tD], incs=[("sc", 1)])
                P.emit("vector", lambda e: e.tensor_scalar(out=tD[:], in0=tD[:], scalar1=1.0,
                       scalar2=None, op0=ALU.add), reads=[B_tD], writes=[B_tD], incs=[("ve", 1)])
                P.emit("vector", lambda e: e.reciprocal(out=tD[:], in_=tD[:]),
                       reads=[B_tD], writes=[B_tD], incs=[("ve", 1)])
                P.emit("vector", lambda e: e.tensor_tensor(out=tA[:], in0=tA[:], in1=tB[:],
                       op=ALU.mult), reads=[B_tA, B_tB], writes=[B_tA], incs=[("ve", 1)])
                P.emit("vector", lambda e: e.tensor_tensor(out=wtk[:], in0=tA[:],
                       in1=tD[:].rearrange("p (a b) -> p a b", b=1).broadcast_to([128, 8, 8]), op=ALU.mult),
                       reads=[B_tA, B_tD], writes=[B_wtk], incs=[("ve", 1)])

            fns = [lambda e, tc=tc: e.transpose(pwf[:, tc * 128:(tc + 1) * 128],
                   wtk[:, tc * 8:(tc + 1) * 8], identb[:]) for tc in range(8)]
            P.emit("tensor", fns, reads=[B_wtk, B_const], writes=[B_pwf], incs=[("te", 1)])
            P.emit("scalar", lambda e: e.copy(out=wfe[:], in_=pwf[:]),
                   reads=[B_pwf], writes=[B_wfe], incs=[("sc", 1)])
            exp = expand1 if layer == 0 else expand2
            fns = [lambda e, t=t, exp=exp: e.matmul(pg1[:, t * 512:(t + 1) * 512], exp[:],
                   wfe[:, t * 512:(t + 1) * 512], start=True, stop=True) for t in range(2)]
            P.emit("tensor", fns, reads=[B_wfe, B_const], writes=[B_pg1], incs=[("te", 1)])
            load_panel("a0" if layer == 0 else "la1", 0, 128, 0)
            gemm(pg2, B_pg2, 0)
            P.emit("scalar", lambda e: e.copy(out=g1[:, 0:1024], in_=pg2[:]),
                   reads=[B_pg2], writes=[B_g1], incs=[("sc", 1)])
            P.emit("vector", lambda e: e.tensor_tensor(out=hw[:], in0=pg1[:], in1=g1[:, 0:1024],
                   op=ALU.mult), reads=[B_pg1, B_g1], writes=[B_hw], incs=[("ve", 1)])
            P.join_all(exclude=("cc",))
            psg.close()
            stg.close()

        emit_gate_lora(0)
        if KDEBUG:
            dma("sync", "ds", dbg["dhw"], hw[:], reads=[B_hw])

        x1d_t = x1d.rearrange("(c p) t -> c p t", p=128)
        x1b_t = x1b.rearrange("(c p) t -> c p t", p=128)
        y_t = y.rearrange("(c p) t -> c p t", p=128)

        def emit_epilogue(layer):
            is0 = layer == 0
            wo_name = "w0o" if is0 else "w1o"
            wf_name = "w0f" if is0 else "bw1"
            bias_sb = b0sb if is0 else b1sb
            nsc = NCH if is0 else 8
            ste = ExitStack()
            bml = ste.enter_context(nc.sbuf_tensor(f"bml_{layer}", [128, 4096], BF16))
            gsb = ste.enter_context(nc.sbuf_tensor(f"gsb_{layer}", [128, 1024], BF16))
            tsb = ste.enter_context(nc.sbuf_tensor(f"tsb_{layer}", [128, 1024], BF16))
            xob = [ste.enter_context(nc.sbuf_tensor(f"xob{layer}_{i}", [128, 1024], F32)) for i in range(2)]
            B_bml, B_gsb, B_tsb = Buf(), Buf(), Buf()
            B_xob = [Buf(), Buf()]
            ret = None
            pse = ExitStack()
            s_ps = pse.enter_context(nc.psum_tensor(f"s_ps_{layer}", [128, 1024], F32))
            g_ps = pse.enter_context(nc.psum_tensor(f"g_ps_{layer}", [128, 1024], F32))
            B_sps, B_gps = Buf(), Buf()
            if is0:
                xb1 = ste.enter_context(nc.sbuf_tensor(f"xb1_{layer}", [128, 1024], BF16))
                sq1 = ste.enter_context(nc.sbuf_tensor(f"sq1_{layer}", [128, 1024], BF16))
                B_xb1, B_sq1 = Buf(), Buf()
                sx1 = pse.enter_context(nc.psum_tensor(f"sx1_{layer}", [1, 1024], F32))
                sx21 = pse.enter_context(nc.psum_tensor(f"sx21_{layer}", [1, 1024], F32))
                B_sx1, B_sx21 = Buf(), Buf()
                ret = (ste, pse, sx1, sx21, B_sx1, B_sx21, s_ps, B_sps)

            dma("gpsimd", "dg", bml[:], ap["bm0" if is0 else "lb1"], writes=[B_bml])
            load_panel(wo_name, 0, 128, 0, nch=nsc)
            if is0:
                load_panel(wf_name, 0, 128, 2)
            for oc in range(NCH):
                so, sf = oc % 2, 2 + (oc % 2)
                if oc + 1 < NCH:
                    load_panel(wo_name, oc + 1, 128, (oc + 1) % 2, nch=nsc)
                    if is0 or oc + 1 >= 8:
                        load_panel(wf_name, oc + 1, 128, 2 + ((oc + 1) % 2))
                po = oc % 2
                if is0:
                    dma("sync", "ds", xob[po][:], xloc_t[oc][:, 256:1280], writes=[B_xob[po]])
                else:
                    dma("sync", "ds", xob[po][:], x1d_t[oc], reads=[B_x1d[oc]], writes=[B_xob[po]])
                fns = []
                for c in range(nsc):
                    for t in range(2):
                        fns.append(lambda e, c=c, t=t, so=so: e.matmul(
                            s_ps[:, t * 512:(t + 1) * 512], wp[so][:, c * 128:(c + 1) * 128],
                            big2[:, c * TOK + t * 512: c * TOK + (t + 1) * 512],
                            start=(c == 0), stop=(c == nsc - 1)))
                P.emit("tensor", fns, reads=[B_wp[so], B_const] + B_big[:nsc],
                       writes=[B_sps], incs=[("te", 1)])
                gpre = (not is0) and oc < 8
                if not gpre:
                    fns = []
                    for c in range(NCH):
                        for t in range(2):
                            fns.append(lambda e, c=c, t=t, sf=sf: e.matmul(
                                g_ps[:, t * 512:(t + 1) * 512], wp[sf][:, c * 128:(c + 1) * 128],
                                xn[:, c * TOK + t * 512: c * TOK + (t + 1) * 512],
                                start=(c == 0), stop=False))
                    for t in range(2):
                        fns.append(lambda e, t=t, oc=oc: e.matmul(
                            g_ps[:, t * 512:(t + 1) * 512], bml[:, oc * 128:(oc + 1) * 128],
                            hw[:, t * 512:(t + 1) * 512], start=False, stop=True))
                    P.emit("tensor", fns, reads=[B_wp[sf], B_bml, B_hw, B_const] + B_xn,
                           writes=[B_gps], incs=[("te", 1)])
                    P.emit("scalar", lambda e, oc=oc: e.activation(out=gsb[:], in_=g_ps[:],
                           func=AF.Silu, bias=bias_sb[:, oc:oc + 1], scale=1.0),
                           reads=[B_gps, B_const], writes=[B_gsb], incs=[("sc", 1)])
                else:
                    P.emit("scalar", lambda e, oc=oc: e.activation(out=gsb[:],
                           in_=big2[:, (24 + oc) * TOK:(25 + oc) * TOK],
                           func=AF.Silu, bias=bias_sb[:, oc:oc + 1], scale=1.0),
                           reads=[B_big[24 + oc], B_const], writes=[B_gsb], incs=[("sc", 1)])
                P.emit("vector", lambda e: e.tensor_tensor(out=tsb[:], in0=s_ps[:], in1=gsb[:],
                       op=ALU.mult), reads=[B_sps, B_gsb], writes=[B_tsb], incs=[("ve", 1)])
                P.emit("gpsimd", lambda e, po=po: e.tensor_tensor(out=xob[po][:], in0=xob[po][:],
                       in1=tsb[:], op=ALU.add),
                       reads=[B_xob[po], B_tsb], writes=[B_xob[po]], incs=[("gp", 1)])
                if is0:
                    dma("sync", "ds", x1d_t[oc], xob[po][:], reads=[B_xob[po]], writes=[B_x1d[oc]])
                    P.emit("vector", lambda e, po=po: e.tensor_copy(out=xb1[:], in_=xob[po][:]),
                           reads=[B_xob[po]], writes=[B_xb1], incs=[("ve", 1)])
                    dma("sync", "ds", x1b_t[oc], xb1[:], reads=[B_xb1], writes=[B_x1b[oc]])
                    P.emit("vector", lambda e: e.tensor_tensor(out=sq1[:], in0=xb1[:], in1=xb1[:],
                           op=ALU.mult), reads=[B_xb1], writes=[B_sq1], incs=[("ve", 1)])
                    for (src, Bs, pst, Bp) in ((xb1, B_xb1, sx1, B_sx1), (sq1, B_sq1, sx21, B_sx21)):
                        fns = [lambda e, src=src, pst=pst, t=t, oc=oc: e.matmul(
                            pst[0:1, t * 512:(t + 1) * 512], ones1b[:], src[:, t * 512:(t + 1) * 512],
                            start=(oc == 0), stop=(oc == NCH - 1)) for t in range(2)]
                        P.emit("tensor", fns, reads=[Bs, B_const], writes=[Bp], incs=[("te", 1)])
                else:
                    dma("sync", "ds", y_t[oc], xob[po][:], reads=[B_xob[po]], writes=[B_y])
            if not is0:
                P.join_all()
                pse.close()
                ste.close()
            return ret

        st6 = ExitStack()
        rstdb1 = st6.enter_context(nc.sbuf_tensor("rstdb1", [128, 1024], F32))
        negmrb1 = st6.enter_context(nc.sbuf_tensor("negmrb1", [128, 1024], F32))

        (ste0, pse0, sx1, sx21, B_sx1, B_sx21, sps0, B_sps0) = emit_epilogue(0)

        B_stat1, B_rstdb1, B_negmrb1 = Buf(), Buf(), Buf()
        B_xscr6, B_xscr6b = Buf(), Buf()
        emit_stats_finalize(B_stat1, sx1, sx21, B_sx1, B_sx21,
                            rstdb1, negmrb1, B_rstdb1, B_negmrb1, 1024,
                            sps0[:, 0:512], B_sps0)
        snap6 = [(s, P.cnt[s]) for s in P.cnt if P.cnt[s] > 0]
        pse0.close()
        ste0.close()

        st6b = ExitStack()
        xscr6 = st6b.enter_context(nc.sbuf_tensor("xscr6", [128, 1024], F32))
        xscr6b = st6b.enter_context(nc.sbuf_tensor("xscr6b", [128, 1024], F32))
        x6 = [st6b.enter_context(nc.sbuf_tensor(f"x6_{i}", [128, 1024], BF16)) for i in range(4)]
        B_x6 = [Buf() for _ in range(4)]

        st7 = ExitStack()
        kevac = st7.enter_context(nc.sbuf_tensor("kevac", [128, 1024], BF16))
        B_kevac = Buf()
        ps7 = ExitStack()
        p7 = ps7.enter_context(nc.psum_tensor("p7", [128, 1024], F32))
        B_p7 = Buf()
        kvs_t = kvs.rearrange("(c p) t -> c p t", p=128)
        QS = 8 * TOK

        jobs = [("w1k", oc) for oc in range(8)] + [("w1v", oc) for oc in range(8)]

        def load_job(j):
            wname, oc = jobs[j]
            dma("gpsimd", "dg", wp[j % 4][:, 0:NCH * 128], ap[wname][oc],
                writes=[B_wp[j % 4]])

        for j in range(3):
            load_job(j)

        for c in range(NCH):
            p = c % 4
            eng, sem = ("vector", "ve") if c % 2 == 0 else ("gpsimd", "gp")
            scr, Bscr = (xscr6, B_xscr6) if c % 2 == 0 else (xscr6b, B_xscr6b)
            w6 = snap6 if c < 4 else ()
            dma("sync", "ds", x6[p][:], x1b_t[c], reads=[B_x1b[c]], writes=[B_x6[p]],
                waits=w6)
            P.emit(eng, lambda e, p=p, scr=scr: e.tensor_tensor(out=scr[:], in0=x6[p][:],
                   in1=rstdb1[:], op=ALU.mult),
                   reads=[B_x6[p], B_rstdb1], writes=[Bscr], incs=[(sem, 1)], waits=w6)
            P.emit(eng, lambda e, c=c, scr=scr: e.tensor_tensor(out=xn[:, c * TOK:(c + 1) * TOK],
                   in0=scr[:], in1=negmrb1[:], op=ALU.add),
                   reads=[Bscr, B_negmrb1], writes=[B_xn[c]], incs=[(sem, 1)])
        if KDEBUG:
            for c in range(NCH):
                dma("sync", "ds", dbg["dxn1"].rearrange("(c p) t -> c p t", p=128)[c],
                    xn[:, c * TOK:(c + 1) * TOK], reads=[B_xn[c]])

        for j, (wname, oc) in enumerate(jobs):
            slot = j % 4
            if j + 3 < len(jobs):
                load_job(j + 3)
            if j < 2:
                for c in range(NCH):
                    fns = [lambda e, c=c, t=t, slot=slot: e.matmul(
                        p7[:, t * 512:(t + 1) * 512], wp[slot][:, c * 128:(c + 1) * 128],
                        xn[:, c * TOK + t * 512: c * TOK + (t + 1) * 512],
                        start=(c == 0), stop=(c == NCH - 1)) for t in range(2)]
                    wb = [B_p7] if c in (0, NCH - 1) else []
                    P.emit("tensor", fns, reads=[B_wp[slot], B_xn[c]], writes=wb,
                           incs=[("te", 1)],
                           waits=(snap6 if j == 0 and c == 0 else ()))
            else:
                fns = []
                for c in range(NCH):
                    for t in range(2):
                        fns.append(lambda e, c=c, t=t, slot=slot: e.matmul(
                            p7[:, t * 512:(t + 1) * 512], wp[slot][:, c * 128:(c + 1) * 128],
                            xn[:, c * TOK + t * 512: c * TOK + (t + 1) * 512],
                            start=(c == 0), stop=(c == NCH - 1)))
                P.emit("tensor", fns, reads=[B_wp[slot], B_const] + B_xn,
                       writes=[B_p7], incs=[("te", 1)])
            P.emit("scalar", lambda e: e.copy(out=kevac[:], in_=p7[:]),
                   reads=[B_p7], writes=[B_kevac], incs=[("sc", 1)],
                   waits=(snap6 if j == 0 else ()))
            dma("sync", "ds", kvs_t[j], kevac[:], reads=[B_kevac], writes=[B_kvs])
        _kvw = [(f"ds{i}", P.cnt[f"ds{i}"]) for i in range(NLANES["ds"])]
        P.emit("gpsimd", lambda e: e.collective_compute(
            "AllGather", ALU.bypass, replica_groups=[[0, 1], [2, 3], [4, 5], [6, 7]],
            ins=[kvs], outs=[kva]), reads=[B_kvs], writes=[B_kva], incs=[("cc", 1)],
            waits=_kvw)
        P.join_all(exclude=("cc",))
        ps7.close()

        emit_gate_lora(1)

        ps7 = ExitStack()
        p7 = ps7.enter_context(nc.psum_tensor("p7b", [128, 1024], F32))
        B_p7 = Buf()

        def load_wqs_panel(oc, slot):
            src = wqall[:, oc].rearrange("r p i j -> p r (i j)")
            dstv = wp[slot][:, 0:NCH * 128].rearrange("p (r y) -> p r y", y=512)
            dma("sync", "ds", dstv, src, reads=[B_wqall], writes=[B_wp[slot]])

        load_wqs_panel(0, 0)
        for oc in range(8):
            slot = oc % 2
            if oc + 1 < 8:
                load_wqs_panel(oc + 1, (oc + 1) % 2)
            fns = []
            for c in range(NCH):
                for t in range(2):
                    fns.append(lambda e, c=c, t=t, slot=slot: e.matmul(
                        p7[:, t * 512:(t + 1) * 512], wp[slot][:, c * 128:(c + 1) * 128],
                        xn[:, c * TOK + t * 512: c * TOK + (t + 1) * 512],
                        start=(c == 0), stop=(c == NCH - 1)))
            P.emit("tensor", fns, reads=[B_wp[slot], B_const] + B_xn,
                   writes=[B_p7], incs=[("te", 1)])
            P.emit("scalar", lambda e, oc=oc: e.copy(
                   out=big2[:, QS + oc * TOK: QS + (oc + 1) * TOK], in_=p7[:]),
                   reads=[B_p7], writes=[B_big[8 + oc]], incs=[("sc", 1)])
        P.join_all(exclude=("cc",))
        ps7.close()
        st7.close()
        st6b.close()
        st6.close()
        if KDEBUG:
            for c in range(8):
                dma("sync", "ds", dbg["dqs"].rearrange("(c p) t -> c p t", p=128)[c],
                    big2[:, QS + c * TOK: QS + (c + 1) * TOK], reads=[B_big[8 + c]])

        if KDEBUG:
            dma("sync", "ds", dbg["dhw1"], hw[:], reads=[B_hw])

        st9 = ExitStack()
        Kh2 = [st9.enter_context(nc.sbuf_tensor(f"Kh{i}", [128, 2048], BF16)) for i in range(2)]
        Vh2 = [st9.enter_context(nc.sbuf_tensor(f"Vh{i}", [128, 2048], BF16)) for i in range(2)]
        Vtm2 = [st9.enter_context(nc.sbuf_tensor(f"Vtm{i}", [128, 2048], BF16)) for i in range(2)]
        rdenS = st9.enter_context(nc.sbuf_tensor("rdenS", [128, 512], F32))
        bm9 = st9.enter_context(nc.sbuf_tensor("bm9", [128, 4096], BF16))
        B_bm9 = Buf()
        B_Kh2 = [Buf(), Buf()]
        B_Vh2 = [Buf(), Buf()]
        B_Vtm2 = [Buf(), Buf()]
        B_rden = Buf()
        PTB = 16 * TOK
        B_pt = [Buf() for _ in range(8)]
        ps9 = ExitStack()
        sc9 = [ps9.enter_context(nc.psum_tensor(f"sc9_{i}", [128, 512], F32)) for i in range(3)]
        av9 = ps9.enter_context(nc.psum_tensor("av9", [128, 512], F32))
        den9 = ps9.enter_context(nc.psum_tensor("den9", [128, 512], F32))
        vt9 = ps9.enter_context(nc.psum_tensor("vt9", [128, 1024], BF16))
        g9 = ps9.enter_context(nc.psum_tensor("g9", [128, 512], F32))
        B_sc9 = [Buf() for _ in range(3)]
        B_av9, B_den9, B_vt9, B_g9 = Buf(), Buf(), Buf(), Buf()

        def emit_g_filler():
            th = []
            th.append(lambda: dma("gpsimd", "dg", bm9[:], ap["lb1"], writes=[B_bm9]))
            th.append(lambda: load_panel("bw1", 0, 128, 0))
            for oc in range(8):
                slot = oc % 2
                if oc + 1 < 8:
                    th.append(lambda oc=oc: load_panel("bw1", oc + 1, 128, (oc + 1) % 2))
                for t in range(2):
                    for g in range(8):
                        fns = []
                        for c in range(g * 4, g * 4 + 4):
                            fns.append(lambda e, c=c, t=t, slot=slot: e.matmul(
                                g9[:], wp[slot][:, c * 128:(c + 1) * 128],
                                xn[:, c * TOK + t * 512: c * TOK + (t + 1) * 512],
                                start=(c == 0), stop=False))
                        th.append(lambda fns=fns, slot=slot: P.emit("tensor", fns,
                            reads=[B_wp[slot], B_const] + B_xn, writes=[B_g9], incs=[("te", 1)]))
                    th.append(lambda t=t, oc=oc: P.emit("tensor",
                        lambda e, t=t, oc=oc: e.matmul(g9[:], bm9[:, oc * 128:(oc + 1) * 128],
                            hw[:, t * 512:(t + 1) * 512], start=False, stop=True),
                        reads=[B_bm9, B_hw, B_const], writes=[B_g9], incs=[("te", 1)]))
                    th.append(lambda t=t, oc=oc: P.emit("scalar", lambda e, t=t, oc=oc: e.copy(
                        out=big2[:, (24 + oc) * TOK + t * 512:(24 + oc) * TOK + (t + 1) * 512],
                        in_=g9[:]), reads=[B_g9], writes=[B_big[24 + oc]], incs=[("sc", 1)]))
            return th

        gfill = emit_g_filler()
        gfi = 0

        def gf(n=1):
            nonlocal gfi
            for _ in range(n):
                if gfi < len(gfill):
                    gfill[gfi]()
                    gfi += 1

        def emit_kv_load(h):
            hp = h % 2
            for r in range(2):
                dma("sync", "ds", Kh2[hp][:, r * 1024:(r + 1) * 1024],
                    kva[r, h * 128:(h + 1) * 128, :], reads=[B_kva], writes=[B_Kh2[hp]],
                    waits=[("cc", P.cnt["cc"])])
                dma("sync", "ds", Vh2[hp][:, r * 1024:(r + 1) * 1024],
                    kva[r, 1024 + h * 128: 1024 + (h + 1) * 128, :],
                    reads=[B_kva], writes=[B_Vh2[hp]], waits=[("cc", P.cnt["cc"])])

        pendfin = [None]

        def emit_fin():
            if pendfin[0] is None:
                return
            h, qgv = pendfin[0]
            pendfin[0] = None
            P.emit("vector", lambda e: e.reciprocal(out=rdenS[:], in_=den9[:]),
                   reads=[B_den9], writes=[B_rden], incs=[("ve", 1)])
            P.emit("vector", lambda e, h=h, qgv=qgv: e.tensor_tensor(
                   out=big2[:, h * TOK + qgv * 512: h * TOK + (qgv + 1) * 512],
                   in0=av9[:], in1=rdenS[:], op=ALU.mult),
                   reads=[B_av9, B_rden], writes=[B_big[h]], incs=[("ve", 1)])

        emit_kv_load(0)
        for h in range(NKV):
            hp = h % 2
            Kh, Vh, Vtm = Kh2[hp], Vh2[hp], Vtm2[hp]
            B_Kh, B_Vh, B_Vtm = B_Kh2[hp], B_Vh2[hp], B_Vtm2[hp]
            if h + 1 < NKV:
                emit_kv_load(h + 1)
            for g in range(2):
                fns = [lambda e, g=g, t=t, Vh=Vh: e.transpose(vt9[:, t * 128:(t + 1) * 128],
                       Vh[:, (g * 8 + t) * 128:(g * 8 + t + 1) * 128], identb[:]) for t in range(8)]
                P.emit("tensor", fns, reads=[B_Vh, B_const], writes=[B_vt9], incs=[("te", 1)])
                P.emit("vector", lambda e, g=g, Vtm=Vtm: e.tensor_copy(
                       out=Vtm[:, g * 1024:(g + 1) * 1024],
                       in_=vt9[:]), reads=[B_vt9], writes=[B_Vtm], incs=[("ve", 1)])
                gf(1)
            for qgv in range(2):
                Qs = big2[:, QS + h * TOK + qgv * 512: QS + h * TOK + (qgv + 1) * 512]

                def s_mm(kc, Kh=Kh, B_Kh=B_Kh, Qs=Qs, h=h):
                    P.emit("tensor", lambda e, kc=kc, Kh=Kh, Qs=Qs: e.matmul(
                        sc9[kc % 3][:], Kh[:, kc * 128:(kc + 1) * 128], Qs,
                        start=True, stop=True),
                        reads=[B_Kh, B_big[8 + h]], writes=[B_sc9[kc % 3]], incs=[("te", 1)])

                def e_act(kc):
                    P.emit("scalar", lambda e, kc=kc: e.activation(
                        out=big2[:, PTB + (kc % 8) * 512: PTB + (kc % 8 + 1) * 512],
                        in_=sc9[kc % 3][:], func=AF.Exp, scale=SCALE),
                        reads=[B_sc9[kc % 3]], writes=[B_pt[kc % 8]], incs=[("sc", 1)])

                for kc in range(3):
                    s_mm(kc)
                    e_act(kc)
                emit_fin()
                for kc in range(16):
                    pt = big2[:, PTB + (kc % 8) * 512: PTB + (kc % 8 + 1) * 512]
                    P.emit("tensor", lambda e, pt=pt, kc=kc, Vtm=Vtm: e.matmul(
                        av9[:], Vtm[:, kc * 128:(kc + 1) * 128], pt,
                        start=(kc == 0), stop=(kc == 15)),
                        reads=[B_pt[kc % 8], B_Vtm], writes=[B_av9], incs=[("te", 1)])
                    P.emit("tensor", lambda e, pt=pt, kc=kc: e.matmul(
                        den9[:], ones128[:], pt, start=(kc == 0), stop=(kc == 15)),
                        reads=[B_pt[kc % 8], B_const], writes=[B_den9], incs=[("te", 1)])
                    if kc + 3 < 16:
                        s_mm(kc + 3)
                        e_act(kc + 3)
                    gf(1)
                pendfin[0] = (h, qgv)
        emit_fin()
        gf(len(gfill))

        P.join_all()
        ps9.close()
        st9.close()
        if KDEBUG:
            for c in range(8):
                dma("sync", "ds", dbg["dval"].rearrange("(c p) t -> c p t", p=128)[c],
                    big2[:, c * TOK:(c + 1) * TOK], reads=[B_big[c]])
            for c in range(NCH):
                dma("sync", "ds", dbg["dx1"].rearrange("(c p) t -> c p t", p=128)[c],
                    x1d_t[c], reads=[B_x1d[c]])

        emit_epilogue(1)

        P.join("sync")

        with nc.Block() as block:
            prog.replay(block)
    return nc


def _panelize(wt, ncol=128):
    din, o = wt.shape
    nch, npan = din // 128, o // ncol
    return np.ascontiguousarray(
        wt.reshape(nch, 128, npan, ncol).transpose(2, 1, 0, 3).reshape(npan, 128, nch * ncol))


def kernel(**inputs):
    f32 = np.float32
    x = np.asarray(inputs["L0_attn_x"], f32)
    W = {}
    W["w0q"] = _panelize(np.asarray(inputs["L0_attn_Wq"], f32).T)
    W["w0k"] = _panelize(np.asarray(inputs["L0_attn_Wk"], f32).T)
    W["w0v"] = _panelize(np.asarray(inputs["L0_attn_Wv"], f32).T)
    W["w0o"] = _panelize(np.asarray(inputs["L0_attn_Wo"], f32).T)
    W["w0f"] = _panelize(np.asarray(inputs["L0_ffn_W"], f32).T)
    W["w0fq"] = _panelize(np.asarray(inputs["L0_ffn_Wq"], f32).T)
    W["w0fk"] = _panelize(np.asarray(inputs["L0_ffn_Wk"], f32).T)
    W["a0"] = _panelize(np.asarray(inputs["L0_ffn_A"], f32).reshape(128, D).T)
    W["bm0"] = np.ascontiguousarray(np.asarray(inputs["L0_ffn_Bm"], f32).transpose(0, 2, 1).reshape(128, D))
    w1qf = np.asarray(inputs["L1_attn_wq"], f32)
    W["w1k"] = _panelize(np.asarray(inputs["L1_attn_wk"], f32))
    W["w1v"] = _panelize(np.asarray(inputs["L1_attn_wv"], f32))
    W["w1o"] = _panelize(np.asarray(inputs["L1_attn_wo"], f32))
    W["gw1"] = _panelize(np.asarray(inputs["L1_ffn_gate_w"], f32).T, ncol=8)
    W["bw1"] = _panelize(np.asarray(inputs["L1_ffn_base_w"], f32).T)
    W["la1"] = _panelize(np.asarray(inputs["L1_ffn_lora_A"], f32).reshape(128, D).T)
    W["lb1"] = np.ascontiguousarray(np.asarray(inputs["L1_ffn_lora_B"], f32).transpose(0, 2, 1).reshape(128, D))
    b0 = np.ascontiguousarray(np.asarray(inputs["L0_ffn_b"], f32).reshape(32, 128).T)
    b1 = np.ascontiguousarray(np.asarray(inputs["L1_ffn_base_b"], f32).reshape(32, 128).T)

    consts = [_consts_for(0), _consts_for(1)]
    in_maps = []
    for core in range(8):
        b, half = core // 2, core % 2
        off = 1024 * half
        lo, hi = off - 256, off + 1280
        s0, s1 = max(lo, 0), min(hi, 2048)
        xl = np.zeros((LOC, D), f32)
        xl[s0 - lo:s1 - lo] = x[b, s0:s1]
        m = {"xloc": np.ascontiguousarray(xl.T), "b0": b0, "b1": b1,
             "w1qsl": np.ascontiguousarray(
                 w1qf[core * 512:(core + 1) * 512].reshape(4, 128, 4096))}
        m.update(W)
        m.update(consts[half])
        in_maps.append(m)

    nc = build_nc()
    res = run_bass_kernel_spmd(nc, in_maps, core_ids=list(range(8)))
    out = np.zeros((4, 2048, 4096), np.float32)
    for core in range(8):
        b, half = core // 2, core % 2
        out[b, 1024 * half:1024 * (half + 1)] = res.results[core]["y"].T
    return out


# version 17
# speedup vs baseline: 1.0403x; 1.0050x over previous
import numpy as np
import ml_dtypes
import concourse.bass as bass
import concourse.mybir as mybir
from concourse.bass_utils import run_bass_kernel_spmd
from contextlib import ExitStack

F32 = mybir.dt.float32
BF16 = mybir.dt.bfloat16
AF = mybir.ActivationFunctionType
ALU = mybir.AluOpType
AXX = mybir.AxisListType.X

KDEBUG = False

D = 4096
NCH = 32
TOK = 1024
LOC = 1536
H0 = 32
NKV = 8
SCALE = 1.0 / float(np.sqrt(128.0))
NEG = -1e9


class Buf:
    __slots__ = ("w", "r")

    def __init__(self):
        self.w = None
        self.r = {}

    def rd_waits(self):
        return [self.w] if self.w is not None else []

    def wr_waits(self):
        ws = dict(self.r)
        if self.w is not None:
            s, v = self.w
            ws[s] = max(ws.get(s, -1), v)
        return list(ws.items())


class Prog:
    ENGS = ("tensor", "vector", "scalar", "gpsimd", "sync")

    def __init__(self, nc):
        self.nc = nc
        self.ev = {e: [] for e in self.ENGS}
        self.sems = {}
        self.cnt = {}
        self.lastwait = {e: {} for e in self.ENGS}

    def sem(self, name):
        self.sems[name] = self.nc.alloc_semaphore(name)
        self.cnt[name] = 0

    def emit(self, eng, fns, reads=(), writes=(), waits=(), incs=()):
        if callable(fns):
            fns = [fns]
        wset = {}
        for s, v in waits:
            wset[s] = max(wset.get(s, -1), v)
        for b in reads:
            for s, v in b.rd_waits():
                wset[s] = max(wset.get(s, -1), v)
        for b in writes:
            for s, v in b.wr_waits():
                wset[s] = max(wset.get(s, -1), v)
        fw = []
        for s, v in wset.items():
            if self.lastwait[eng].get(s, -1) < v:
                fw.append((self.sems[s], v))
                self.lastwait[eng][s] = v
        marks = []
        for s, n in incs:
            self.cnt[s] += n
            marks.append((s, self.cnt[s]))
        mark = marks[0] if marks else None
        self.ev[eng].append((fns, fw, [(self.sems[s], n) for s, n in incs]))
        if mark is not None:
            for b in reads:
                b.r[mark[0]] = max(b.r.get(mark[0], -1), mark[1])
            for b in writes:
                b.w = mark
                b.r = {}
        return mark

    def join(self, eng, sems=None):
        if sems is None:
            sems = list(self.sems)
        fw = []
        for s in sems:
            v = self.cnt[s]
            if v > 0 and self.lastwait[eng].get(s, -1) < v:
                fw.append((self.sems[s], v))
                self.lastwait[eng][s] = v
        if fw:
            self.ev[eng].append(([], fw, []))

    def join_all(self, exclude=()):
        for e in self.ENGS:
            self.join(e, [s for s in self.sems if s not in exclude])

    def replay(self, block):
        def mk(engname):
            def body(e):
                for fns, fw, incs in self.ev[engname]:
                    for h, v in fw:
                        e.wait_ge(h, v)
                    ins = None
                    for f in fns:
                        ins = f(e)
                    for h, n in incs:
                        assert ins is not None
                        ins = ins.then_inc(h, n)
            return body
        block.tensor(mk("tensor"))
        block.vector(mk("vector"))
        block.scalar(mk("scalar"))
        block.gpsimd(mk("gpsimd"))
        block.sync(mk("sync"))


def _rope_cache(S=2048, Dh=128, base=10000.0):
    inv = 1.0 / base ** (np.arange(0, Dh, 2, dtype=np.float32) / Dh)
    t = np.arange(S, dtype=np.float32)
    f = np.outer(t, inv)
    cos = np.zeros((S, Dh), np.float32)
    sin = np.zeros((S, Dh), np.float32)
    cos[:, 0::2] = np.cos(f); cos[:, 1::2] = np.cos(f)
    sin[:, 0::2] = np.sin(f); sin[:, 1::2] = np.sin(f)
    return cos, sin


def _consts_for(half):
    off = 1024 * half
    cosF, sinF = _rope_cache()
    qpos = np.arange(off, off + TOK)
    kpos = np.clip(np.arange(off - 256, off + 1280), 0, 2047)
    c = {}
    c["cosq"] = np.ascontiguousarray((cosF[qpos] * SCALE).T).astype(ml_dtypes.bfloat16)
    c["sinq"] = np.ascontiguousarray((sinF[qpos] * SCALE).T).astype(ml_dtypes.bfloat16)
    c["cosk"] = np.ascontiguousarray(cosF[kpos].T).astype(ml_dtypes.bfloat16)
    c["sink"] = np.ascontiguousarray(sinF[kpos].T).astype(ml_dtypes.bfloat16)
    P = np.zeros((128, 128), np.float32)
    for d in range(64):
        P[d, d + 64] = -1.0
        P[d + 64, d] = 1.0
    c["permT"] = np.ascontiguousarray(P.T).astype(ml_dtypes.bfloat16)
    c["identb"] = np.eye(128, dtype=ml_dtypes.bfloat16)
    c["identf"] = np.eye(8, dtype=np.float32)
    mask = np.zeros((128, 12), np.float32)
    if half == 0:
        mask[:, 0] = NEG; mask[:, 1] = NEG
    else:
        mask[:, 10] = NEG; mask[:, 11] = NEG
    c["maskb"] = mask
    ramp = np.arange(256, dtype=np.float32) / 255.0
    coef = np.zeros((8, 512), np.float32)
    for w in range(5):
        if w >= 1:
            coef[w, 0:256] = ramp
        if w <= 3:
            coef[w, 256:512] = 1.0 - ramp
    if half == 0:
        coef[0, 256:512] = 0.0
        coef[1, 0:256] = 1.0
    coef2 = np.zeros((8, 1024), np.float32)
    coef2[0, 0:512] = coef[1]; coef2[1, 0:512] = coef[0]; coef2[2, 0:512] = coef[2]
    coef2[0, 512:1024] = coef[3]; coef2[1, 512:1024] = coef[4]
    c["coef"] = coef2
    ROW_OF = {1: 0, 0: 1, 2: 2, 3: 0, 4: 1}
    basis = np.zeros((128, 40), ml_dtypes.bfloat16)
    for w in range(5):
        basis[:, w * 8 + ROW_OF[w]] = 1.0
    c["basis"] = basis
    c["ones1b"] = np.ones((128, 1), ml_dtypes.bfloat16)
    c["onesbf"] = np.ones((1, 128), np.float32)
    rs = np.zeros((8, 6 * 128), ml_dtypes.bfloat16)
    for w in range(6):
        rs[w, w * 128:(w + 1) * 128] = 1.0
    c["rowselb"] = rs
    blk = np.zeros((128, 16), ml_dtypes.bfloat16)
    for oc in range(2):
        for p in range(128):
            blk[p, oc * 8 + oc * 4 + p // 32] = 1.0
    c["blocksel"] = blk
    ex1 = np.zeros((8, 128), ml_dtypes.bfloat16)
    ex2 = np.zeros((8, 128), ml_dtypes.bfloat16)
    for m in range(128):
        ex1[m // 16, m] = 1.0
        ex2[m // 16, m] = 2.0
    c["expand1"] = ex1
    c["expand2"] = ex2
    c["ones128"] = np.ones((128, 128), ml_dtypes.bfloat16)
    return c


_CONST_SHAPES = {
    "cosq": ([128, 1024], BF16), "sinq": ([128, 1024], BF16),
    "cosk": ([128, 1536], BF16), "sink": ([128, 1536], BF16),
    "permT": ([128, 128], BF16), "identb": ([128, 128], BF16),
    "identf": ([8, 8], F32), "maskb": ([128, 12], F32),
    "coef": ([8, 1024], F32), "basis": ([128, 40], BF16),
    "ones1b": ([128, 1], BF16), "rowselb": ([8, 768], BF16),
    "onesbf": ([1, 128], F32), "blocksel": ([128, 16], BF16),
    "expand1": ([8, 128], BF16), "expand2": ([8, 128], BF16),
    "ones128": ([128, 128], BF16),
}

_WEIGHT_SHAPES = {
    "w0q": [32, 128, D], "w0k": [1, 128, D], "w0v": [1, 128, D],
    "w0o": [32, 128, D], "w0f": [32, 128, D],
    "w0fq": [2, 128, D], "w0fk": [2, 128, D],
    "a0": [1, 128, D], "bm0": [128, D],
    "w1k": [8, 128, D], "w1v": [8, 128, D], "w1o": [32, 128, 1024],
    "gw1": [1, 128, 256], "bw1": [32, 128, D], "la1": [1, 128, D],
    "lb1": [128, D],
}


def build_nc():
    nc = bass.Bass("TRN2", target_bir_lowering=False)
    ap = {}
    ap["xloc"] = nc.dram_tensor("xloc", [D, LOC], F32, kind="ExternalInput").ap()
    for n, shp in _WEIGHT_SHAPES.items():
        ap[n] = nc.dram_tensor(n, shp, F32, kind="ExternalInput").ap()
    ap["b0"] = nc.dram_tensor("b0", [128, 32], F32, kind="ExternalInput").ap()
    ap["b1"] = nc.dram_tensor("b1", [128, 32], F32, kind="ExternalInput").ap()
    for n, (shp, dt) in _CONST_SHAPES.items():
        ap[n] = nc.dram_tensor(n, shp, dt, kind="ExternalInput").ap()
    y = nc.dram_tensor("y", [D, TOK], F32, kind="ExternalOutput").ap()
    dbg = {}
    if KDEBUG:
        for n, shp, dt in [("dxn", [D, TOK], BF16), ("dao", [D, TOK], BF16),
                           ("dx1", [D, TOK], F32), ("dxn1", [D, TOK], BF16),
                           ("dqs", [1024, TOK], BF16), ("dval", [1024, TOK], BF16),
                           ("dhw", [128, TOK], BF16), ("dhw1", [128, TOK], BF16),
                           ("dkr", [128, LOC], BF16)]:
            dbg[n] = nc.dram_tensor(n, shp, dt, kind="ExternalOutput").ap()
    ap["w1qsl"] = nc.dram_tensor("w1qsl", [4, 128, D], F32, kind="ExternalInput").ap()
    x1d = nc.dram_tensor("x1d", [D, TOK], F32).ap()
    x1b = nc.dram_tensor("x1b", [D, TOK], BF16).ap()
    kvs = nc.dram_tensor("kvs", [2048, TOK], BF16).ap()
    kva = nc.dram_tensor("kva", [2, 2048, TOK], BF16).ap()
    wqpart = nc.dram_tensor("wqpart", [8, 128, 4, 128], BF16).ap()
    wqall = nc.dram_tensor("wqall", [8, 8, 128, 4, 128], BF16, addr_space="Shared").ap()

    st = ExitStack()
    with st:
        xn = st.enter_context(nc.sbuf_tensor("xn", [128, NCH * TOK], BF16))
        big2 = st.enter_context(nc.sbuf_tensor("big2", [128, NCH * TOK], BF16))
        wp = [st.enter_context(nc.sbuf_tensor(f"wp{i}", [128, 4096], BF16)) for i in range(4)]
        hw = st.enter_context(nc.sbuf_tensor("hw", [128, 1024], BF16))
        permT = st.enter_context(nc.sbuf_tensor("permT_s", [128, 128], BF16))
        identb = st.enter_context(nc.sbuf_tensor("identb_s", [128, 128], BF16))
        identf = st.enter_context(nc.sbuf_tensor("identf_s", [8, 8], F32))
        maskb = st.enter_context(nc.sbuf_tensor("maskb_s", [128, 12], F32))
        basis = st.enter_context(nc.sbuf_tensor("basis_s", [128, 40], BF16))
        ones1b = st.enter_context(nc.sbuf_tensor("ones1b_s", [128, 1], BF16))
        rowselb = st.enter_context(nc.sbuf_tensor("rowselb_s", [8, 768], BF16))
        onesbf = st.enter_context(nc.sbuf_tensor("onesbf_s", [1, 128], F32))
        blocksel = st.enter_context(nc.sbuf_tensor("blocksel_s", [128, 16], BF16))
        expand1 = st.enter_context(nc.sbuf_tensor("expand1_s", [8, 128], BF16))
        expand2 = st.enter_context(nc.sbuf_tensor("expand2_s", [8, 128], BF16))
        ones128 = st.enter_context(nc.sbuf_tensor("ones128_s", [128, 128], BF16))
        b0sb = st.enter_context(nc.sbuf_tensor("b0sb", [128, 32], F32))
        b1sb = st.enter_context(nc.sbuf_tensor("b1sb", [128, 32], F32))

        prog = Prog(nc)
        for s in ("te", "ve", "sc", "gp", "cc"):
            prog.sem(s)
        NLANES = {"ds": 8, "dg": 6}
        for q, nl in NLANES.items():
            for i in range(nl):
                prog.sem(f"{q}{i}")
        _rr = {"ds": 0, "dg": 0}
        P = prog

        B_xn = [Buf() for _ in range(NCH)]
        B_big = [Buf() for _ in range(NCH)]
        B_wp = [Buf() for _ in range(4)]
        B_hw = Buf()
        B_const = Buf()
        B_x1d = [Buf() for _ in range(NCH)]
        B_x1b = [Buf() for _ in range(NCH)]
        B_kvs = Buf()
        B_kva = Buf()
        B_y = Buf()

        def dma(eng, q, out, in_, reads=(), writes=(), waits=(), **kw):
            lane = f"{q}{_rr[q]}"
            _rr[q] = (_rr[q] + 1) % NLANES[q]
            w = list(waits) + [(lane, P.cnt[lane])]
            return P.emit(eng, lambda e: e.dma_start(out=out, in_=in_, **kw),
                          reads=reads, writes=writes, waits=w, incs=[(lane, 16)])

        for sb, n in [(permT, "permT"), (identb, "identb"), (identf, "identf"),
                      (maskb, "maskb"), (basis, "basis"), (ones1b, "ones1b"),
                      (rowselb, "rowselb"), (onesbf, "onesbf"), (blocksel, "blocksel"),
                      (expand1, "expand1"), (expand2, "expand2"), (ones128, "ones128"),
                      (b0sb, "b0"), (b1sb, "b1")]:
            dma("sync", "ds", sb[:], ap[n], writes=[B_const])

        def load_panel(wname, pan, ncol, slot, nch=NCH):
            return dma("gpsimd", "dg", wp[slot][:, 0:nch * ncol], ap[wname][pan],
                       writes=[B_wp[slot]])

        def emit_stats_finalize(B_stat, sx_ps, sx2_ps, B_sxps, B_sx2ps,
                                rstdb, negmrb, B_rstdb, B_negmrb, ncols, bc_ps, B_bcps):
            inv = 1.0 / float(D)
            mrow = negmrb[0:1, 0:ncols]
            scr = rstdb[0:1, 0:ncols]
            P.emit("vector", lambda e: e.tensor_scalar(out=mrow, in0=sx_ps[0:1, 0:ncols],
                   scalar1=inv, scalar2=None, op0=ALU.mult),
                   reads=[B_sxps], writes=[B_stat], incs=[("ve", 1)])
            P.emit("vector", lambda e: e.tensor_tensor(out=scr, in0=mrow, in1=mrow, op=ALU.mult),
                   reads=[B_stat], writes=[B_stat], incs=[("ve", 1)])
            P.emit("vector", lambda e: e.scalar_tensor_tensor(out=scr, in0=sx2_ps[0:1, 0:ncols],
                   scalar=inv, in1=scr, op0=ALU.mult, op1=ALU.subtract),
                   reads=[B_sx2ps, B_stat], writes=[B_stat], incs=[("ve", 1)])
            P.emit("vector", lambda e: e.tensor_scalar(out=scr, in0=scr,
                   scalar1=1e-5, scalar2=None, op0=ALU.add),
                   reads=[B_stat], writes=[B_stat], incs=[("ve", 1)])
            P.emit("scalar", lambda e: e.activation(out=scr, in_=scr, func=AF.Sqrt),
                   reads=[B_stat], writes=[B_stat], incs=[("sc", 1)])
            P.emit("vector", lambda e: e.reciprocal(out=scr, in_=scr),
                   reads=[B_stat], writes=[B_stat], incs=[("ve", 1)])
            P.emit("vector", lambda e: e.tensor_tensor(out=mrow, in0=mrow, in1=scr, op=ALU.mult),
                   reads=[B_stat], writes=[B_stat], incs=[("ve", 1)])
            P.emit("vector", lambda e: e.tensor_scalar(out=mrow, in0=mrow,
                   scalar1=-1.0, scalar2=None, op0=ALU.mult),
                   reads=[B_stat], writes=[B_stat], incs=[("ve", 1)])
            for (srcr, dst, Bd) in ((mrow, negmrb, B_negmrb), (scr, rstdb, B_rstdb)):
                for s in range(0, ncols, 512):
                    n = min(512, ncols - s)
                    P.emit("tensor", lambda e, srcr=srcr, s=s, n=n: e.matmul(
                        bc_ps[:, 0:n], onesbf[:], srcr[0:1, s:s + n],
                        start=True, stop=True),
                        reads=[B_stat, B_const], writes=[B_bcps], incs=[("te", 1)])
                    P.emit("scalar", lambda e, dst=dst, s=s, n=n: e.copy(
                        out=dst[:, s:s + n], in_=bc_ps[:, 0:n]),
                        reads=[B_bcps, B_stat], writes=[Bd], incs=[("sc", 1)])

        xb = big2[:, 0:1536]
        sqv = big2[:, 1536:3072]
        coskb = big2[:, 3072:4608]
        sinkb = big2[:, 4608:6144]
        kraw = big2[:, 6144:7680]
        xnh = big2[:, 7680:8192]
        B_xb, B_sq, B_ck, B_sk, B_kraw, B_xnh = Buf(), Buf(), Buf(), Buf(), Buf(), Buf()

        st23 = ExitStack()
        Kr = st23.enter_context(nc.sbuf_tensor("Kr", [128, LOC], BF16))
        V0 = st23.enter_context(nc.sbuf_tensor("V0", [128, LOC], BF16))
        B_Kr, B_V0 = Buf(), Buf()

        st12 = ExitStack()
        xf = [st12.enter_context(nc.sbuf_tensor(f"xf{i}", [128, LOC], F32)) for i in range(2)]
        rstdb = st12.enter_context(nc.sbuf_tensor("rstdb", [128, LOC], F32))
        negmrb = st12.enter_context(nc.sbuf_tensor("negmrb", [128, LOC], F32))
        xscr = st12.enter_context(nc.sbuf_tensor("xscr", [128, LOC], F32))
        xscrb = st12.enter_context(nc.sbuf_tensor("xscrb", [128, 1024], F32))
        B_xfa = [Buf(), Buf()]
        B_xfb = [Buf(), Buf()]
        B_stat, B_rstdb, B_negmrb, B_xscr = Buf(), Buf(), Buf(), Buf()

        ps12 = ExitStack()
        sx_ps = ps12.enter_context(nc.psum_tensor("sx_ps", [1, LOC], F32))
        sx2_ps = ps12.enter_context(nc.psum_tensor("sx2_ps", [1, LOC], F32))
        bc_ps = ps12.enter_context(nc.psum_tensor("bc_ps", [128, 512], F32))
        B_sxps, B_sx2ps, B_bcps = Buf(), Buf(), Buf()

        xloc_t = ap["xloc"].rearrange("(c p) t -> c p t", p=128)

        for c in range(NCH):
            p = c % 2
            dma("sync", "ds", xf[p][:, 0:768], xloc_t[c][:, 0:768], writes=[B_xfa[p]])
            dma("sync", "ds", xf[p][:, 768:1536], xloc_t[c][:, 768:1536], writes=[B_xfb[p]])
            P.emit("vector", lambda e, p=p: e.tensor_copy(out=xb, in_=xf[p][:]),
                   reads=[B_xfa[p], B_xfb[p]], writes=[B_xb], incs=[("ve", 1)])
            P.emit("scalar", lambda e, c=c: e.copy(out=xn[:, c * TOK:(c + 1) * TOK],
                   in_=xb[:, 256:1280]),
                   reads=[B_xb], writes=[B_xn[c]], incs=[("sc", 1)])
            P.emit("vector", lambda e: e.tensor_tensor(out=sqv, in0=xb, in1=xb, op=ALU.mult),
                   reads=[B_xb], writes=[B_sq], incs=[("ve", 1)])
            for (src, Bs, pst, Bp) in ((xb, B_xb, sx_ps, B_sxps), (sqv, B_sq, sx2_ps, B_sx2ps)):
                fns = [lambda e, src=src, pst=pst, s=s, c=c: e.matmul(
                    pst[0:1, s * 512:(s + 1) * 512], ones1b[:], src[:, s * 512:(s + 1) * 512],
                    start=(c == 0), stop=(c == NCH - 1)) for s in range(3)]
                P.emit("tensor", fns, reads=[Bs, B_const], writes=[Bp], incs=[("te", 1)])

        emit_stats_finalize(B_stat, sx_ps, sx2_ps, B_sxps, B_sx2ps,
                            rstdb, negmrb, B_rstdb, B_negmrb, LOC, bc_ps, B_bcps)
        P.join_all()
        ps12.close()

        psKV = ExitStack()
        k_ps = psKV.enter_context(nc.psum_tensor("k_ps", [128, LOC], F32))
        v_ps = psKV.enter_context(nc.psum_tensor("v_ps", [128, LOC], F32))
        pkp = psKV.enter_context(nc.psum_tensor("pkp", [128, 512], F32))
        vt_ps = psKV.enter_context(nc.psum_tensor("vt_ps", [128, 768], BF16))
        B_kps, B_vps, B_pkp, B_vtps = Buf(), Buf(), Buf(), Buf()

        load_panel("w0k", 0, 128, 2)
        load_panel("w0v", 0, 128, 3)
        load_panel("w0q", 0, 128, 0)

        B_xso, B_xsob, B_xsch = Buf(), Buf(), Buf()
        for c in range(NCH):
            p = c % 2
            dma("sync", "ds", xf[p][:, 0:256], xloc_t[c][:, 0:256], writes=[B_xfa[p]])
            dma("sync", "ds", xf[p][:, 256:512], xloc_t[c][:, 1280:1536], writes=[B_xfb[p]])
            eng, sem = ("vector", "ve") if c % 2 == 0 else ("gpsimd", "gp")
            so, Bso = (xscr[:, 0:1024], B_xso) if c % 2 == 0 else (xscrb[:], B_xsob)
            P.emit(eng, lambda e, c=c, so=so: e.tensor_tensor(
                   out=so, in0=xn[:, c * TOK:(c + 1) * TOK],
                   in1=rstdb[:, 256:1280], op=ALU.mult),
                   reads=[B_xn[c], B_rstdb], writes=[Bso], incs=[(sem, 1)])
            P.emit(eng, lambda e, c=c, so=so: e.tensor_tensor(
                   out=xn[:, c * TOK:(c + 1) * TOK], in0=so,
                   in1=negmrb[:, 256:1280], op=ALU.add),
                   reads=[Bso, B_negmrb], writes=[B_xn[c]], incs=[(sem, 1)])
            P.emit("vector", lambda e, p=p: e.tensor_tensor(out=xscr[:, 1024:1280],
                   in0=xf[p][:, 0:256], in1=rstdb[:, 0:256], op=ALU.mult),
                   reads=[B_xfa[p], B_rstdb], writes=[B_xsch], incs=[("ve", 1)])
            P.emit("vector", lambda e: e.tensor_tensor(out=xnh[:, 0:256],
                   in0=xscr[:, 1024:1280], in1=negmrb[:, 0:256], op=ALU.add),
                   reads=[B_xsch, B_negmrb], writes=[B_xnh], incs=[("ve", 1)])
            P.emit("vector", lambda e, p=p: e.tensor_tensor(out=xscr[:, 1280:1536],
                   in0=xf[p][:, 256:512], in1=rstdb[:, 1280:1536], op=ALU.mult),
                   reads=[B_xfb[p], B_rstdb], writes=[B_xsch], incs=[("ve", 1)])
            P.emit("vector", lambda e: e.tensor_tensor(out=xnh[:, 256:512],
                   in0=xscr[:, 1280:1536], in1=negmrb[:, 1280:1536], op=ALU.add),
                   reads=[B_xsch, B_negmrb], writes=[B_xnh], incs=[("ve", 1)])
            segs = [(0, 256, "h", 0, True, False), (256, 512, "o", 0, False, True),
                    (512, 1024, "o", 256, True, True),
                    (1024, 1280, "o", 768, True, False), (1280, 1536, "h", 256, False, True)]
            for (pst, Bp, slot) in ((k_ps, B_kps, 2), (v_ps, B_vps, 3)):
                fns = []
                for (a, b, kind, rc, bf, bl) in segs:
                    if kind == "o":
                        rhs = xn[:, c * TOK + rc: c * TOK + rc + (b - a)]
                    else:
                        rhs = xnh[:, rc: rc + (b - a)]
                    fns.append(lambda e, pst=pst, a=a, b=b, rhs=rhs, slot=slot, c=c,
                               bf=bf, bl=bl: e.matmul(
                        pst[:, a:b], wp[slot][:, c * 128:(c + 1) * 128], rhs,
                        start=(c == 0 and bf), stop=(c == NCH - 1 and bl)))
                P.emit("tensor", fns, reads=[B_xn[c], B_xnh, B_wp[slot], B_const],
                       writes=[Bp], incs=[("te", 1)])

        dma("sync", "ds", coskb, ap["cosk"], writes=[B_ck])
        dma("sync", "ds", sinkb, ap["sink"], writes=[B_sk])
        P.emit("scalar", lambda e: e.copy(out=kraw, in_=k_ps[:]),
               reads=[B_kps], writes=[B_kraw], incs=[("sc", 1)])
        P.emit("vector", lambda e: e.tensor_tensor(out=Kr[:], in0=kraw, in1=coskb, op=ALU.mult),
               reads=[B_kraw, B_ck], writes=[B_Kr], incs=[("ve", 1)])
        for s in range(3):
            P.emit("tensor", lambda e, s=s: e.matmul(pkp[:], permT[:],
                   kraw[:, s * 512:(s + 1) * 512], start=True, stop=True),
                   reads=[B_kraw, B_const], writes=[B_pkp], incs=[("te", 1)])
            P.emit("vector", lambda e, s=s: e.tensor_tensor(out=xb[:, 0:512], in0=pkp[:],
                   in1=sinkb[:, s * 512:(s + 1) * 512], op=ALU.mult),
                   reads=[B_pkp, B_sk], writes=[B_xb], incs=[("ve", 1)])
            P.emit("vector", lambda e, s=s: e.tensor_tensor(out=Kr[:, s * 512:(s + 1) * 512],
                   in0=Kr[:, s * 512:(s + 1) * 512], in1=xb[:, 0:512], op=ALU.add),
                   reads=[B_Kr, B_xb], writes=[B_Kr], incs=[("ve", 1)])
        P.emit("scalar", lambda e: e.copy(out=kraw, in_=v_ps[:]),
               reads=[B_vps], writes=[B_kraw], incs=[("sc", 1)])
        for g in range(2):
            fns = [lambda e, t=t, g=g: e.transpose(vt_ps[:, t * 128:(t + 1) * 128],
                   kraw[:, (g * 6 + t) * 128:(g * 6 + t + 1) * 128], identb[:]) for t in range(6)]
            P.emit("tensor", fns, reads=[B_kraw, B_const], writes=[B_vtps], incs=[("te", 1)])
            P.emit("vector", lambda e, g=g: e.tensor_copy(out=V0[:, g * 768:(g + 1) * 768],
                   in_=vt_ps[:]), reads=[B_vtps], writes=[B_V0], incs=[("ve", 1)])

        P.join_all()
        psKV.close()
        st12.close()

        B_wqall = Buf()
        B_wqpart = [Buf() for _ in range(4)]
        for i in range(4):
            a0 = (24 + 4 * (i % 2)) * TOK
            Av = big2[:, a0: a0 + 4096].rearrange("p (h x) -> p h x", x=512)
            bbufs = [B_big[24 + 4 * (i % 2) + k] for k in range(4)]
            dma("gpsimd", "dg", big2[:, a0: a0 + 4096], ap["w1qsl"][i], writes=bbufs)
            P.emit("vector", lambda e, Av=Av: e.tensor_tensor(
                out=Av[:, :, 0:128], in0=Av[:, :, 0:128], in1=Av[:, :, 128:256], op=ALU.add),
                reads=bbufs, writes=bbufs, incs=[("ve", 1)])
            P.emit("vector", lambda e, Av=Av: e.tensor_tensor(
                out=Av[:, :, 256:384], in0=Av[:, :, 256:384], in1=Av[:, :, 384:512], op=ALU.add),
                reads=bbufs, writes=bbufs, incs=[("ve", 1)])
            P.emit("vector", lambda e, Av=Av: e.tensor_tensor(
                out=Av[:, :, 128:256], in0=Av[:, :, 0:128], in1=Av[:, :, 256:384], op=ALU.add),
                reads=bbufs, writes=bbufs, incs=[("ve", 1)])
            dma("sync", "ds", wqpart[:, :, i, :].rearrange("h p j -> p h j"),
                Av[:, :, 128:256], reads=bbufs, writes=[B_wqpart[i]])
        P.emit("gpsimd", lambda e: e.collective_compute(
            "AllGather", ALU.bypass, replica_groups=[[0, 1, 2, 3, 4, 5, 6, 7]],
            ins=[wqpart], outs=[wqall]), reads=B_wqpart, writes=[B_wqall],
            incs=[("cc", 1)])

        if KDEBUG:
            for c in range(NCH):
                dma("sync", "ds", dbg["dxn"].rearrange("(c p) t -> c p t", p=128)[c],
                    xn[:, c * TOK:(c + 1) * TOK], reads=[B_xn[c]])
            dma("sync", "ds", dbg["dkr"], Kr[:], reads=[B_Kr])

        st3 = ExitStack()
        qsb = st3.enter_context(nc.sbuf_tensor("qsb", [128, 1024], BF16))
        qr2 = [st3.enter_context(nc.sbuf_tensor(f"qr{i}", [128, 1024], BF16)) for i in range(2)]
        qtmp = st3.enter_context(nc.sbuf_tensor("qtmp", [128, 1024], BF16))
        rtmp = st3.enter_context(nc.sbuf_tensor("rtmp", [128, 1024], BF16))
        sig = st3.enter_context(nc.sbuf_tensor("sig", [128, 4096], BF16))
        csb = st3.enter_context(nc.sbuf_tensor("csb", [128, 512], BF16))
        cosq = st3.enter_context(nc.sbuf_tensor("cosq_s", [128, 1024], BF16))
        sinq = st3.enter_context(nc.sbuf_tensor("sinq_s", [128, 1024], BF16))
        coef = st3.enter_context(nc.sbuf_tensor("coef_s", [8, 1024], F32))
        csr = st3.enter_context(nc.sbuf_tensor("csr", [8, 512], F32))
        cs = st3.enter_context(nc.sbuf_tensor("cs", [8, 1024], BF16))
        B_qsb, B_qtmp, B_rtmp, B_sig, B_csb = Buf(), Buf(), Buf(), Buf(), Buf()
        B_qr2 = [Buf(), Buf()]
        B_csr = Buf()
        B_csA, B_csB = Buf(), Buf()
        dma("sync", "ds", cosq[:], ap["cosq"], writes=[B_const])
        dma("sync", "ds", sinq[:], ap["sinq"], writes=[B_const])
        dma("sync", "ds", coef[:], ap["coef"], writes=[B_const])

        ps3 = ExitStack()
        scp = ps3.enter_context(nc.psum_tensor("scp", [128, 512], F32))
        qg = ps3.enter_context(nc.psum_tensor("qg", [128, 512], F32))
        wob = [ps3.enter_context(nc.psum_tensor(f"wob{i}", [128, 512], F32)) for i in range(3)]
        denA = ps3.enter_context(nc.psum_tensor("denA", [8, 512], F32))
        denB = ps3.enter_context(nc.psum_tensor("denB", [8, 512], F32))
        csb_ps = ps3.enter_context(nc.psum_tensor("csb_ps", [128, 512], F32))
        B_scp, B_qg = Buf(), Buf()
        B_wob = [Buf() for _ in range(3)]
        B_denA, B_denB, B_csbps = Buf(), Buf(), Buf()

        WGEO = [(0, 256, 256), (0, 512, 0), (256, 512, 0), (512, 512, 0), (768, 256, 0)]
        WBANK = {1: 0, 0: 1, 2: 2, 3: 0, 4: 1}
        ROW_OF = {1: 0, 0: 1, 2: 2, 3: 0, 4: 1}
        DGRP = {1: 0, 0: 0, 2: 0, 3: 1, 4: 1}

        def emit_qgemm_filler(h):
            if h >= H0:
                return []
            slot = h % 2
            par = h % 2
            th = []
            if h + 1 < H0:
                th.append(lambda h=h: load_panel("w0q", h + 1, 128, (h + 1) % 2))
            for t in range(2):
                for g in range(8):
                    fns = []
                    for c in range(g * 4, g * 4 + 4):
                        fns.append(lambda e, c=c, t=t, slot=slot: e.matmul(
                            qg[:], wp[slot][:, c * 128:(c + 1) * 128],
                            xn[:, c * TOK + t * 512: c * TOK + (t + 1) * 512],
                            start=(c == 0), stop=(c == NCH - 1)))
                    th.append(lambda fns=fns, slot=slot: P.emit(
                        "tensor", fns, reads=[B_wp[slot], B_const] + B_xn,
                        writes=[B_qg], incs=[("te", 1)]))
                th.append(lambda t=t: P.emit("scalar",
                    lambda e, t=t: e.copy(out=qsb[:, t * 512:(t + 1) * 512], in_=qg[:]),
                    reads=[B_qg], writes=[B_qsb], incs=[("sc", 1)]))
            for t in range(2):
                th.append(lambda t=t: P.emit("tensor",
                    lambda e, t=t: e.matmul(qg[:], permT[:], qsb[:, t * 512:(t + 1) * 512],
                                            start=True, stop=True),
                    reads=[B_qsb, B_const], writes=[B_qg], incs=[("te", 1)]))
                th.append(lambda t=t, par=par: P.emit("vector",
                    lambda e, t=t, par=par: e.tensor_tensor(
                        out=qr2[par][:, t * 512:(t + 1) * 512], in0=qg[:],
                        in1=sinq[:, t * 512:(t + 1) * 512], op=ALU.mult),
                    reads=[B_qg, B_const], writes=[B_qr2[par]], incs=[("ve", 1)]))
            th.append(lambda par=par: P.emit("vector",
                lambda e: e.tensor_tensor(out=rtmp[:], in0=qsb[:], in1=cosq[:], op=ALU.mult),
                reads=[B_qsb, B_const], writes=[B_rtmp], incs=[("ve", 1)]))
            th.append(lambda par=par: P.emit("vector",
                lambda e, par=par: e.tensor_tensor(out=qr2[par][:], in0=qr2[par][:],
                                                   in1=rtmp[:], op=ALU.add),
                reads=[B_rtmp], writes=[B_qr2[par]], incs=[("ve", 1)]))
            return th

        def emit_unit(h, w, j, with_av, fill=None):
            par = h % 2
            qa, qn, g0 = WGEO[w]
            kc = 2 * w + j
            slotj = j if w not in (4, 0) else 4 + j
            dgrp = DGRP[w]
            dps, Bd = (denA, B_denA) if dgrp == 0 else (denB, B_denB)
            P.emit("tensor", lambda e, kc=kc, qa=qa, qn=qn, g0=g0, par=par: e.matmul(
                scp[:, g0:g0 + qn], Kr[:, kc * 128:(kc + 1) * 128],
                qr2[par][:, qa:qa + qn], start=True, stop=True),
                reads=[B_Kr, B_qr2[par]], writes=[B_scp], incs=[("te", 1)])
            P.emit("scalar", lambda e, slotj=slotj, kc=kc, g0=g0, qn=qn: e.activation(
                out=sig[:, slotj * 512 + g0: slotj * 512 + g0 + qn], in_=scp[:, g0:g0 + qn],
                func=AF.Sigmoid, bias=maskb[:, kc:kc + 1], scale=1.0),
                reads=[B_scp, B_const], writes=[B_sig], incs=[("sc", 1)])
            if fill is not None:
                fill(1)
            P.emit("tensor", lambda e, w=w, j=j, g0=g0, qn=qn, slotj=slotj, dps=dps: e.matmul(
                dps[:, g0:g0 + qn], basis[:, w * 8:(w + 1) * 8],
                sig[:, slotj * 512 + g0: slotj * 512 + g0 + qn],
                start=(w in (1, 3) and j == 0), stop=(w in (2, 4) and j == 3)),
                reads=[B_sig, B_const], writes=[Bd], incs=[("te", 1)])
            if with_av:
                emit_av(h, w, j)

        def emit_av(h, w, j):
            qa, qn, g0 = WGEO[w]
            kc = 2 * w + j
            slotj = j if w not in (4, 0) else 4 + j
            bk = WBANK[w]
            P.emit("tensor", lambda e, w=w, j=j, kc=kc, g0=g0, qn=qn, slotj=slotj, bk=bk: e.matmul(
                wob[bk][:, g0:g0 + qn], V0[:, kc * 128:(kc + 1) * 128],
                sig[:, slotj * 512 + g0: slotj * 512 + g0 + qn], start=(j == 0), stop=(j == 3)),
                reads=[B_sig, B_V0], writes=[B_wob[bk]], incs=[("te", 1)])

        def emit_recip(grp):
            dps, Bd = (denA, B_denA) if grp == 0 else (denB, B_denB)
            Bcs = B_csA if grp == 0 else B_csB
            o = grp * 512
            P.emit("vector", lambda e, dps=dps: e.tensor_scalar(out=csr[:], in0=dps[:],
                   scalar1=1e-30, scalar2=None, op0=ALU.max),
                   reads=[Bd], writes=[B_csr], incs=[("ve", 1)])
            P.emit("vector", lambda e: e.reciprocal(out=csr[:], in_=csr[:]),
                   reads=[B_csr], writes=[B_csr], incs=[("ve", 1)])
            P.emit("vector", lambda e, o=o: e.tensor_tensor(out=cs[:, o:o + 512],
                   in0=coef[:, o:o + 512], in1=csr[:], op=ALU.mult),
                   reads=[B_csr, B_const], writes=[Bcs], incs=[("ve", 1)])

        BROW = {0: ((0, 1), (0, 0)), 1: ((0, 0), (0, 2)),
                2: ((0, 2), (1, 0)), 3: ((1, 0), (1, 1))}

        def emit_blend(h, j):
            (fp, fr), (rp, rr) = BROW[j]
            Bf = B_csA if fp == 0 else B_csB
            Br = B_csA if rp == 0 else B_csB
            P.emit("tensor", [
                lambda e, fp=fp, fr=fr: e.matmul(csb_ps[:, 0:256],
                    rowselb[:, fr * 128:(fr + 1) * 128],
                    cs[0:8, fp * 512 + 256: fp * 512 + 512], start=True, stop=True),
                lambda e, rp=rp, rr=rr: e.matmul(csb_ps[:, 256:512],
                    rowselb[:, rr * 128:(rr + 1) * 128],
                    cs[0:8, rp * 512: rp * 512 + 256], start=True, stop=True),
            ], reads=[Bf, Br, B_const], writes=[B_csbps], incs=[("te", 1)])
            P.emit("scalar", lambda e: e.copy(out=csb[:], in_=csb_ps[:]),
                   reads=[B_csbps], writes=[B_csb], incs=[("sc", 1)])
            bf, br = WBANK[j], WBANK[j + 1]
            P.emit("vector", lambda e, bf=bf: e.tensor_tensor(out=qtmp[:, 0:256],
                   in0=wob[bf][:, 256:512], in1=csb[:, 0:256], op=ALU.mult),
                   reads=[B_wob[bf], B_csb], writes=[B_qtmp], incs=[("ve", 1)])
            P.emit("vector", lambda e, br=br: e.tensor_tensor(out=qtmp[:, 256:512],
                   in0=wob[br][:, 0:256], in1=csb[:, 256:512], op=ALU.mult),
                   reads=[B_wob[br], B_csb], writes=[B_qtmp], incs=[("ve", 1)])
            P.emit("vector", lambda e, j=j, h=h: e.tensor_tensor(
                   out=big2[:, h * TOK + j * 256: h * TOK + (j + 1) * 256],
                   in0=qtmp[:, 0:256], in1=qtmp[:, 256:512], op=ALU.add),
                   reads=[B_qtmp], writes=[B_big[h]], incs=[("ve", 1)])

        for th in emit_qgemm_filler(0):
            th()
        for h in range(H0):
            filler = list(emit_qgemm_filler(h + 1))
            fi = 0

            def fill(n=1):
                nonlocal fi
                for _ in range(n):
                    if fi < len(filler):
                        filler[fi]()
                        fi += 1

            for j in range(4):
                emit_unit(h, 1, j, with_av=False, fill=fill)
            if h > 0:
                emit_blend(h - 1, 2)
                fill(1)
                emit_blend(h - 1, 3)
                fill(1)
            for j in range(4):
                emit_av(h, 1, j)
            fill(1)
            for w in (0, 2):
                for j in range(4):
                    emit_unit(h, w, j, with_av=True, fill=fill)
            emit_recip(0)
            for j in range(4):
                emit_unit(h, 3, j, with_av=False, fill=fill)
            emit_blend(h, 0)
            fill(1)
            emit_blend(h, 1)
            fill(1)
            for j in range(4):
                emit_av(h, 3, j)
            fill(1)
            for j in range(4):
                emit_unit(h, 4, j, with_av=False, fill=fill)
            emit_recip(1)
            for j in range(4):
                emit_av(h, 4, j)
            fill(len(filler))
            if h == H0 - 1:
                emit_blend(h, 2)
                emit_blend(h, 3)

        P.join_all()
        ps3.close()
        st3.close()
        st23.close()

        if KDEBUG:
            for c in range(NCH):
                dma("sync", "ds", dbg["dao"].rearrange("(c p) t -> c p t", p=128)[c],
                    big2[:, c * TOK:(c + 1) * TOK], reads=[B_big[c]])

        def emit_gate_lora(layer):
            stg = ExitStack()
            g1 = stg.enter_context(nc.sbuf_tensor(f"g1_{layer}", [128, 2048], BF16))
            lg = stg.enter_context(nc.sbuf_tensor(f"lg_{layer}", [8, 1024], F32))
            lgT = stg.enter_context(nc.sbuf_tensor(f"lgT_{layer}", [128, 64], F32))
            tA = stg.enter_context(nc.sbuf_tensor(f"tA_{layer}", [128, 64], F32))
            tB = stg.enter_context(nc.sbuf_tensor(f"tB_{layer}", [128, 64], F32))
            tC = stg.enter_context(nc.sbuf_tensor(f"tC_{layer}", [128, 8], F32))
            tD = stg.enter_context(nc.sbuf_tensor(f"tD_{layer}", [128, 8], F32))
            wtk = stg.enter_context(nc.sbuf_tensor(f"wtk_{layer}", [128, 64], BF16))
            wfe = stg.enter_context(nc.sbuf_tensor(f"wfe_{layer}", [8, 1024], BF16))
            B_g1, B_lg, B_lgT = Buf(), Buf(), Buf()
            B_tA, B_tB, B_tC, B_tD, B_wtk, B_wfe = Buf(), Buf(), Buf(), Buf(), Buf(), Buf()

            psg = ExitStack()
            pg1 = psg.enter_context(nc.psum_tensor(f"pg1_{layer}", [128, 1024], F32))
            pg2 = psg.enter_context(nc.psum_tensor(f"pg2_{layer}", [128, 1024], F32))
            psc = psg.enter_context(nc.psum_tensor(f"psc_{layer}", [8, 1024], F32))
            B_psc = Buf()
            ptr = psg.enter_context(nc.psum_tensor(f"ptr_{layer}", [128, 64], F32))
            pwf = psg.enter_context(nc.psum_tensor(f"pwf_{layer}", [8, 1024], BF16))
            B_pg1, B_pg2, B_ptr, B_pwf = Buf(), Buf(), Buf(), Buf()

            def gemm(pst, Bp, slot, ncol=128, extra=None):
                fns = []
                for c in range(NCH):
                    for t in range(2):
                        fns.append(lambda e, c=c, t=t, slot=slot, ncol=ncol, pst=pst: e.matmul(
                            pst[:, t * 512:(t + 1) * 512] if ncol == 128 else
                            pst[0:ncol, t * 512:(t + 1) * 512],
                            wp[slot][:, c * ncol:(c + 1) * ncol],
                            xn[:, c * TOK + t * 512: c * TOK + (t + 1) * 512],
                            start=(c == 0), stop=(c == NCH - 1)))
                P.emit("tensor", fns, reads=[B_wp[slot], B_const] + B_xn,
                       writes=[Bp], incs=[("te", 1)])

            if layer == 0:
                for oc in range(2):
                    load_panel("w0fq", oc, 128, 0)
                    gemm(pg1, B_pg1, 0)
                    P.emit("scalar", lambda e: e.copy(out=g1[:, 0:1024], in_=pg1[:]),
                           reads=[B_pg1], writes=[B_g1], incs=[("sc", 1)])
                    load_panel("w0fk", oc, 128, 1)
                    gemm(pg2, B_pg2, 1)
                    P.emit("vector", lambda e: e.tensor_tensor(out=g1[:, 1024:2048],
                           in0=pg2[:], in1=g1[:, 0:1024], op=ALU.mult),
                           reads=[B_pg2, B_g1], writes=[B_g1], incs=[("ve", 1)])
                    fns = [lambda e, t=t, oc=oc: e.matmul(psc[0:8, t * 512:(t + 1) * 512],
                           blocksel[:, oc * 8:(oc + 1) * 8],
                           g1[:, 1024 + t * 512:1024 + (t + 1) * 512],
                           start=(oc == 0), stop=(oc == 1)) for t in range(2)]
                    P.emit("tensor", fns, reads=[B_g1, B_const], writes=[B_psc], incs=[("te", 1)])
                P.emit("scalar", lambda e: e.copy(out=lg[:], in_=psc[0:8, :]),
                       reads=[B_psc], writes=[B_lg], incs=[("sc", 1)])
            else:
                load_panel("gw1", 0, 8, 0)
                gemm(pg1, B_pg1, 0, ncol=8)
                P.emit("scalar", lambda e: e.copy(out=lg[:], in_=pg1[0:8, :]),
                       reads=[B_pg1], writes=[B_lg], incs=[("sc", 1)])

            fns = [lambda e, tc=tc: e.transpose(ptr[:, tc * 8:(tc + 1) * 8],
                   lg[0:8, tc * 128:(tc + 1) * 128], identf[:]) for tc in range(8)]
            P.emit("tensor", fns, reads=[B_lg, B_const], writes=[B_ptr], incs=[("te", 1)])
            P.emit("vector", lambda e: e.tensor_copy(out=lgT[:], in_=ptr[:]),
                   reads=[B_ptr], writes=[B_lgT], incs=[("ve", 1)])
            v3 = lgT[:].rearrange("p (a b) -> p a b", a=8)
            P.emit("vector", lambda e: e.reduce_max(out=tC[:], in_=v3, axis=AXX),
                   reads=[B_lgT], writes=[B_tC], incs=[("ve", 1)])
            if layer == 0:
                P.emit("vector", lambda e: e.tensor_tensor(out=tA[:], in0=lgT[:],
                       in1=tC[:].rearrange("p (a b) -> p a b", b=1).broadcast_to([128, 8, 8]), op=ALU.subtract),
                       reads=[B_lgT, B_tC], writes=[B_tA], incs=[("ve", 1)])
                P.emit("scalar", lambda e: e.activation(out=tA[:], in_=tA[:], func=AF.Exp,
                       scale=float(1.0 / np.sqrt(32.0))),
                       reads=[B_tA], writes=[B_tA], incs=[("sc", 1)])
                P.emit("vector", lambda e: e.reduce_sum(out=tC[:], in_=tA[:].rearrange(
                       "p (a b) -> p a b", a=8), axis=AXX),
                       reads=[B_tA], writes=[B_tC], incs=[("ve", 1)])
                P.emit("vector", lambda e: e.reciprocal(out=tD[:], in_=tC[:]),
                       reads=[B_tC], writes=[B_tD], incs=[("ve", 1)])
                P.emit("vector", lambda e: e.tensor_tensor(out=wtk[:], in0=tA[:],
                       in1=tD[:].rearrange("p (a b) -> p a b", b=1).broadcast_to([128, 8, 8]), op=ALU.mult),
                       reads=[B_tA, B_tD], writes=[B_wtk], incs=[("ve", 1)])
            else:
                P.emit("vector", lambda e: e.tensor_tensor(out=tA[:], in0=lgT[:],
                       in1=tC[:].rearrange("p (a b) -> p a b", b=1).broadcast_to([128, 8, 8]), op=ALU.is_equal),
                       reads=[B_lgT, B_tC], writes=[B_tA], incs=[("ve", 1)])
                P.emit("vector", lambda e: e.scalar_tensor_tensor(out=tA[:], in0=tA[:],
                       scalar=NEG, in1=lgT[:], op0=ALU.mult, op1=ALU.add),
                       reads=[B_tA, B_lgT], writes=[B_tA], incs=[("ve", 1)])
                P.emit("vector", lambda e: e.reduce_max(out=tD[:], in_=tA[:].rearrange(
                       "p (a b) -> p a b", a=8), axis=AXX),
                       reads=[B_tA], writes=[B_tD], incs=[("ve", 1)])
                P.emit("vector", lambda e: e.tensor_tensor(out=tA[:], in0=lgT[:],
                       in1=tC[:].rearrange("p (a b) -> p a b", b=1).broadcast_to([128, 8, 8]), op=ALU.subtract),
                       reads=[B_lgT, B_tC], writes=[B_tA], incs=[("ve", 1)])
                P.emit("scalar", lambda e: e.activation(out=tA[:], in_=tA[:], func=AF.Exp),
                       reads=[B_tA], writes=[B_tA], incs=[("sc", 1)])
                P.emit("vector", lambda e: e.tensor_tensor(out=tB[:], in0=lgT[:],
                       in1=tD[:].rearrange("p (a b) -> p a b", b=1).broadcast_to([128, 8, 8]), op=ALU.is_ge),
                       reads=[B_lgT, B_tD], writes=[B_tB], incs=[("ve", 1)])
                P.emit("vector", lambda e: e.tensor_tensor(out=tD[:], in0=tD[:], in1=tC[:],
                       op=ALU.subtract), reads=[B_tD, B_tC], writes=[B_tD], incs=[("ve", 1)])
                P.emit("scalar", lambda e: e.activation(out=tD[:], in_=tD[:], func=AF.Exp),
                       reads=[B_tD], writes=[B_tD], incs=[("sc", 1)])
                P.emit("vector", lambda e: e.tensor_scalar(out=tD[:], in0=tD[:], scalar1=1.0,
                       scalar2=None, op0=ALU.add), reads=[B_tD], writes=[B_tD], incs=[("ve", 1)])
                P.emit("vector", lambda e: e.reciprocal(out=tD[:], in_=tD[:]),
                       reads=[B_tD], writes=[B_tD], incs=[("ve", 1)])
                P.emit("vector", lambda e: e.tensor_tensor(out=tA[:], in0=tA[:], in1=tB[:],
                       op=ALU.mult), reads=[B_tA, B_tB], writes=[B_tA], incs=[("ve", 1)])
                P.emit("vector", lambda e: e.tensor_tensor(out=wtk[:], in0=tA[:],
                       in1=tD[:].rearrange("p (a b) -> p a b", b=1).broadcast_to([128, 8, 8]), op=ALU.mult),
                       reads=[B_tA, B_tD], writes=[B_wtk], incs=[("ve", 1)])

            fns = [lambda e, tc=tc: e.transpose(pwf[:, tc * 128:(tc + 1) * 128],
                   wtk[:, tc * 8:(tc + 1) * 8], identb[:]) for tc in range(8)]
            P.emit("tensor", fns, reads=[B_wtk, B_const], writes=[B_pwf], incs=[("te", 1)])
            P.emit("scalar", lambda e: e.copy(out=wfe[:], in_=pwf[:]),
                   reads=[B_pwf], writes=[B_wfe], incs=[("sc", 1)])
            exp = expand1 if layer == 0 else expand2
            fns = [lambda e, t=t, exp=exp: e.matmul(pg1[:, t * 512:(t + 1) * 512], exp[:],
                   wfe[:, t * 512:(t + 1) * 512], start=True, stop=True) for t in range(2)]
            P.emit("tensor", fns, reads=[B_wfe, B_const], writes=[B_pg1], incs=[("te", 1)])
            load_panel("a0" if layer == 0 else "la1", 0, 128, 0)
            gemm(pg2, B_pg2, 0)
            P.emit("scalar", lambda e: e.copy(out=g1[:, 0:1024], in_=pg2[:]),
                   reads=[B_pg2], writes=[B_g1], incs=[("sc", 1)])
            P.emit("vector", lambda e: e.tensor_tensor(out=hw[:], in0=pg1[:], in1=g1[:, 0:1024],
                   op=ALU.mult), reads=[B_pg1, B_g1], writes=[B_hw], incs=[("ve", 1)])
            P.join_all(exclude=("cc",))
            psg.close()
            stg.close()

        emit_gate_lora(0)
        if KDEBUG:
            dma("sync", "ds", dbg["dhw"], hw[:], reads=[B_hw])

        x1d_t = x1d.rearrange("(c p) t -> c p t", p=128)
        x1b_t = x1b.rearrange("(c p) t -> c p t", p=128)
        y_t = y.rearrange("(c p) t -> c p t", p=128)

        def emit_epilogue(layer):
            is0 = layer == 0
            wo_name = "w0o" if is0 else "w1o"
            wf_name = "w0f" if is0 else "bw1"
            bias_sb = b0sb if is0 else b1sb
            nsc = NCH if is0 else 8
            ste = ExitStack()
            bml = ste.enter_context(nc.sbuf_tensor(f"bml_{layer}", [128, 4096], BF16))
            gsb = ste.enter_context(nc.sbuf_tensor(f"gsb_{layer}", [128, 1024], BF16))
            tsb = ste.enter_context(nc.sbuf_tensor(f"tsb_{layer}", [128, 1024], BF16))
            xob = [ste.enter_context(nc.sbuf_tensor(f"xob{layer}_{i}", [128, 1024], F32)) for i in range(2)]
            B_bml, B_gsb, B_tsb = Buf(), Buf(), Buf()
            B_xob = [Buf(), Buf()]
            ret = None
            pse = ExitStack()
            s_ps = pse.enter_context(nc.psum_tensor(f"s_ps_{layer}", [128, 1024], F32))
            g_ps = pse.enter_context(nc.psum_tensor(f"g_ps_{layer}", [128, 1024], F32))
            B_sps, B_gps = Buf(), Buf()
            if is0:
                xb1 = ste.enter_context(nc.sbuf_tensor(f"xb1_{layer}", [128, 1024], BF16))
                sq1 = ste.enter_context(nc.sbuf_tensor(f"sq1_{layer}", [128, 1024], BF16))
                B_xb1, B_sq1 = Buf(), Buf()
                sx1 = pse.enter_context(nc.psum_tensor(f"sx1_{layer}", [1, 1024], F32))
                sx21 = pse.enter_context(nc.psum_tensor(f"sx21_{layer}", [1, 1024], F32))
                B_sx1, B_sx21 = Buf(), Buf()
                ret = (ste, pse, sx1, sx21, B_sx1, B_sx21, s_ps, B_sps)

            dma("gpsimd", "dg", bml[:], ap["bm0" if is0 else "lb1"], writes=[B_bml])
            load_panel(wo_name, 0, 128, 0, nch=nsc)
            if is0:
                load_panel(wf_name, 0, 128, 2)
            for oc in range(NCH):
                so, sf = oc % 2, 2 + (oc % 2)
                if oc + 1 < NCH:
                    load_panel(wo_name, oc + 1, 128, (oc + 1) % 2, nch=nsc)
                    if is0 or oc + 1 >= 8:
                        load_panel(wf_name, oc + 1, 128, 2 + ((oc + 1) % 2))
                po = oc % 2
                if is0:
                    dma("sync", "ds", xob[po][:], xloc_t[oc][:, 256:1280], writes=[B_xob[po]])
                else:
                    dma("sync", "ds", xob[po][:], x1d_t[oc], reads=[B_x1d[oc]], writes=[B_xob[po]])
                fns = []
                for c in range(nsc):
                    for t in range(2):
                        fns.append(lambda e, c=c, t=t, so=so: e.matmul(
                            s_ps[:, t * 512:(t + 1) * 512], wp[so][:, c * 128:(c + 1) * 128],
                            big2[:, c * TOK + t * 512: c * TOK + (t + 1) * 512],
                            start=(c == 0), stop=(c == nsc - 1)))
                P.emit("tensor", fns, reads=[B_wp[so], B_const] + B_big[:nsc],
                       writes=[B_sps], incs=[("te", 1)])
                gpre = (not is0) and oc < 8
                if not gpre:
                    fns = []
                    for c in range(NCH):
                        for t in range(2):
                            fns.append(lambda e, c=c, t=t, sf=sf: e.matmul(
                                g_ps[:, t * 512:(t + 1) * 512], wp[sf][:, c * 128:(c + 1) * 128],
                                xn[:, c * TOK + t * 512: c * TOK + (t + 1) * 512],
                                start=(c == 0), stop=False))
                    for t in range(2):
                        fns.append(lambda e, t=t, oc=oc: e.matmul(
                            g_ps[:, t * 512:(t + 1) * 512], bml[:, oc * 128:(oc + 1) * 128],
                            hw[:, t * 512:(t + 1) * 512], start=False, stop=True))
                    P.emit("tensor", fns, reads=[B_wp[sf], B_bml, B_hw, B_const] + B_xn,
                           writes=[B_gps], incs=[("te", 1)])
                    P.emit("scalar", lambda e, oc=oc: e.activation(out=gsb[:], in_=g_ps[:],
                           func=AF.Silu, bias=bias_sb[:, oc:oc + 1], scale=1.0),
                           reads=[B_gps, B_const], writes=[B_gsb], incs=[("sc", 1)])
                else:
                    P.emit("scalar", lambda e, oc=oc: e.activation(out=gsb[:],
                           in_=big2[:, (24 + oc) * TOK:(25 + oc) * TOK],
                           func=AF.Silu, bias=bias_sb[:, oc:oc + 1], scale=1.0),
                           reads=[B_big[24 + oc], B_const], writes=[B_gsb], incs=[("sc", 1)])
                P.emit("vector", lambda e: e.tensor_tensor(out=tsb[:], in0=s_ps[:], in1=gsb[:],
                       op=ALU.mult), reads=[B_sps, B_gsb], writes=[B_tsb], incs=[("ve", 1)])
                P.emit("gpsimd", lambda e, po=po: e.tensor_tensor(out=xob[po][:], in0=xob[po][:],
                       in1=tsb[:], op=ALU.add),
                       reads=[B_xob[po], B_tsb], writes=[B_xob[po]], incs=[("gp", 1)])
                if is0:
                    dma("sync", "ds", x1d_t[oc], xob[po][:], reads=[B_xob[po]], writes=[B_x1d[oc]])
                    P.emit("vector", lambda e, po=po: e.tensor_copy(out=xb1[:], in_=xob[po][:]),
                           reads=[B_xob[po]], writes=[B_xb1], incs=[("ve", 1)])
                    dma("sync", "ds", x1b_t[oc], xb1[:], reads=[B_xb1], writes=[B_x1b[oc]])
                    P.emit("vector", lambda e: e.tensor_tensor(out=sq1[:], in0=xb1[:], in1=xb1[:],
                           op=ALU.mult), reads=[B_xb1], writes=[B_sq1], incs=[("ve", 1)])
                    for (src, Bs, pst, Bp) in ((xb1, B_xb1, sx1, B_sx1), (sq1, B_sq1, sx21, B_sx21)):
                        fns = [lambda e, src=src, pst=pst, t=t, oc=oc: e.matmul(
                            pst[0:1, t * 512:(t + 1) * 512], ones1b[:], src[:, t * 512:(t + 1) * 512],
                            start=(oc == 0), stop=(oc == NCH - 1)) for t in range(2)]
                        P.emit("tensor", fns, reads=[Bs, B_const], writes=[Bp], incs=[("te", 1)])
                else:
                    dma("sync", "ds", y_t[oc], xob[po][:], reads=[B_xob[po]], writes=[B_y])
            if not is0:
                P.join_all()
                pse.close()
                ste.close()
            return ret

        st6 = ExitStack()
        rstdb1 = st6.enter_context(nc.sbuf_tensor("rstdb1", [128, 1024], F32))
        negmrb1 = st6.enter_context(nc.sbuf_tensor("negmrb1", [128, 1024], F32))

        (ste0, pse0, sx1, sx21, B_sx1, B_sx21, sps0, B_sps0) = emit_epilogue(0)

        B_stat1, B_rstdb1, B_negmrb1 = Buf(), Buf(), Buf()
        B_xscr6, B_xscr6b = Buf(), Buf()
        emit_stats_finalize(B_stat1, sx1, sx21, B_sx1, B_sx21,
                            rstdb1, negmrb1, B_rstdb1, B_negmrb1, 1024,
                            sps0[:, 0:512], B_sps0)
        snap6 = [(s, P.cnt[s]) for s in P.cnt if P.cnt[s] > 0]
        pse0.close()
        ste0.close()

        st6b = ExitStack()
        xscr6 = st6b.enter_context(nc.sbuf_tensor("xscr6", [128, 1024], F32))
        xscr6b = st6b.enter_context(nc.sbuf_tensor("xscr6b", [128, 1024], F32))
        x6 = [st6b.enter_context(nc.sbuf_tensor(f"x6_{i}", [128, 1024], BF16)) for i in range(4)]
        B_x6 = [Buf() for _ in range(4)]

        st7 = ExitStack()
        kevac = st7.enter_context(nc.sbuf_tensor("kevac", [128, 1024], BF16))
        B_kevac = Buf()
        ps7 = ExitStack()
        p7 = ps7.enter_context(nc.psum_tensor("p7", [128, 1024], F32))
        B_p7 = Buf()
        kvs_t = kvs.rearrange("(c p) t -> c p t", p=128)
        QS = 8 * TOK

        jobs = [("w1k", oc) for oc in range(8)] + [("w1v", oc) for oc in range(8)]

        def load_job(j):
            wname, oc = jobs[j]
            dma("gpsimd", "dg", wp[j % 4][:, 0:NCH * 128], ap[wname][oc],
                writes=[B_wp[j % 4]])

        for j in range(3):
            load_job(j)

        for c in range(NCH):
            p = c % 4
            eng, sem = ("vector", "ve") if c % 2 == 0 else ("gpsimd", "gp")
            scr, Bscr = (xscr6, B_xscr6) if c % 2 == 0 else (xscr6b, B_xscr6b)
            w6 = snap6 if c < 4 else ()
            dma("sync", "ds", x6[p][:], x1b_t[c], reads=[B_x1b[c]], writes=[B_x6[p]],
                waits=w6)
            P.emit(eng, lambda e, p=p, scr=scr: e.tensor_tensor(out=scr[:], in0=x6[p][:],
                   in1=rstdb1[:], op=ALU.mult),
                   reads=[B_x6[p], B_rstdb1], writes=[Bscr], incs=[(sem, 1)], waits=w6)
            P.emit(eng, lambda e, c=c, scr=scr: e.tensor_tensor(out=xn[:, c * TOK:(c + 1) * TOK],
                   in0=scr[:], in1=negmrb1[:], op=ALU.add),
                   reads=[Bscr, B_negmrb1], writes=[B_xn[c]], incs=[(sem, 1)])
        if KDEBUG:
            for c in range(NCH):
                dma("sync", "ds", dbg["dxn1"].rearrange("(c p) t -> c p t", p=128)[c],
                    xn[:, c * TOK:(c + 1) * TOK], reads=[B_xn[c]])

        for j, (wname, oc) in enumerate(jobs):
            slot = j % 4
            if j + 3 < len(jobs):
                load_job(j + 3)
            if j < 2:
                for c in range(NCH):
                    fns = [lambda e, c=c, t=t, slot=slot: e.matmul(
                        p7[:, t * 512:(t + 1) * 512], wp[slot][:, c * 128:(c + 1) * 128],
                        xn[:, c * TOK + t * 512: c * TOK + (t + 1) * 512],
                        start=(c == 0), stop=(c == NCH - 1)) for t in range(2)]
                    wb = [B_p7] if c in (0, NCH - 1) else []
                    P.emit("tensor", fns, reads=[B_wp[slot], B_xn[c]], writes=wb,
                           incs=[("te", 1)],
                           waits=(snap6 if j == 0 and c == 0 else ()))
            else:
                fns = []
                for c in range(NCH):
                    for t in range(2):
                        fns.append(lambda e, c=c, t=t, slot=slot: e.matmul(
                            p7[:, t * 512:(t + 1) * 512], wp[slot][:, c * 128:(c + 1) * 128],
                            xn[:, c * TOK + t * 512: c * TOK + (t + 1) * 512],
                            start=(c == 0), stop=(c == NCH - 1)))
                P.emit("tensor", fns, reads=[B_wp[slot], B_const] + B_xn,
                       writes=[B_p7], incs=[("te", 1)])
            P.emit("scalar", lambda e: e.copy(out=kevac[:], in_=p7[:]),
                   reads=[B_p7], writes=[B_kevac], incs=[("sc", 1)],
                   waits=(snap6 if j == 0 else ()))
            dma("sync", "ds", kvs_t[j], kevac[:], reads=[B_kevac], writes=[B_kvs])
        _kvw = [(f"ds{i}", P.cnt[f"ds{i}"]) for i in range(NLANES["ds"])]
        P.emit("gpsimd", lambda e: e.collective_compute(
            "AllGather", ALU.bypass, replica_groups=[[0, 1], [2, 3], [4, 5], [6, 7]],
            ins=[kvs], outs=[kva]), reads=[B_kvs], writes=[B_kva], incs=[("cc", 1)],
            waits=_kvw)
        P.join_all(exclude=("cc",))
        ps7.close()

        emit_gate_lora(1)

        ps7 = ExitStack()
        p7 = ps7.enter_context(nc.psum_tensor("p7b", [128, 1024], F32))
        B_p7 = Buf()

        def load_wqs_panel(oc, slot):
            src = wqall[:, oc].rearrange("r p i j -> p r (i j)")
            dstv = wp[slot][:, 0:NCH * 128].rearrange("p (r y) -> p r y", y=512)
            dma("sync", "ds", dstv, src, reads=[B_wqall], writes=[B_wp[slot]])

        load_wqs_panel(0, 0)
        for oc in range(8):
            slot = oc % 2
            if oc + 1 < 8:
                load_wqs_panel(oc + 1, (oc + 1) % 2)
            fns = []
            for c in range(NCH):
                for t in range(2):
                    fns.append(lambda e, c=c, t=t, slot=slot: e.matmul(
                        p7[:, t * 512:(t + 1) * 512], wp[slot][:, c * 128:(c + 1) * 128],
                        xn[:, c * TOK + t * 512: c * TOK + (t + 1) * 512],
                        start=(c == 0), stop=(c == NCH - 1)))
            P.emit("tensor", fns, reads=[B_wp[slot], B_const] + B_xn,
                   writes=[B_p7], incs=[("te", 1)])
            P.emit("scalar", lambda e, oc=oc: e.copy(
                   out=big2[:, QS + oc * TOK: QS + (oc + 1) * TOK], in_=p7[:]),
                   reads=[B_p7], writes=[B_big[8 + oc]], incs=[("sc", 1)])
        P.join_all(exclude=("cc",))
        ps7.close()
        st7.close()
        st6b.close()
        st6.close()
        if KDEBUG:
            for c in range(8):
                dma("sync", "ds", dbg["dqs"].rearrange("(c p) t -> c p t", p=128)[c],
                    big2[:, QS + c * TOK: QS + (c + 1) * TOK], reads=[B_big[8 + c]])

        if KDEBUG:
            dma("sync", "ds", dbg["dhw1"], hw[:], reads=[B_hw])

        st9 = ExitStack()
        Kh2 = [st9.enter_context(nc.sbuf_tensor(f"Kh{i}", [128, 2048], BF16)) for i in range(2)]
        Vh2 = [st9.enter_context(nc.sbuf_tensor(f"Vh{i}", [128, 2048], BF16)) for i in range(2)]
        Vtm2 = [st9.enter_context(nc.sbuf_tensor(f"Vtm{i}", [128, 2048], BF16)) for i in range(2)]
        rdenS = st9.enter_context(nc.sbuf_tensor("rdenS", [128, 512], F32))
        bm9 = st9.enter_context(nc.sbuf_tensor("bm9", [128, 4096], BF16))
        B_bm9 = Buf()
        B_Kh2 = [Buf(), Buf()]
        B_Vh2 = [Buf(), Buf()]
        B_Vtm2 = [Buf(), Buf()]
        B_rden = Buf()
        PTB = 16 * TOK
        B_pt = [Buf() for _ in range(8)]
        ps9 = ExitStack()
        sc9 = [ps9.enter_context(nc.psum_tensor(f"sc9_{i}", [128, 512], F32)) for i in range(3)]
        av9 = ps9.enter_context(nc.psum_tensor("av9", [128, 512], F32))
        den9 = ps9.enter_context(nc.psum_tensor("den9", [128, 512], F32))
        vt9 = ps9.enter_context(nc.psum_tensor("vt9", [128, 1024], BF16))
        g9 = ps9.enter_context(nc.psum_tensor("g9", [128, 512], F32))
        B_sc9 = [Buf() for _ in range(3)]
        B_av9, B_den9, B_vt9, B_g9 = Buf(), Buf(), Buf(), Buf()

        def emit_g_filler():
            th = []
            th.append(lambda: dma("gpsimd", "dg", bm9[:], ap["lb1"], writes=[B_bm9]))
            th.append(lambda: load_panel("bw1", 0, 128, 0))
            for oc in range(8):
                slot = oc % 2
                if oc + 1 < 8:
                    th.append(lambda oc=oc: load_panel("bw1", oc + 1, 128, (oc + 1) % 2))
                for t in range(2):
                    for g in range(8):
                        fns = []
                        for c in range(g * 4, g * 4 + 4):
                            fns.append(lambda e, c=c, t=t, slot=slot: e.matmul(
                                g9[:], wp[slot][:, c * 128:(c + 1) * 128],
                                xn[:, c * TOK + t * 512: c * TOK + (t + 1) * 512],
                                start=(c == 0), stop=False))
                        th.append(lambda fns=fns, slot=slot: P.emit("tensor", fns,
                            reads=[B_wp[slot], B_const] + B_xn, writes=[B_g9], incs=[("te", 1)]))
                    th.append(lambda t=t, oc=oc: P.emit("tensor",
                        lambda e, t=t, oc=oc: e.matmul(g9[:], bm9[:, oc * 128:(oc + 1) * 128],
                            hw[:, t * 512:(t + 1) * 512], start=False, stop=True),
                        reads=[B_bm9, B_hw, B_const], writes=[B_g9], incs=[("te", 1)]))
                    th.append(lambda t=t, oc=oc: P.emit("scalar", lambda e, t=t, oc=oc: e.copy(
                        out=big2[:, (24 + oc) * TOK + t * 512:(24 + oc) * TOK + (t + 1) * 512],
                        in_=g9[:]), reads=[B_g9], writes=[B_big[24 + oc]], incs=[("sc", 1)]))
            return th

        gfill = emit_g_filler()
        gfi = 0

        def gf(n=1):
            nonlocal gfi
            for _ in range(n):
                if gfi < len(gfill):
                    gfill[gfi]()
                    gfi += 1

        def emit_kv_load(h):
            hp = h % 2
            for r in range(2):
                dma("sync", "ds", Kh2[hp][:, r * 1024:(r + 1) * 1024],
                    kva[r, h * 128:(h + 1) * 128, :], reads=[B_kva], writes=[B_Kh2[hp]],
                    waits=[("cc", P.cnt["cc"])])
                dma("sync", "ds", Vh2[hp][:, r * 1024:(r + 1) * 1024],
                    kva[r, 1024 + h * 128: 1024 + (h + 1) * 128, :],
                    reads=[B_kva], writes=[B_Vh2[hp]], waits=[("cc", P.cnt["cc"])])

        pendfin = [None]

        def emit_fin():
            if pendfin[0] is None:
                return
            h, qgv = pendfin[0]
            pendfin[0] = None
            P.emit("vector", lambda e: e.reciprocal(out=rdenS[:], in_=den9[:]),
                   reads=[B_den9], writes=[B_rden], incs=[("ve", 1)])
            P.emit("vector", lambda e, h=h, qgv=qgv: e.tensor_tensor(
                   out=big2[:, h * TOK + qgv * 512: h * TOK + (qgv + 1) * 512],
                   in0=av9[:], in1=rdenS[:], op=ALU.mult),
                   reads=[B_av9, B_rden], writes=[B_big[h]], incs=[("ve", 1)])

        emit_kv_load(0)
        for h in range(NKV):
            hp = h % 2
            Kh, Vh, Vtm = Kh2[hp], Vh2[hp], Vtm2[hp]
            B_Kh, B_Vh, B_Vtm = B_Kh2[hp], B_Vh2[hp], B_Vtm2[hp]
            if h + 1 < NKV:
                emit_kv_load(h + 1)
            for g in range(2):
                fns = [lambda e, g=g, t=t, Vh=Vh: e.transpose(vt9[:, t * 128:(t + 1) * 128],
                       Vh[:, (g * 8 + t) * 128:(g * 8 + t + 1) * 128], identb[:]) for t in range(8)]
                P.emit("tensor", fns, reads=[B_Vh, B_const], writes=[B_vt9], incs=[("te", 1)])
                P.emit("vector", lambda e, g=g, Vtm=Vtm: e.tensor_copy(
                       out=Vtm[:, g * 1024:(g + 1) * 1024],
                       in_=vt9[:]), reads=[B_vt9], writes=[B_Vtm], incs=[("ve", 1)])
                gf(1)
            for qgv in range(2):
                Qs = big2[:, QS + h * TOK + qgv * 512: QS + h * TOK + (qgv + 1) * 512]

                def s_mm(kc, Kh=Kh, B_Kh=B_Kh, Qs=Qs, h=h):
                    P.emit("tensor", lambda e, kc=kc, Kh=Kh, Qs=Qs: e.matmul(
                        sc9[kc % 3][:], Kh[:, kc * 128:(kc + 1) * 128], Qs,
                        start=True, stop=True),
                        reads=[B_Kh, B_big[8 + h]], writes=[B_sc9[kc % 3]], incs=[("te", 1)])

                def e_act(kc):
                    P.emit("scalar", lambda e, kc=kc: e.activation(
                        out=big2[:, PTB + (kc % 8) * 512: PTB + (kc % 8 + 1) * 512],
                        in_=sc9[kc % 3][:], func=AF.Exp, scale=SCALE),
                        reads=[B_sc9[kc % 3]], writes=[B_pt[kc % 8]], incs=[("sc", 1)])

                for kc in range(3):
                    s_mm(kc)
                    e_act(kc)
                emit_fin()
                for kc in range(16):
                    pt = big2[:, PTB + (kc % 8) * 512: PTB + (kc % 8 + 1) * 512]
                    P.emit("tensor", lambda e, pt=pt, kc=kc, Vtm=Vtm: e.matmul(
                        av9[:], Vtm[:, kc * 128:(kc + 1) * 128], pt,
                        start=(kc == 0), stop=(kc == 15)),
                        reads=[B_pt[kc % 8], B_Vtm], writes=[B_av9], incs=[("te", 1)])
                    P.emit("tensor", lambda e, pt=pt, kc=kc: e.matmul(
                        den9[:], ones128[:], pt, start=(kc == 0), stop=(kc == 15)),
                        reads=[B_pt[kc % 8], B_const], writes=[B_den9], incs=[("te", 1)])
                    if kc + 3 < 16:
                        s_mm(kc + 3)
                        e_act(kc + 3)
                    gf(1)
                pendfin[0] = (h, qgv)
        emit_fin()
        gf(len(gfill))

        P.join_all()
        ps9.close()
        st9.close()
        if KDEBUG:
            for c in range(8):
                dma("sync", "ds", dbg["dval"].rearrange("(c p) t -> c p t", p=128)[c],
                    big2[:, c * TOK:(c + 1) * TOK], reads=[B_big[c]])
            for c in range(NCH):
                dma("sync", "ds", dbg["dx1"].rearrange("(c p) t -> c p t", p=128)[c],
                    x1d_t[c], reads=[B_x1d[c]])

        emit_epilogue(1)

        P.join("sync")

        with nc.Block() as block:
            prog.replay(block)
    return nc


def _panelize(wt, ncol=128):
    din, o = wt.shape
    nch, npan = din // 128, o // ncol
    return np.ascontiguousarray(
        wt.reshape(nch, 128, npan, ncol).transpose(2, 1, 0, 3).reshape(npan, 128, nch * ncol))


def kernel(**inputs):
    f32 = np.float32
    x = np.asarray(inputs["L0_attn_x"], f32)
    W = {}
    W["w0q"] = _panelize(np.asarray(inputs["L0_attn_Wq"], f32).T)
    W["w0k"] = _panelize(np.asarray(inputs["L0_attn_Wk"], f32).T)
    W["w0v"] = _panelize(np.asarray(inputs["L0_attn_Wv"], f32).T)
    W["w0o"] = _panelize(np.asarray(inputs["L0_attn_Wo"], f32).T)
    W["w0f"] = _panelize(np.asarray(inputs["L0_ffn_W"], f32).T)
    W["w0fq"] = _panelize(np.asarray(inputs["L0_ffn_Wq"], f32).T)
    W["w0fk"] = _panelize(np.asarray(inputs["L0_ffn_Wk"], f32).T)
    W["a0"] = _panelize(np.asarray(inputs["L0_ffn_A"], f32).reshape(128, D).T)
    W["bm0"] = np.ascontiguousarray(np.asarray(inputs["L0_ffn_Bm"], f32).transpose(0, 2, 1).reshape(128, D))
    w1qf = np.asarray(inputs["L1_attn_wq"], f32)
    W["w1k"] = _panelize(np.asarray(inputs["L1_attn_wk"], f32))
    W["w1v"] = _panelize(np.asarray(inputs["L1_attn_wv"], f32))
    W["w1o"] = _panelize(np.asarray(inputs["L1_attn_wo"], f32))
    W["gw1"] = _panelize(np.asarray(inputs["L1_ffn_gate_w"], f32).T, ncol=8)
    W["bw1"] = _panelize(np.asarray(inputs["L1_ffn_base_w"], f32).T)
    W["la1"] = _panelize(np.asarray(inputs["L1_ffn_lora_A"], f32).reshape(128, D).T)
    W["lb1"] = np.ascontiguousarray(np.asarray(inputs["L1_ffn_lora_B"], f32).transpose(0, 2, 1).reshape(128, D))
    b0 = np.ascontiguousarray(np.asarray(inputs["L0_ffn_b"], f32).reshape(32, 128).T)
    b1 = np.ascontiguousarray(np.asarray(inputs["L1_ffn_base_b"], f32).reshape(32, 128).T)

    consts = [_consts_for(0), _consts_for(1)]
    in_maps = []
    for core in range(8):
        b, half = core // 2, core % 2
        off = 1024 * half
        lo, hi = off - 256, off + 1280
        s0, s1 = max(lo, 0), min(hi, 2048)
        xl = np.zeros((LOC, D), f32)
        xl[s0 - lo:s1 - lo] = x[b, s0:s1]
        m = {"xloc": np.ascontiguousarray(xl.T), "b0": b0, "b1": b1,
             "w1qsl": np.ascontiguousarray(
                 w1qf[core * 512:(core + 1) * 512].reshape(4, 128, 4096))}
        m.update(W)
        m.update(consts[half])
        in_maps.append(m)

    nc = build_nc()
    res = run_bass_kernel_spmd(nc, in_maps, core_ids=list(range(8)))
    out = np.zeros((4, 2048, 4096), np.float32)
    for core in range(8):
        b, half = core // 2, core % 2
        out[b, 1024 * half:1024 * (half + 1)] = res.results[core]["y"].T
    return out
